# version 01
# baseline (speedup 1.0000x reference)
import numpy as np
import ml_dtypes
import contextlib
import concourse.bass as bass
import concourse.mybir as mybir
from concourse.bass_utils import run_bass_kernel_spmd

F32 = mybir.dt.float32
BF16 = mybir.dt.bfloat16
I32 = mybir.dt.int32
AF = mybir.ActivationFunctionType
ALU = mybir.AluOpType
NPBF = ml_dtypes.bfloat16

BB, SEQ, DM = 4, 2048, 2048
H, DKH, DVH = 4, 256, 512
CH = 128
NCH = SEQ // CH
MLP = 16384
MSL = MLP // 8
EPS = 1e-5

DEBUG_STOP_AFTER_B = False
DEBUG_X1 = False
DEBUG_ONT = False
DEBUG_S = False
DEBUG_NO_INTER = False
DEBUG_FREEZE_SB = False
DEBUG_RETSYNC = False
DEBUG_B = False
DEBUG_BSYNC = False
DEBUG_NO_DEQUANT = False

ENGS = ("tensor", "vector", "scalar", "gpsimd", "sync")


class Bld:
    def __init__(self, nc):
        self.nc = nc
        self.q = {e: [] for e in ENGS}
        self.sems = {}
        self.seen = {e: {} for e in ENGS}

    def add_sem(self, name, handle):
        self.sems[name] = [handle, 0]

    def val(self, name):
        return self.sems[name][1]

    def emit(self, eng, fn):
        self.q[eng].append(fn)

    def wait(self, eng, name, v=None):
        if v is None:
            v = self.sems[name][1]
        if v is None or v <= 0:
            return
        if self.seen[eng].get(name, -1) >= v:
            return
        self.seen[eng][name] = v
        h = self.sems[name][0]
        self.emit(eng, lambda e, h=h, v=v: e.wait_ge(h, v))

    def op(self, eng, fn, inc=None, inc_n=1, waits=()):
        for w in waits:
            if w is None:
                continue
            n, v = w
            self.wait(eng, n, v)
        if inc is None:
            self.emit(eng, fn)
            return None
        s = self.sems[inc]
        s[1] += inc_n
        v = s[1]
        h = s[0]
        self.emit(eng, lambda e, fn=fn, h=h, n=inc_n: fn(e).then_inc(h, n))
        return v

    def dma(self, eng, out, in_, inc, waits=()):
        return self.op(eng, lambda e, o=out, i=in_: e.dma_start(out=o, in_=i),
                       inc=inc, inc_n=16, waits=waits)

    def barrier(self, eng_list, sem_list):
        for e in eng_list:
            for s in sem_list:
                self.wait(e, s, self.val(s))


def _carve(arena_ap, off_bytes, dt, shape):
    isz = {F32: 4, BF16: 2, I32: 4}[dt]
    assert off_bytes % 32 == 0, off_bytes
    n_el = 1
    for s in shape[1:]:
        n_el *= s
    a = arena_ap.bitcast(dt)
    start = off_bytes // isz
    v = a[:, start:start + n_el]
    if len(shape) == 3:
        v = v.rearrange("p (a b) -> p a b", b=shape[2])
    elif len(shape) == 4:
        v = v.rearrange("p (a b c) -> p a b c", b=shape[2], c=shape[3])
    end = off_bytes + n_el * isz
    return v[0:shape[0]], (end + 31) // 32 * 32


def build_program():
    nc = bass.Bass("TRN2", target_bir_lowering=False, debug=False)
    dt_ = nc.dram_tensor

    xT = dt_("xT", [DM, SEQ], F32, kind="ExternalInput").ap()
    wqT = dt_("wqT", [DM, 512], F32, kind="ExternalInput").ap()
    wkT = dt_("wkT", [DM, 512], F32, kind="ExternalInput").ap()
    wvT = dt_("wvT", [DM, 1024], F32, kind="ExternalInput").ap()
    wgaT = dt_("wgaT", [DM, DM], F32, kind="ExternalInput").ap()
    woT = dt_("woT", [DM, DM], F32, kind="ExternalInput").ap()
    gnormB = dt_("gnormB", [128, DVH], F32, kind="ExternalInput").ap()
    ropeC = dt_("ropeC", [128, SEQ], F32, kind="ExternalInput").ap()
    ropeS = dt_("ropeS", [128, SEQ], F32, kind="ExternalInput").ap()
    dmaskT = dt_("dmaskT", [128, 2 * CH], F32, kind="ExternalInput").ap()
    ginb = dt_("ginb", [128, 2 * CH], F32, kind="ExternalInput").ap()
    gkb = dt_("gkb", [128, 2], F32, kind="ExternalInput").ap()
    gCc = dt_("gCc", [128, 2], F32, kind="ExternalInput").ap()
    identM = dt_("identM", [128, 128], F32, kind="ExternalInput").ap()
    shifts = dt_("shifts", [128, 1], I32, kind="ExternalInput").ap()
    cpool = dt_("cpool", [128, 8], F32, kind="ExternalInput").ap()
    onescol = dt_("onescol", [128, 1], F32, kind="ExternalInput").ap()
    onesrow = dt_("onesrow", [1, 128], F32, kind="ExternalInput").ap()
    e0row = dt_("e0row", [128, 128], F32, kind="ExternalInput").ap()
    gpackR = dt_("gpackR", [MSL, DM], I32, kind="ExternalInput").ap()
    gscalR = dt_("gscalR", [MSL, DM], F32, kind="ExternalInput").ap()
    upackR = dt_("upackR", [MSL, DM], I32, kind="ExternalInput").ap()
    uscalR = dt_("uscalR", [MSL, DM], F32, kind="ExternalInput").ap()
    dpackR = dt_("dpackR", [MSL, DM], I32, kind="ExternalInput").ap()
    dscalR = dt_("dscalR", [MSL, DM], F32, kind="ExternalInput").ap()

    out = dt_("out", [DM, 1024], F32, kind="ExternalOutput").ap()
    dbg_x1 = dt_("dbg_x1", [DM, SEQ], F32, kind="ExternalOutput").ap() \
        if DEBUG_X1 else None
    dbg_ont = dt_("dbg_ont", [1024, SEQ], F32, kind="ExternalOutput").ap() \
        if DEBUG_ONT else None
    dbg_S = dt_("dbg_S", [128, 2048], F32, kind="ExternalOutput").ap() \
        if DEBUG_S else None
    dbg_Sb = dt_("dbg_Sb", [128, 1024], F32, kind="ExternalOutput").ap() \
        if DEBUG_S else None
    dbg_osc = dt_("dbg_osc", [128, 512], F32, kind="ExternalOutput").ap() \
        if DEBUG_S else None
    dbg_ofu = dt_("dbg_ofu", [2048, 1024], F32, kind="ExternalOutput").ap() \
        if DEBUG_B else None
    dbg_g = dt_("dbg_g", [2048, 512], F32, kind="ExternalOutput").ap() \
        if DEBUG_B else None
    dbg_ao = dt_("dbg_ao", [2048, 512], F32, kind="ExternalOutput").ap() \
        if DEBUG_B else None
    dbg_qt = dt_("dbg_qt", [128, 512], F32, kind="ExternalOutput").ap() \
        if DEBUG_S else None

    exin0 = dt_("exin0", [1024, 1024], BF16).ap()
    exin1 = dt_("exin1", [1024, 1024], BF16).ap()
    ex0 = dt_("ex0", [2048, 1024], BF16).ap()
    ex1 = dt_("ex1", [2048, 1024], BF16).ap()
    gin = dt_("gin", [DM, SEQ], BF16).ap()
    X1G = dt_("X1G", [4 * DM, SEQ], BF16).ap()
    WSg = dt_("WSg", [DM, MSL], BF16).ap()
    WSu = dt_("WSu", [DM, MSL], BF16).ap()
    WSd = dt_("WSd", [MSL, DM], BF16).ap()
    GT = dt_("GT", [16 * MSL, 512], BF16).ap()
    IT = dt_("IT", [16 * MSL, 512], BF16).ap()
    PD0 = dt_("PD0", [8 * DM, 512], BF16).ap()
    PD1 = dt_("PD1", [8 * DM, 512], BF16).ap()
    RSo0 = dt_("RSo0", [DM, 512], BF16).ap()
    RSo1 = dt_("RSo1", [DM, 512], BF16).ap()
    XB = dt_("XB", [DM, SEQ], BF16).ap()
    WGA2 = dt_("WGA2", [DM, DM], BF16).ap()
    WO2 = dt_("WO2", [DM, DM], BF16).ap()

    es = contextlib.ExitStack()
    with es:
        E = es.enter_context
        AR_t = E(nc.sbuf_tensor("ARENA", [128, 73728], BF16))
        AR = AR_t[:]
        s_dmask = E(nc.sbuf_tensor("s_dmask", [128, 2 * CH], F32))
        s_ginb = E(nc.sbuf_tensor("s_ginb", [128, 2 * CH], F32))
        s_gkb = E(nc.sbuf_tensor("s_gkb", [128, 2], F32))
        s_gCc = E(nc.sbuf_tensor("s_gCc", [128, 2], F32))
        s_gnorm = E(nc.sbuf_tensor("s_gnorm", [128, DVH], F32))
        s_ident = E(nc.sbuf_tensor("s_ident", [128, 128], BF16))
        s_shift = E(nc.sbuf_tensor("s_shift", [128, 1], I32))
        s_cpool = E(nc.sbuf_tensor("s_cpool", [128, 8], F32))
        s_onesc = E(nc.sbuf_tensor("s_onesc", [128, 1], BF16))
        s_onesr = E(nc.sbuf_tensor("s_onesr", [1, 128], F32))
        s_E0 = E(nc.sbuf_tensor("s_E0", [128, 128], F32))
        SC = E(nc.sbuf_tensor("SC", [128, 10752], BF16))
        DQ = E(nc.sbuf_tensor("DQ", [128, 14336], BF16))
        p_a = E(nc.psum_tensor("p_a", [128, 512], F32))
        p_b = E(nc.psum_tensor("p_b", [128, 512], F32))
        p_c = E(nc.psum_tensor("p_c", [128, 512], F32))
        p_d = E(nc.psum_tensor("p_d", [128, 512], F32))
        p_s0 = E(nc.psum_tensor("p_s0", [128, 512], F32))
        p_s1 = E(nc.psum_tensor("p_s1", [128, 512], F32))
        p_t = E(nc.psum_tensor("p_t", [128, 1024], BF16))

        bld = Bld(nc)
        for s in ("LD0", "PE", "VE", "SE", "GP", "CC", "XP", "XG",
                  "WP0", "WP1", "XR0", "XR1", "OFU", "GIN", "XF0", "XF1",
                  "BA", "BB", "W0S", "S2R", "DQP", "DQS", "WSG", "WSU",
                  "WSD", "GTS", "ITS", "PDS0", "PDS1", "EXN", "OUTS", "DBG"):
            bld.add_sem(s, E(nc.semaphore(s)))

        eps_ap = s_cpool[:, 0:1]
        zero_ap = s_cpool[:, 1:2]

        MB = 1024 * 1024
        o = 0
        wq_sb, o = _carve(AR, o, BF16, [128, 16, 512])
        wk_sb, o = _carve(AR, o, BF16, [128, 16, 512])
        wv_sb, o = _carve(AR, o, BF16, [128, 16, 1024])
        xp_sb, o = _carve(AR, o, BF16, [128, 16, 512])
        q_sb, o = _carve(AR, o, BF16, [128, 4, 512])
        k_sb, o = _carve(AR, o, BF16, [128, 4, 512])
        v_sb, o = _carve(AR, o, BF16, [128, 4, 2, 512])
        S_f, o = _carve(AR, o, F32, [128, 2, 1024])
        S_b, o = _carve(AR, o, BF16, [128, 2, 1024])
        a_end = o
        ONT_OFF = (73728 * 2) - (4 * MB // 128)
        ont_sb, _ = _carve(AR, ONT_OFF, BF16, [128, 8, 2048])
        assert a_end <= ONT_OFF, (a_end, ONT_OFF)
        o = 0
        ofu_sb, o = _carve(AR, o, BF16, [128, 16, 1024])
        g_sb, o = _carve(AR, o, BF16, [128, 16, 512])
        sres, o = _carve(AR, o, F32, [128, 16, 512])
        sbf, o = _carve(AR, o, BF16, [128, 16, 512])
        wpan0, o = _carve(AR, o, BF16, [128, 16, 128])
        wpan1, o = _carve(AR, o, BF16, [128, 16, 128])
        xres0, o = _carve(AR, o, F32, [128, 512])
        xres1, o = _carve(AR, o, F32, [128, 512])
        assert o <= ONT_OFF
        o = 0
        W0, o = _carve(AR, o, BF16, [128, 16, 2048])
        xf0, o = _carve(AR, o, BF16, [128, 16, 512])
        xf1, o = _carve(AR, o, BF16, [128, 16, 512])
        bufA, o = _carve(AR, o, BF16, [128, 16, 512])
        bufB, o = _carve(AR, o, BF16, [128, 16, 512])
        bufC, o = _carve(AR, o, BF16, [128, 16, 512])
        assert o <= 73728 * 2
        o = 0
        s2_sb, o = _carve(AR, o, BF16, [128, 16, 512])
        sq2_sb, o = _carve(AR, o, BF16, [128, 16, 512])
        outp, o = _carve(AR, o, F32, [128, 16, 512])

        o = 0
        o_scr, o = _carve(SC[:], o, F32, [128, 512])
        sq_scr, o = _carve(SC[:], o, F32, [128, 512])
        attn_sb, o = _carve(SC[:], o, BF16, [128, 128])
        ktil, o = _carve(SC[:], o, BF16, [128, 256])
        qtil, o = _carve(SC[:], o, BF16, [128, 256])
        on_sb, o = _carve(SC[:], o, BF16, [128, 512])
        ms_scr, o = _carve(SC[:], o, F32, [128, 1])
        rr_scr, o = _carve(SC[:], o, F32, [128, 1])
        statm, o = _carve(SC[:], o, F32, [128, 512])
        statr, o = _carve(SC[:], o, F32, [128, 512])
        stat_q, o = _carve(SC[:], o, F32, [1, 512])
        stat_t, o = _carve(SC[:], o, F32, [1, 512])
        mb_b, o = _carve(SC[:], o, F32, [128, 512])
        rb_b, o = _carve(SC[:], o, F32, [128, 512])
        assert o <= 10752 * 2

        o = 0
        s_ropeC, o = _carve(DQ[:], o, BF16, [128, SEQ])
        s_ropeS, o = _carve(DQ[:], o, BF16, [128, SEQ])
        t_rope, o = _carve(DQ[:], o, F32, [128, 512])
        u_rope, o = _carve(DQ[:], o, F32, [128, 512])
        o = 0
        ge_a, o = _carve(DQ[:], o, F32, [128, 512])
        ge_b, o = _carve(DQ[:], o, F32, [128, 512])
        dq_pack, o = _carve(DQ[:], o, I32, [128, 1024])
        dq_scal, o = _carve(DQ[:], o, F32, [128, 1024])
        dq_idx, o = _carve(DQ[:], o, I32, [128, 512])
        dq_c, o = _carve(DQ[:], o, I32, [128, 512])
        dq_f, o = _carve(DQ[:], o, F32, [128, 512])
        dq_r1, o = _carve(DQ[:], o, F32, [128, 512])
        dq_r2, o = _carve(DQ[:], o, F32, [128, 512])
        dq_s2, o = _carve(DQ[:], o, F32, [128, 512])
        dq_w, o = _carve(DQ[:], o, BF16, [128, 2048])
        assert o <= 14336 * 2, o

        banks = [p_a, p_b, p_c, p_d]
        bank_guard = [None, None, None, None]
        bank_i = [0]

        def next_bank():
            i = bank_i[0] % 4
            bank_i[0] += 1
            if bank_guard[i] is not None:
                bld.wait("tensor", *bank_guard[i])
            return i

        def mm_accum(bi, tiles, n_slice=None):
            pa = banks[bi][:] if n_slice is None else banks[bi][:, 0:n_slice]
            nT = len(tiles)
            v = None
            for i, (lt, rh) in enumerate(tiles):
                st = (i == 0)
                if i == nT - 1:
                    v = bld.op("tensor",
                               lambda e, lt=lt, rh=rh, st=st, pa=pa:
                               e.matmul(pa, lt, rh, start=st, stop=True),
                               inc="PE")
                else:
                    bld.op("tensor",
                           lambda e, lt=lt, rh=rh, st=st, pa=pa:
                           e.matmul(pa, lt, rh, start=st, stop=False))
            return v

        def evac(bi, eng, fn, waits=()):
            sem = {"vector": "VE", "scalar": "SE"}[eng]
            v = bld.op(eng, fn, inc=sem, waits=waits)
            bank_guard[bi] = (sem, v)
            return v

        dq_state = {"guard": []}

        def dequant_block(packR, scalR, WS, pb, st_sem):
            last = None
            for hf in range(2):
                w_ = list(dq_state["guard"])
                hsl = slice(hf * 1024, (hf + 1) * 1024)
                vld = bld.dma("sync", dq_pack,
                              packR.rearrange("(a p) n -> p a n", p=128)
                              [:, pb, hsl], "DQP", waits=w_)
                vld2 = bld.dma("sync", dq_scal,
                               scalR.rearrange("(a p) n -> p a n", p=128)
                               [:, pb, hsl], "DQS")
                last = _dq_half(vld, vld2, hf)
                dq_state["guard"] = [("VE", last)]
            vst = bld.dma("sync",
                          WS.rearrange("(a p) n -> p a n", p=128)[:, pb, :],
                          dq_w, st_sem, waits=[("VE", last)])
            dq_state["guard"] = [("VE", last), (st_sem, vst)]
            return vst

        def _dq_half(vld, vld2, hf):
            last = None
            for i in range(2):
                sl = slice(i * 512, (i + 1) * 512)
                pk = dq_pack[:, sl]
                scl = dq_scal[:, sl]
                wo = dq_w[:, hf * 1024 + i * 512:hf * 1024 + (i + 1) * 512]
                bld.op("vector", lambda e, pk=pk: e.tensor_scalar(
                    dq_idx, pk, s_shift[:], 15, ALU.logical_shift_right,
                    ALU.bitwise_and),
                    waits=[("DQP", vld), ("DQS", vld2)])
                bld.op("vector", lambda e: e.tensor_scalar(
                    dq_c, dq_idx, 7, None, ALU.bitwise_and))
                bld.op("vector", lambda e: e.tensor_scalar(
                    dq_idx, dq_idx, 8, None, ALU.bitwise_and))
                vf = bld.op("vector", lambda e: e.tensor_scalar(
                    dq_f, dq_idx, -0.125, 0.5, ALU.mult, ALU.add), inc="VE")
                bld.op("vector", lambda e: e.tensor_scalar(
                    dq_r1, dq_c, -4.0, 0.0, ALU.add, ALU.max))
                vr = bld.op("vector", lambda e: e.tensor_scalar(
                    dq_r2, dq_c, -6.0, 0.0, ALU.add, ALU.max), inc="VE")
                bld.op("gpsimd", lambda e: e.tensor_scalar(
                    dq_r2, dq_r2, 2.0, None, ALU.mult), waits=[("VE", vr)])
                bld.op("gpsimd", lambda e: e.tensor_tensor(
                    dq_s2, dq_r1, dq_r2, ALU.add))
                vg = bld.op("gpsimd", lambda e: e.tensor_tensor(
                    dq_s2, dq_s2, dq_c, ALU.add), inc="GP")
                bld.op("vector", lambda e: e.tensor_tensor(
                    dq_s2, dq_s2, dq_f, ALU.mult),
                    waits=[("GP", vg), ("VE", vf)])
                last = bld.op("vector", lambda e, scl=scl, wo=wo: e.tensor_tensor(
                    wo, dq_s2, scl, ALU.mult), inc="VE")
            return last

        def layer_norm(src_f32, srcbf, dst_bf, dst_f32, sqbuf, ready):
            sum_ap = p_s0[0:1, :]
            sq_ap = p_s1[0:1, :]
            for kb in range(16):
                st, sp = (kb == 0), (kb == 15)
                if sp:
                    vsum = bld.op("tensor", lambda e, kb=kb, st=st: e.matmul(
                        sum_ap, s_onesc[:], srcbf[:, kb, :], start=st, stop=True),
                        inc="PE", waits=list(ready))
                else:
                    bld.op("tensor", lambda e, kb=kb, st=st: e.matmul(
                        sum_ap, s_onesc[:], srcbf[:, kb, :], start=st, stop=False),
                        waits=list(ready))
            vsq_in = bld.op("vector", lambda e: e.tensor_tensor(
                sqbuf[:], srcbf[:], srcbf[:], ALU.mult), inc="VE",
                waits=list(ready))
            for kb in range(16):
                st, sp = (kb == 0), (kb == 15)
                if sp:
                    vsq = bld.op("tensor", lambda e, kb=kb, st=st: e.matmul(
                        sq_ap, s_onesc[:], sqbuf[:, kb, :], start=st, stop=True),
                        inc="PE", waits=[("VE", vsq_in)])
                else:
                    bld.op("tensor", lambda e, kb=kb, st=st: e.matmul(
                        sq_ap, s_onesc[:], sqbuf[:, kb, :], start=st, stop=False),
                        waits=[("VE", vsq_in)])
            vme = bld.op("vector", lambda e: e.tensor_scalar(
                statm[0:1, :], sum_ap, 1.0 / 2048, None, ALU.mult),
                inc="VE", waits=[("PE", vsum)])
            bld.op("vector", lambda e: e.tensor_scalar(
                stat_q, sq_ap, 1.0 / 2048, None, ALU.mult),
                waits=[("PE", vsq)])
            bld.op("vector", lambda e: e.tensor_tensor(
                stat_t, statm[0:1, :], statm[0:1, :], ALU.mult))
            vvar = bld.op("vector", lambda e: e.tensor_tensor(
                stat_q, stat_q, stat_t, ALU.subtract), inc="VE")
            bld.op("scalar", lambda e: e.activation(
                stat_q, stat_q, AF.Ln, bias=eps_ap[0:1, :], scale=1.0),
                waits=[("VE", vvar)])
            vrs = bld.op("scalar", lambda e: e.activation(
                statr[0:1, :], stat_q, AF.Exp, bias=zero_ap[0:1, :],
                scale=-0.5), inc="SE")
            bi = next_bank()
            vbc1 = bld.op("tensor", lambda e, bi=bi: e.matmul(
                banks[bi][:], s_E0[:], statm, start=True, stop=True),
                inc="PE", waits=[("SE", vrs), ("VE", vme)])
            bj = next_bank()
            vbc2 = bld.op("tensor", lambda e, bj=bj: e.matmul(
                banks[bj][:], s_E0[:], statr, start=True, stop=True),
                inc="PE")
            evac(bi, "vector", lambda e, bi=bi: e.tensor_copy(
                mb_b, banks[bi][:]), waits=[("PE", vbc1)])
            evac(bj, "vector", lambda e, bj=bj: e.tensor_copy(
                rb_b, banks[bj][:]), waits=[("PE", vbc2)])
            last = None
            src = src_f32 if src_f32 is not None else srcbf
            dst = dst_bf if dst_bf is not None else dst_f32
            for kb in range(16):
                bld.op("vector", lambda e, kb=kb: e.tensor_tensor(
                    o_scr, src[:, kb, :], mb_b, ALU.subtract))
                last = bld.op("vector", lambda e, kb=kb: e.tensor_tensor(
                    dst[:, kb, :], o_scr, rb_b, ALU.mult), inc="VE")
            return last

        bld.dma("sync", s_dmask[:], dmaskT, "LD0")
        bld.dma("sync", s_ginb[:], ginb, "LD0")
        bld.dma("sync", s_gkb[:], gkb, "LD0")
        bld.dma("sync", s_gCc[:], gCc, "LD0")
        bld.dma("sync", s_gnorm[:], gnormB, "LD0")
        bld.dma("sync", s_shift[:], shifts, "LD0")
        bld.dma("sync", s_cpool[:], cpool, "LD0")
        bld.dma("sync", s_onesr[:], onesrow, "LD0")
        bld.dma("sync", s_E0[:], e0row, "LD0")
        bld.dma("gpsimd", s_ropeC[:], ropeC, "LD0")
        bld.dma("gpsimd", s_ropeS[:], ropeS, "LD0")
        bld.dma("gpsimd", s_ident[:], identM, "LD0")
        bld.dma("gpsimd", s_onesc[:], onescol, "LD0")
        v_xb = bld.dma("gpsimd", XB, xT, "LD0")
        v_wga2 = bld.dma("gpsimd", WGA2, wgaT, "LD0")
        v_wo2 = bld.dma("gpsimd", WO2, woT, "LD0")
        v_wq = bld.dma("gpsimd", wq_sb, wqT.rearrange("(a p) n -> p a n", p=128), "LD0")
        v_wk = bld.dma("gpsimd", wk_sb, wkT.rearrange("(a p) n -> p a n", p=128), "LD0")
        v_wv = bld.dma("gpsimd", wv_sb, wvT.rearrange("(a p) n -> p a n", p=128), "LD0")
        v_consts = bld.val("LD0")
        for e in ENGS:
            bld.wait(e, "LD0", v_consts)

        bld.op("vector", lambda e: e.memset(statm, 0.0))
        bld.op("vector", lambda e: e.memset(statr, 0.0))
        bld.op("vector", lambda e: e.memset(S_f, 0.0))
        vS1 = bld.op("vector", lambda e: e.memset(S_b, 0.0), inc="VE")

        sbf_ready = {0: ("VE", vS1), 1: ("VE", vS1)}
        xp_guard = None
        pt_guard = None
        ps_guard = None
        g_attn = {}
        dbg_guard = {}

        for p in range(4):
            cols = slice(p * 512, (p + 1) * 512)
            w_ = [xp_guard] if xp_guard else []
            vxp = bld.dma("sync", xp_sb,
                          XB.rearrange("(a p) t -> p a t", p=128)[:, :, cols],
                          "XP", waits=w_)
            for (w_sb, dst, vw) in ((wq_sb, q_sb, v_wq), (wk_sb, k_sb, v_wk)):
                for h in range(2):
                    bi = next_bank()
                    bj = next_bank()
                    vmm = None
                    for (b_, mb) in ((bi, 0), (bj, 1)):
                        m = h * 2 + mb
                        tiles = [(w_sb[:, kb, m * 128:(m + 1) * 128],
                                  xp_sb[:, kb, :]) for kb in range(16)]
                        bld.wait("tensor", "XP", vxp)
                        vmm = mm_accum(b_, tiles)
                    p0 = banks[bi][:]
                    p1 = banks[bj][:]
                    cC = s_ropeC[:, cols]
                    cS = s_ropeS[:, cols]
                    d0 = dst[:, h * 2 + 0, :]
                    d1 = dst[:, h * 2 + 1, :]
                    bld.wait("vector", "PE", vmm)
                    bld.op("vector", lambda e, p0=p0, cC=cC: e.tensor_tensor(
                        t_rope, p0, cC, ALU.mult))
                    bld.op("vector", lambda e, p1=p1, cS=cS: e.tensor_tensor(
                        u_rope, p1, cS, ALU.mult))
                    bld.op("vector", lambda e, d0=d0: e.tensor_tensor(
                        d0, t_rope, u_rope, ALU.subtract))
                    bld.op("vector", lambda e, p1=p1, cC=cC: e.tensor_tensor(
                        t_rope, p1, cC, ALU.mult))
                    bld.op("vector", lambda e, p0=p0, cS=cS: e.tensor_tensor(
                        u_rope, p0, cS, ALU.mult))
                    vev = bld.op("vector", lambda e, d1=d1: e.tensor_tensor(
                        d1, t_rope, u_rope, ALU.add), inc="VE")
                    bank_guard[bi] = ("VE", vev)
                    bank_guard[bj] = ("VE", vev)
            for tmb in range(4):
                for h in range(2):
                    bi = next_bank()
                    tiles = [(xp_sb[:, kb, tmb * 128:(tmb + 1) * 128],
                              wv_sb[:, kb, h * 512:(h + 1) * 512])
                             for kb in range(16)]
                    bld.wait("tensor", "XP", vxp)
                    vmm = mm_accum(bi, tiles)
                    evac(bi, "vector",
                         lambda e, bi=bi, tmb=tmb, h=h: e.tensor_copy(
                             v_sb[:, tmb, h, :], banks[bi][:]),
                         waits=[("PE", vmm)])
            vqk_done = bld.val("VE")
            for cc in range(4):
                ch_cols = slice(cc * 128, (cc + 1) * 128)
                gch = 4 * p + cc
                for h in range(2):
                    bi = next_bank()
                    tiles = [(k_sb[:, h * 2 + mb, ch_cols],
                              q_sb[:, h * 2 + mb, ch_cols]) for mb in range(2)]
                    bld.wait("tensor", "VE", vqk_done)
                    vat = mm_accum(bi, tiles, n_slice=128)
                    vev = evac(bi, "vector",
                               lambda e, bi=bi, h=h: e.tensor_tensor(
                                   attn_sb, banks[bi][:, 0:128],
                                   s_dmask[:, h * CH:(h + 1) * CH], ALU.mult),
                               waits=[("PE", vat), g_attn.get("attn")])
                    if pt_guard is not None:
                        bld.wait("tensor", *pt_guard)
                    vtr = None
                    for mb in range(2):
                        vtr = bld.op("tensor",
                                     lambda e, h=h, mb=mb, ch_cols=ch_cols:
                                     e.transpose(
                                         p_t[:, mb * 128:(mb + 1) * 128],
                                         k_sb[:, h * 2 + mb, ch_cols],
                                         s_ident[:]), inc="PE")
                    vkt = bld.op("vector", lambda e, h=h: e.tensor_scalar(
                        ktil, p_t[:, 0:256], s_gkb[:, h:h + 1], None,
                        ALU.mult), inc="VE",
                        waits=[("PE", vtr), g_attn.get("ktil")])
                    vqt = None
                    for mb in range(2):
                        vqt = bld.op("vector",
                                     lambda e, h=h, mb=mb, ch_cols=ch_cols:
                                     e.tensor_tensor(
                                         qtil[:, mb * 128:(mb + 1) * 128],
                                         q_sb[:, h * 2 + mb, ch_cols],
                                         s_ginb[:, h * CH:(h + 1) * CH],
                                         ALU.mult), inc="VE",
                                     waits=[g_attn.get("qtil")])
                    bo = next_bank()
                    bld.wait("tensor", "VE", vev)
                    if DEBUG_NO_INTER:
                        vo = bld.op("tensor", lambda e, bo=bo, cc=cc, h=h:
                                    e.matmul(banks[bo][:], attn_sb,
                                             v_sb[:, cc, h, :],
                                             start=True, stop=True), inc="PE")
                        bld.wait("tensor", "VE", vqt)
                        bld.wait("tensor", *sbf_ready[h])
                    else:
                        bld.op("tensor", lambda e, bo=bo, cc=cc, h=h: e.matmul(
                            banks[bo][:], attn_sb, v_sb[:, cc, h, :],
                            start=True, stop=False))
                        bld.wait("tensor", "VE", vqt)
                        bld.wait("tensor", *sbf_ready[h])
                        bld.op("tensor", lambda e, bo=bo, h=h: e.matmul(
                            banks[bo][:], qtil[:, 0:128], S_b[:, h, 0:512],
                            start=False, stop=False))
                        vo = bld.op("tensor", lambda e, bo=bo, h=h: e.matmul(
                            banks[bo][:], qtil[:, 128:256],
                            S_b[:, h, 512:1024],
                            start=False, stop=True), inc="PE")
                    g_attn["attn"] = ("PE", vo)
                    g_attn["qtil"] = ("PE", vo)
                    bld.wait("tensor", "VE", vkt)
                    if ps_guard is not None:
                        bld.wait("tensor", *ps_guard)
                    vsu = None
                    for mb, psx in ((0, p_s0), (1, p_s1)):
                        vsu = bld.op("tensor",
                                     lambda e, mb=mb, cc=cc, h=h, psx=psx:
                                     e.matmul(
                                         psx[:],
                                         ktil[:, mb * 128:(mb + 1) * 128],
                                         v_sb[:, cc, h, :], start=True,
                                         stop=True), inc="PE")
                    g_attn["ktil"] = ("PE", vsu)
                    vg = bld.op("gpsimd", lambda e, h=h: e.tensor_scalar(
                        S_f[:, h, :], S_f[:, h, :], s_gCc[:, h:h + 1], None,
                        ALU.mult), inc="GP",
                        waits=[sbf_ready[h], dbg_guard.get("vg")])
                    bld.op("vector", lambda e, h=h: e.tensor_tensor(
                        S_f[:, h, 0:512], S_f[:, h, 0:512], p_s0[:], ALU.add),
                        waits=[("GP", vg), ("PE", vsu)])
                    va = bld.op("vector", lambda e, h=h: e.tensor_tensor(
                        S_f[:, h, 512:1024], S_f[:, h, 512:1024], p_s1[:],
                        ALU.add), inc="VE")
                    ps_guard = ("VE", va)
                    if not DEBUG_FREEZE_SB:
                        vsb = bld.op("scalar", lambda e, h=h: e.activation(
                            S_b[:, h, :], S_f[:, h, :], AF.Copy, bias=0.0,
                            scale=1.0), inc="SE", waits=[("VE", va)])
                        sbf_ready[h] = ("SE", vsb)
                    bld.wait("vector", "PE", vo)
                    bld.op("vector", lambda e, bo=bo: e.tensor_copy(
                        o_scr, banks[bo][:]))
                    bld.op("vector", lambda e: e.tensor_tensor(
                        sq_scr, o_scr, o_scr, ALU.mult))
                    voc = bld.op("vector", lambda e: e.tensor_reduce(
                        ms_scr, sq_scr, mybir.AxisListType.X, ALU.add),
                        inc="VE")
                    bank_guard[bo] = ("VE", voc)
                    bld.op("scalar", lambda e: e.activation(
                        rr_scr, ms_scr, AF.Ln, bias=eps_ap, scale=1.0 / DVH),
                        waits=[("VE", voc)])
                    vrr = bld.op("scalar", lambda e: e.activation(
                        rr_scr, rr_scr, AF.Exp, bias=zero_ap, scale=-0.5),
                        inc="SE")
                    von = bld.op("vector", lambda e: e.scalar_tensor_tensor(
                        on_sb, o_scr, rr_scr, s_gnorm[:], ALU.mult, ALU.mult),
                        inc="VE",
                        waits=[("SE", vrr), g_attn.get("on")])
                    bld.wait("tensor", "VE", vkt)
                    vto = None
                    for db in range(4):
                        vto = bld.op("tensor", lambda e, db=db: e.transpose(
                            p_t[:, 256 + db * 128:256 + (db + 1) * 128],
                            on_sb[:, db * 128:(db + 1) * 128], s_ident[:]),
                            inc="PE", waits=[("VE", von)])
                    g_attn["on"] = ("PE", vto)
                    dst = ont_sb[:, h * 4:(h + 1) * 4,
                                 gch * 128:(gch + 1) * 128]
                    src = p_t[:, 256:768].rearrange("p (a b) -> p a b", b=128)
                    vont = bld.op("vector",
                                  lambda e, dst=dst, src=src: e.tensor_copy(
                                      dst, src), inc="VE",
                                  waits=[("PE", vto)])
                    pt_guard = ("VE", vont)
                    if DEBUG_S and p == 0 and cc == 0 and h == 1:
                        vdS = bld.dma("gpsimd", dbg_S, S_f.rearrange(
                            "p a b -> p (a b)"), "DBG",
                            waits=[("SE", bld.val("SE"))])
                        dbg_guard["vg"] = ("DBG", vdS)
                    if DEBUG_S and p == 0 and cc == 1 and h == 0:
                        v1_ = bld.dma("gpsimd", dbg_qt[:, 0:256], qtil,
                                      "DBG", waits=[("VE", vqt)])
                        v2_ = bld.dma("gpsimd", dbg_qt[:, 256:512], ktil,
                                      "DBG", waits=[("VE", vkt)])
                        v3_ = bld.dma("gpsimd", dbg_Sb, S_b[:, 0, :], "DBG")
                        v4_ = bld.dma("gpsimd", dbg_osc, o_scr, "DBG")
                        g_attn["qtil"] = ("DBG", v4_)
                        g_attn["ktil"] = ("DBG", v4_)
                        g_attn["on"] = ("DBG", v4_)
                        sbf_ready[0] = ("DBG", v4_)
                        dbg_guard["vg"] = ("DBG", v4_)
            xp_guard = ("PE", bld.val("PE"))

        v_ont_done = bld.val("VE")
        if DEBUG_ONT:
            bld.dma("gpsimd", dbg_ont.rearrange("(a p) t -> p a t", p=128),
                    ont_sb, "DBG", waits=[("VE", v_ont_done)])
        vex0 = bld.dma("gpsimd", exin0.rearrange("(a p) t -> p a t", p=128),
                       ont_sb[:, :, 0:1024], "EXN",
                       waits=[("VE", v_ont_done)])
        vex1 = bld.dma("gpsimd", exin1.rearrange("(a p) t -> p a t", p=128),
                       ont_sb[:, :, 1024:2048], "EXN")
        pair_groups = [[0, 1], [2, 3], [4, 5], [6, 7]]
        bld.op("gpsimd", lambda e: e.collective_compute(
            "AllGather", ALU.bypass, replica_groups=pair_groups,
            ins=[exin0], outs=[ex0]), inc="CC",
            waits=[("EXN", vex1)])
        bld.op("gpsimd", lambda e: e.collective_compute(
            "AllGather", ALU.bypass, replica_groups=pair_groups,
            ins=[exin1], outs=[ex1]), inc="CC")

        a_pe = bld.val("PE")
        bld.barrier(["sync", "gpsimd"], ["PE", "VE", "SE", "GP"])
        xg_guard = None
        wp_guard = [None, None]
        xr_guard = [None, None]
        og_prev = None
        gin_prev = None
        ofu_pe_guard = a_pe
        for half in range(2):
            exbuf = ex0 if half == 0 else ex1
            vof = bld.dma("sync", ofu_sb,
                          exbuf.rearrange("(a p) t -> p a t", p=128),
                          "OFU",
                          waits=[("CC", half + 1), ("PE", ofu_pe_guard)])
            for grp in range(2):
                gcols = slice(half * 1024 + grp * 512,
                              half * 1024 + grp * 512 + 512)
                ocols = slice(grp * 512, (grp + 1) * 512)
                w_ = list(xg_guard) if xg_guard else []
                vxg = bld.dma("sync", sbf,
                              XB.rearrange("(a p) t -> p a t", p=128)
                              [:, :, gcols], "XG", waits=w_)
                vsilu = None
                for mb in range(16):
                    wp = (wpan0, wpan1)[mb % 2]
                    w_ = [wp_guard[mb % 2]] if wp_guard[mb % 2] else []
                    vwp = bld.dma("sync", wp,
                                  WGA2.rearrange("(a p) n -> p a n", p=128)
                                  [:, :, mb * 128:(mb + 1) * 128],
                                  "WP%d" % (mb % 2), waits=w_)
                    bi = next_bank()
                    tiles = [(wp[:, kb, :], sbf[:, kb, :]) for kb in range(16)]
                    bld.wait("tensor", "WP%d" % (mb % 2), vwp)
                    bld.wait("tensor", "XG", vxg)
                    vmm = mm_accum(bi, tiles)
                    wp_guard[mb % 2] = ("PE", vmm)
                    vsilu = evac(bi, "scalar",
                                 lambda e, bi=bi, mb=mb: e.activation(
                                     g_sb[:, mb, :], banks[bi][:], AF.Silu,
                                     bias=0.0, scale=1.0),
                                 waits=[("PE", vmm), og_prev])
                if DEBUG_B and half == 0 and grp == 0:
                    vd_ = bld.dma("gpsimd",
                                  dbg_ofu.rearrange("(a p) t -> p a t", p=128),
                                  ofu_sb, "DBG", waits=[("OFU", vof)])
                    vd2_ = bld.dma("gpsimd",
                                   dbg_g.rearrange("(a p) t -> p a t", p=128),
                                   g_sb, "DBG", waits=[("SE", vsilu)])
                    dbg_guard["og"] = ("DBG", vd2_)
                vog = bld.op("vector", lambda e, ocols=ocols: e.tensor_tensor(
                    ofu_sb[:, :, ocols], ofu_sb[:, :, ocols], g_sb[:],
                    ALU.mult), inc="VE",
                    waits=[("SE", vsilu), ("OFU", vof),
                           dbg_guard.get("og")])
                og_prev = ("VE", vog)
                vsh = None
                for mb in range(16):
                    wp = (wpan0, wpan1)[mb % 2]
                    w_ = [wp_guard[mb % 2]] if wp_guard[mb % 2] else []
                    vwp = bld.dma("sync", wp,
                                  WO2.rearrange("(a p) n -> p a n", p=128)
                                  [:, :, mb * 128:(mb + 1) * 128],
                                  "WP%d" % (mb % 2), waits=w_)
                    xr = (xres0, xres1)[mb % 2]
                    w_ = [xr_guard[mb % 2]] if xr_guard[mb % 2] else []
                    vxr = bld.dma("sync", xr,
                                  xT.rearrange("(a p) t -> p a t", p=128)
                                  [:, mb, gcols], "XR%d" % (mb % 2), waits=w_)
                    bi = next_bank()
                    tiles = [(wp[:, kb, :], ofu_sb[:, kb, ocols])
                             for kb in range(16)]
                    bld.wait("tensor", "WP%d" % (mb % 2), vwp)
                    bld.wait("tensor", "VE", vog)
                    vmm = mm_accum(bi, tiles)
                    wp_guard[mb % 2] = ("PE", vmm)
                    vadd = bld.op("vector",
                                  lambda e, bi=bi, mb=mb, xr=xr:
                                  e.tensor_tensor(sres[:, mb, :],
                                                  banks[bi][:], xr, ALU.add),
                                  inc="VE",
                                  waits=[("PE", vmm),
                                         ("XR%d" % (mb % 2), vxr),
                                         gin_prev])
                    xr_guard[mb % 2] = ("VE", vadd)
                    vsh = evac(bi, "vector", lambda e, mb=mb: e.tensor_copy(
                        sbf[:, mb, :], sres[:, mb, :]))
                if DEBUG_B and half == 0 and grp == 0:
                    vd3_ = bld.dma("gpsimd",
                                   dbg_ao.rearrange("(a p) t -> p a t", p=128),
                                   sres, "DBG", waits=[("VE", vsh)])
                    vsh_g = ("DBG", vd3_)
                else:
                    vsh_g = None
                vx1 = layer_norm(sres, sbf, sbf, None, g_sb,
                                 [("VE", vsh), vsh_g] if vsh_g else
                                 [("VE", vsh)])
                gin_dst = gin.rearrange("(a p) t -> p a t", p=128)[:, :, gcols]
                vgin = bld.dma("sync", gin_dst, sbf, "GIN",
                               waits=[("VE", vx1)])
                gin_prev = ("GIN", vgin)
                if DEBUG_X1:
                    dbg_dst = dbg_x1.rearrange("(a p) t -> p a t", p=128)[:, :, gcols]
                    vdbg = bld.dma("gpsimd", dbg_dst, sbf, "DBG",
                                   waits=[("VE", vx1)])
                    gin_prev = ("DBG", vdbg)
                xg_guard = [("VE", vx1), gin_prev]
                ofu_pe_guard = bld.val("PE")
                if DEBUG_BSYNC:
                    vpe_ = bld.op("tensor", lambda e: e.nop(), inc="PE")
                    vve_ = bld.op("vector", lambda e: e.engine_nop(), inc="VE")
                    vse_ = bld.op("scalar", lambda e: e.nop(), inc="SE")
                    vgp_ = bld.op("gpsimd", lambda e: e.engine_nop(), inc="GP")
                    for eng_ in ("tensor", "vector", "scalar", "gpsimd", "sync"):
                        bld.wait(eng_, "PE", vpe_)
                        bld.wait(eng_, "VE", vve_)
                        bld.wait(eng_, "SE", vse_)
                        bld.wait(eng_, "GP", vgp_)
                if not DEBUG_NO_DEQUANT:
                    for pb in range(half * 8 + grp * 4,
                                    half * 8 + grp * 4 + 4):
                        dequant_block(gpackR, gscalR, WSg, pb, "WSG")
        v_wsg_done = bld.val("WSG")
        v_gin_done = bld.val("GIN")

        batch_groups = [[0, 2, 4, 6], [1, 3, 5, 7]]
        bld.op("gpsimd", lambda e: e.collective_compute(
            "AllGather", ALU.bypass, replica_groups=batch_groups,
            ins=[gin], outs=[X1G]), inc="CC",
            waits=[("GIN", v_gin_done)])
        ccx = bld.val("CC")

        if DEBUG_STOP_AFTER_B:
            bld.wait("gpsimd", "DBG", bld.val("DBG"))
            bld.wait("gpsimd", "GIN", bld.val("GIN"))
            bld.wait("gpsimd", "CC", bld.val("CC"))
            _finish(nc, es, bld, E)
            return nc

        b_pe = bld.val("PE")
        bld.barrier(["sync", "gpsimd"], ["PE", "VE", "SE", "GP"])
        vW0 = bld.dma("sync", W0, WSg.rearrange("(a p) n -> p a n", p=128),
                      "W0S", waits=[("WSG", v_wsg_done), ("PE", b_pe)])

        def x1g_panel(t):
            rb, c0 = t // 4, (t % 4) * 512
            return X1G.rearrange("(rb a p) t -> p rb a t", p=128, rb=4)[
                :, rb, :, c0:c0 + 512]

        GTr = GT.rearrange("(t a p) n -> p t a n", p=128, t=16)
        ITr = IT.rearrange("(t a p) n -> p t a n", p=128, t=16)

        xf_guard = [None, None]
        gA_guard = None
        for t in range(16):
            xf = (xf0, xf1)[t % 2]
            w_ = [("CC", ccx)] + ([xf_guard[t % 2]] if xf_guard[t % 2] else [])
            vxf = bld.dma("sync", xf, x1g_panel(t), "XF%d" % (t % 2),
                          waits=w_)
            last_ev = None
            for mb in range(16):
                bi = next_bank()
                tiles = [(W0[:, kb, mb * 128:(mb + 1) * 128], xf[:, kb, :])
                         for kb in range(16)]
                bld.wait("tensor", "W0S", vW0)
                bld.wait("tensor", "XF%d" % (t % 2), vxf)
                vmm = mm_accum(bi, tiles)
                last_ev = evac(bi, "scalar",
                               lambda e, bi=bi, mb=mb: e.activation(
                                   bufA[:, mb, :], banks[bi][:], AF.Copy,
                                   bias=0.0, scale=1.0),
                               waits=[("PE", vmm), gA_guard])
            xf_guard[t % 2] = ("PE", bld.val("PE"))
            vgt = bld.dma("sync", GTr[:, t, :, :], bufA, "GTS",
                          waits=[("SE", last_ev)])
            gA_guard = ("GTS", vgt)
            for j in range(2):
                pb = 2 * t + j
                if pb < 16:
                    dequant_block(upackR, uscalR, WSu, pb, "WSU")
                else:
                    dequant_block(dpackR, dscalR, WSd, pb - 16, "WSD")
        v_wsu_done = bld.val("WSU")
        p1a_pe = bld.val("PE")

        vW0u = bld.dma("sync", W0, WSu.rearrange("(a p) n -> p a n", p=128),
                       "W0S", waits=[("WSU", v_wsu_done), ("PE", p1a_pe)])
        bld.wait("sync", "GTS", bld.val("GTS"))
        gp_guard = [None, None]
        it_prev = None
        ge_guard = [None, None]
        for t in range(16):
            xf = (xf0, xf1)[t % 2]
            w_ = [xf_guard[t % 2]] if xf_guard[t % 2] else []
            vxf = bld.dma("sync", xf, x1g_panel(t), "XF%d" % (t % 2),
                          waits=w_)
            gp = (bufA, bufB)[t % 2]
            w_ = [gp_guard[t % 2]] if gp_guard[t % 2] else []
            vgp = bld.dma("sync", gp, GTr[:, t, :, :],
                          ("BA", "BB")[t % 2], waits=w_)
            last_ev = None
            vth_last = None
            for mb in range(16):
                bi = next_bank()
                tiles = [(W0[:, kb, mb * 128:(mb + 1) * 128], xf[:, kb, :])
                         for kb in range(16)]
                bld.wait("tensor", "W0S", vW0u)
                bld.wait("tensor", "XF%d" % (t % 2), vxf)
                vmm = mm_accum(bi, tiles)
                gmb = gp[:, mb, :]
                bld.wait("scalar", ("BA", "BB")[t % 2], vgp)
                w_ = [("VE", ge_guard[0])] if ge_guard[0] else []
                vsqg = bld.op("scalar", lambda e, gmb=gmb: e.activation(
                    ge_a, gmb, AF.Square, bias=0.0, scale=1.0), inc="SE",
                    waits=w_)
                bld.op("vector", lambda e, gmb=gmb: e.tensor_tensor(
                    ge_a, ge_a, gmb, ALU.mult), waits=[("SE", vsqg)])
                bld.op("vector", lambda e, gmb=gmb: e.scalar_tensor_tensor(
                    ge_a, ge_a, 0.044715, gmb, ALU.mult, ALU.add))
                vu = bld.op("vector", lambda e: e.tensor_scalar(
                    ge_a, ge_a, 0.7978845608028654, None, ALU.mult), inc="VE")
                ge_guard[0] = vu
                w_ = [("VE", ge_guard[1])] if ge_guard[1] else []
                vth = bld.op("scalar", lambda e: e.activation(
                    ge_b, ge_a, AF.Tanh, bias=0.0, scale=1.0), inc="SE",
                    waits=[("VE", vu)] + w_)
                vth_last = vth
                bld.op("vector", lambda e: e.tensor_scalar(
                    ge_b, ge_b, 0.5, 0.5, ALU.mult, ALU.add),
                    waits=[("SE", vth)])
                bld.op("vector", lambda e, gmb=gmb: e.tensor_tensor(
                    ge_b, ge_b, gmb, ALU.mult))
                w_ = [("PE", vmm)] + ([it_prev] if it_prev else [])
                last_ev = evac(bi, "vector",
                               lambda e, bi=bi, mb=mb: e.tensor_tensor(
                                   bufC[:, mb, :], ge_b, banks[bi][:],
                                   ALU.mult), waits=w_)
                ge_guard[1] = last_ev
            gp_guard[t % 2] = ("VE", last_ev)
            xf_guard[t % 2] = ("PE", bld.val("PE"))
            vit = bld.dma("sync", ITr[:, t, :, :], bufC, "ITS",
                          waits=[("VE", last_ev)])
            it_prev = ("ITS", vit)
        v_wsd_done = bld.val("WSD")
        p1b_pe = bld.val("PE")

        vW0d = bld.dma("sync", W0, WSd.rearrange("(a p) n -> p a n", p=128),
                       "W0S", waits=[("WSD", v_wsd_done), ("PE", p1b_pe)])
        bld.wait("sync", "ITS", bld.val("ITS"))
        order = [0, 2, 4, 6, 8, 10, 12, 14, 1, 3, 5, 7, 9, 11, 13, 15]
        it_guard = [gp_guard[0], gp_guard[1]]
        dn_prev = it_prev
        vrs0 = None
        for oi, t in enumerate(order):
            xf = (xf0, xf1)[oi % 2]
            w_ = [xf_guard[oi % 2]] if xf_guard[oi % 2] else []
            vxf = bld.dma("sync", xf, x1g_panel(t), "XF%d" % (oi % 2),
                          waits=w_)
            it_ = (bufA, bufB)[oi % 2]
            w_ = [it_guard[oi % 2]] if it_guard[oi % 2] else []
            vit = bld.dma("sync", it_, ITr[:, t, :, :],
                          ("BA", "BB")[oi % 2], waits=w_)
            last_ev = None
            for mb in range(16):
                bi = next_bank()
                tiles = [(W0[:, kb, mb * 128:(mb + 1) * 128], it_[:, kb, :])
                         for kb in range(16)]
                bld.wait("tensor", "W0S", vW0d)
                bld.wait("tensor", ("BA", "BB")[oi % 2], vit)
                vmm = mm_accum(bi, tiles)
                w_ = [("PE", vmm), ("XF%d" % (oi % 2), vxf)] + (
                    [dn_prev] if dn_prev else [])
                last_ev = evac(bi, "vector",
                               lambda e, bi=bi, mb=mb, xf=xf:
                               e.scalar_tensor_tensor(
                                   bufC[:, mb, :], xf[:, mb, :], 0.5,
                                   banks[bi][:], ALU.mult, ALU.add),
                               waits=w_)
            it_guard[oi % 2] = ("PE", bld.val("PE"))
            xf_guard[oi % 2] = ("VE", last_ev)
            pd = PD0 if t % 2 == 0 else PD1
            pdsem = "PDS0" if t % 2 == 0 else "PDS1"
            vpd = bld.dma("sync",
                          pd.rearrange("(r a p) n -> p r a n", p=128, r=8)
                          [:, t // 2, :, :], bufC, pdsem,
                          waits=[("VE", last_ev)])
            dn_prev = (pdsem, vpd)
            if oi == 7:
                v_pd0 = bld.val("PDS0")
                vrs0 = bld.op("gpsimd", lambda e: e.collective_compute(
                    "ReduceScatter", ALU.add, replica_groups=[list(range(8))],
                    ins=[PD0], outs=[RSo0]), inc="CC",
                    waits=[("PDS0", v_pd0)])
        v_pd1 = bld.val("PDS1")
        vrs1 = bld.op("gpsimd", lambda e: e.collective_compute(
            "ReduceScatter", ALU.add, replica_groups=[list(range(8))],
            ins=[PD1], outs=[RSo1]), inc="CC",
            waits=[("PDS1", v_pd1)])

        p2_pe = bld.val("PE")
        bld.barrier(["sync"], ["PE", "VE", "SE", "GP"])
        vout = None
        for q in range(2):
            rso = RSo0 if q == 0 else RSo1
            ccv = vrs0 if q == 0 else vrs1
            w_ = [("CC", ccv), ("PE", p2_pe if q == 0 else bld.val("PE"))]
            if vout is not None:
                w_.append(("VE", vout))
            vs2 = bld.dma("sync", s2_sb,
                          rso.rearrange("(a p) n -> p a n", p=128),
                          "S2R", waits=w_)
            vout = layer_norm(None, s2_sb, None, outp, sq2_sb,
                              [("S2R", vs2)])
            bld.dma("sync",
                    out.rearrange("(a p) t -> p a t", p=128)
                    [:, :, q * 512:(q + 1) * 512], outp,
                    "OUTS", waits=[("VE", vout)])
        bld.wait("sync", "OUTS", bld.val("OUTS"))
        bld.wait("gpsimd", "OUTS", bld.val("OUTS"))

        _finish(nc, es, bld, E)
    return nc


def _finish(nc, es, bld, E):
    block = E(nc.Block())

    @block.tensor
    def _(eng):
        for fn in bld.q["tensor"]:
            fn(eng)

    @block.vector
    def _(eng):
        for fn in bld.q["vector"]:
            fn(eng)

    @block.scalar
    def _(eng):
        for fn in bld.q["scalar"]:
            fn(eng)

    @block.gpsimd
    def _(eng):
        for fn in bld.q["gpsimd"]:
            fn(eng)

    @block.sync
    def _(eng):
        for fn in bld.q["sync"]:
            fn(eng)


GAMMA = 1.0 - 2.0 ** (-5.0 - np.arange(H, dtype=np.float64))


def _host_stage(core, inp):
    b = core // 2
    hs = core % 2
    g = GAMMA
    x = np.asarray(inp["L0_attn_x"])
    Wq = np.asarray(inp["L0_attn_Wq"])
    Wk = np.asarray(inp["L0_attn_Wk"])
    Wv = np.asarray(inp["L0_attn_Wv"])
    Wg = np.asarray(inp["L0_attn_Wg"])
    Wo = np.asarray(inp["L0_attn_Wo"])
    gn = np.asarray(inp["L0_attn_gnorm_w"])

    d = {}
    d["xT"] = np.ascontiguousarray(x[b].T)
    d["wqT"] = np.ascontiguousarray(Wq[512 * hs:512 * hs + 512].T)
    d["wkT"] = np.ascontiguousarray(Wk[512 * hs:512 * hs + 512].T)
    d["wvT"] = np.ascontiguousarray(Wv[1024 * hs:1024 * hs + 1024].T)
    d["wgaT"] = np.ascontiguousarray(Wg.T)
    d["woT"] = np.ascontiguousarray(Wo.T)
    d["gnormB"] = np.ascontiguousarray(
        np.broadcast_to(gn.astype(np.float32), (128, DVH)))
    inv = 1.0 / 10000.0 ** (np.arange(0, DKH, 2, dtype=np.float64) / DKH)
    ang = np.arange(SEQ, dtype=np.float64)[:, None] * inv[None, :]
    d["ropeC"] = np.ascontiguousarray(np.cos(ang).T.astype(np.float32))
    d["ropeS"] = np.ascontiguousarray(np.sin(ang).T.astype(np.float32))
    heads = [2 * hs, 2 * hs + 1]
    dm = np.zeros((128, 2 * CH), np.float32)
    gi = np.zeros((128, 2 * CH), np.float32)
    gk = np.zeros((128, 2), np.float32)
    gC = np.zeros((128, 2), np.float32)
    i_ = np.arange(CH)
    for j, h in enumerate(heads):
        gh = g[h]
        expo = (i_[:, None] - i_[None, :]).astype(np.float64)
        mask = np.where(expo >= 0, gh ** expo, 0.0) / 16.0
        dm[:, j * CH:(j + 1) * CH] = mask.T.astype(np.float32)
        gi[:, j * CH:(j + 1) * CH] = np.broadcast_to(
            (gh ** (i_ + 1) / 16.0).astype(np.float32), (128, CH))
        gk[0:CH, j] = (gh ** (CH - 1 - i_)).astype(np.float32)
        gC[:, j] = gh ** CH
    d["dmaskT"] = dm
    d["ginb"] = gi
    d["gkb"] = gk
    d["gCc"] = gC
    d["identM"] = np.eye(128, dtype=np.float32)
    sh = np.zeros((128, 1), np.int32)
    sh[1::2] = 4
    d["shifts"] = sh
    cp = np.zeros((128, 8), np.float32)
    cp[:, 0] = EPS
    d["cpool"] = cp
    d["onescol"] = np.ones((128, 1), np.float32)
    d["onesrow"] = np.ones((1, 128), np.float32)
    e0 = np.zeros((128, 128), np.float32)
    e0[0, :] = 1.0
    d["e0row"] = e0
    gp = np.asarray(inp["L0_ffn_gate_W_packed"])
    gs = np.asarray(inp["L0_ffn_gate_scales"])
    up = np.asarray(inp["L0_ffn_up_W_packed"])
    us = np.asarray(inp["L0_ffn_up_scales"])
    dp = np.asarray(inp["L0_ffn_down_W_packed"])
    ds = np.asarray(inp["L0_ffn_down_scales"])
    sl = slice(core * MSL, (core + 1) * MSL)
    d["gpackR"] = np.ascontiguousarray(np.repeat(gp.T[:, sl], 2, axis=0))
    d["gscalR"] = np.ascontiguousarray(np.repeat(gs.T[:, sl], 32, axis=0))
    d["upackR"] = np.ascontiguousarray(np.repeat(up.T[:, sl], 2, axis=0))
    d["uscalR"] = np.ascontiguousarray(np.repeat(us.T[:, sl], 32, axis=0))
    d["dpackR"] = np.ascontiguousarray(
        np.repeat(dp.T[core * MSL // 2:(core + 1) * MSL // 2, :], 2, axis=0))
    d["dscalR"] = np.ascontiguousarray(
        np.repeat(ds.T[core * MSL // 32:(core + 1) * MSL // 32, :], 32, axis=0))
    return d


def kernel(**inputs):
    in_maps = [_host_stage(c, inputs) for c in range(8)]
    nc = build_program()
    res = run_bass_kernel_spmd(nc, in_maps, core_ids=list(range(8)))
    full = np.empty((BB, SEQ, DM), np.float32)
    for c in range(8):
        o = np.asarray(res.results[c]["out"])
        full[c // 2, 1024 * (c % 2):1024 * (c % 2) + 1024, :] = o.T
    return full


# version 02
# speedup vs baseline: 1.2521x; 1.2521x over previous
import numpy as np
import ml_dtypes
import contextlib
import concourse.bass as bass
import concourse.mybir as mybir
from concourse.bass_utils import run_bass_kernel_spmd

F32 = mybir.dt.float32
BF16 = mybir.dt.bfloat16
I32 = mybir.dt.int32
AF = mybir.ActivationFunctionType
ALU = mybir.AluOpType
NPBF = ml_dtypes.bfloat16

BB, SEQ, DM = 4, 2048, 2048
H, DKH, DVH = 4, 256, 512
CH = 128
NCH = SEQ // CH
MLP = 16384
MSL = MLP // 8
EPS = 1e-5

DEBUG_STOP_AFTER_B = False
DEBUG_X1 = False
DEBUG_ONT = False
DEBUG_S = False
DEBUG_NO_INTER = False
DEBUG_FREEZE_SB = False
DEBUG_RETSYNC = False
DEBUG_B = False
DEBUG_BSYNC = False
DEBUG_NO_DEQUANT = False

ENGS = ("tensor", "vector", "scalar", "gpsimd", "sync")


class Bld:
    def __init__(self, nc):
        self.nc = nc
        self.q = {e: [] for e in ENGS}
        self.sems = {}
        self.seen = {e: {} for e in ENGS}

    def add_sem(self, name, handle):
        self.sems[name] = [handle, 0]

    def val(self, name):
        return self.sems[name][1]

    def emit(self, eng, fn):
        self.q[eng].append(fn)

    def wait(self, eng, name, v=None):
        if v is None:
            v = self.sems[name][1]
        if v is None or v <= 0:
            return
        if self.seen[eng].get(name, -1) >= v:
            return
        self.seen[eng][name] = v
        h = self.sems[name][0]
        self.emit(eng, lambda e, h=h, v=v: e.wait_ge(h, v))

    def op(self, eng, fn, inc=None, inc_n=1, waits=()):
        for w in waits:
            if w is None:
                continue
            n, v = w
            self.wait(eng, n, v)
        if inc is None:
            self.emit(eng, fn)
            return None
        s = self.sems[inc]
        s[1] += inc_n
        v = s[1]
        h = s[0]
        self.emit(eng, lambda e, fn=fn, h=h, n=inc_n: fn(e).then_inc(h, n))
        return v

    def dma(self, eng, out, in_, inc, waits=()):
        return self.op(eng, lambda e, o=out, i=in_: e.dma_start(out=o, in_=i),
                       inc=inc, inc_n=16, waits=waits)

    def barrier(self, eng_list, sem_list):
        for e in eng_list:
            for s in sem_list:
                self.wait(e, s, self.val(s))


def _carve(arena_ap, off_bytes, dt, shape):
    isz = {F32: 4, BF16: 2, I32: 4}[dt]
    assert off_bytes % 32 == 0, off_bytes
    n_el = 1
    for s in shape[1:]:
        n_el *= s
    a = arena_ap.bitcast(dt)
    start = off_bytes // isz
    v = a[:, start:start + n_el]
    if len(shape) == 3:
        v = v.rearrange("p (a b) -> p a b", b=shape[2])
    elif len(shape) == 4:
        v = v.rearrange("p (a b c) -> p a b c", b=shape[2], c=shape[3])
    end = off_bytes + n_el * isz
    return v[0:shape[0]], (end + 31) // 32 * 32


def build_program():
    nc = bass.Bass("TRN2", target_bir_lowering=False, debug=False)
    dt_ = nc.dram_tensor

    xT = dt_("xT", [DM, SEQ], F32, kind="ExternalInput").ap()
    wqT = dt_("wqT", [DM, 512], F32, kind="ExternalInput").ap()
    wkT = dt_("wkT", [DM, 512], F32, kind="ExternalInput").ap()
    wvT = dt_("wvT", [DM, 1024], F32, kind="ExternalInput").ap()
    wgaT = dt_("wgaT", [128 * 16 * 16, 128], F32, kind="ExternalInput").ap()
    woT = dt_("woT", [128 * 16 * 16, 128], F32, kind="ExternalInput").ap()
    gnormB = dt_("gnormB", [128, DVH], F32, kind="ExternalInput").ap()
    ropeC = dt_("ropeC", [128, SEQ], F32, kind="ExternalInput").ap()
    ropeS = dt_("ropeS", [128, SEQ], F32, kind="ExternalInput").ap()
    dmaskT = dt_("dmaskT", [128, 2 * CH], F32, kind="ExternalInput").ap()
    ginb = dt_("ginb", [128, 2 * CH], F32, kind="ExternalInput").ap()
    gkb = dt_("gkb", [128, 2], F32, kind="ExternalInput").ap()
    gCc = dt_("gCc", [128, 2], F32, kind="ExternalInput").ap()
    identM = dt_("identM", [128, 128], F32, kind="ExternalInput").ap()
    shifts = dt_("shifts", [128, 1], I32, kind="ExternalInput").ap()
    cpool = dt_("cpool", [128, 8], F32, kind="ExternalInput").ap()
    onescol = dt_("onescol", [128, 1], F32, kind="ExternalInput").ap()
    onesrow = dt_("onesrow", [1, 128], F32, kind="ExternalInput").ap()
    e0row = dt_("e0row", [128, 128], F32, kind="ExternalInput").ap()
    gpackR = dt_("gpackR", [MSL, DM], I32, kind="ExternalInput").ap()
    gscalR = dt_("gscalR", [MSL, DM], F32, kind="ExternalInput").ap()
    upackR = dt_("upackR", [MSL, DM], I32, kind="ExternalInput").ap()
    uscalR = dt_("uscalR", [MSL, DM], F32, kind="ExternalInput").ap()
    dpackR = dt_("dpackR", [MSL, DM], I32, kind="ExternalInput").ap()
    dscalR = dt_("dscalR", [MSL, DM], F32, kind="ExternalInput").ap()

    out = dt_("out", [DM, 1024], F32, kind="ExternalOutput").ap()
    dbg_x1 = dt_("dbg_x1", [DM, SEQ], F32, kind="ExternalOutput").ap() \
        if DEBUG_X1 else None
    dbg_ont = dt_("dbg_ont", [1024, SEQ], F32, kind="ExternalOutput").ap() \
        if DEBUG_ONT else None
    dbg_S = dt_("dbg_S", [128, 2048], F32, kind="ExternalOutput").ap() \
        if DEBUG_S else None
    dbg_Sb = dt_("dbg_Sb", [128, 1024], F32, kind="ExternalOutput").ap() \
        if DEBUG_S else None
    dbg_osc = dt_("dbg_osc", [128, 512], F32, kind="ExternalOutput").ap() \
        if DEBUG_S else None
    dbg_ofu = dt_("dbg_ofu", [2048, 1024], F32, kind="ExternalOutput").ap() \
        if DEBUG_B else None
    dbg_g = dt_("dbg_g", [2048, 512], F32, kind="ExternalOutput").ap() \
        if DEBUG_B else None
    dbg_ao = dt_("dbg_ao", [2048, 512], F32, kind="ExternalOutput").ap() \
        if DEBUG_B else None
    dbg_qt = dt_("dbg_qt", [128, 512], F32, kind="ExternalOutput").ap() \
        if DEBUG_S else None

    exin0 = dt_("exin0", [1024, 1024], BF16).ap()
    exin1 = dt_("exin1", [1024, 1024], BF16).ap()
    ex0 = dt_("ex0", [2048, 1024], BF16).ap()
    ex1 = dt_("ex1", [2048, 1024], BF16).ap()
    gin = dt_("gin", [DM, SEQ], BF16).ap()
    X1G = dt_("X1G", [4 * DM, SEQ], BF16).ap()
    WSg = dt_("WSg", [DM, MSL], BF16).ap()
    WSu = dt_("WSu", [DM, MSL], BF16).ap()
    WSd = dt_("WSd", [MSL, DM], BF16).ap()
    GT = dt_("GT", [16 * MSL, 512], BF16).ap()
    IT = dt_("IT", [16 * MSL, 512], BF16).ap()
    PD0 = dt_("PD0", [8 * DM, 512], BF16).ap()
    PD1 = dt_("PD1", [8 * DM, 512], BF16).ap()
    RSo0 = dt_("RSo0", [DM, 512], BF16).ap()
    RSo1 = dt_("RSo1", [DM, 512], BF16).ap()
    XB = dt_("XB", [DM, SEQ], BF16).ap()
    WGA2 = dt_("WGA2", [128 * 16 * 16, 128], BF16).ap()
    WO2 = dt_("WO2", [128 * 16 * 16, 128], BF16).ap()

    es = contextlib.ExitStack()
    with es:
        E = es.enter_context
        AR_t = E(nc.sbuf_tensor("ARENA", [128, 73728], BF16))
        AR = AR_t[:]
        s_dmask = E(nc.sbuf_tensor("s_dmask", [128, 2 * CH], F32))
        s_ginb = E(nc.sbuf_tensor("s_ginb", [128, 2 * CH], F32))
        s_gkb = E(nc.sbuf_tensor("s_gkb", [128, 2], F32))
        s_gCc = E(nc.sbuf_tensor("s_gCc", [128, 2], F32))
        s_gnorm = E(nc.sbuf_tensor("s_gnorm", [128, DVH], F32))
        s_ident = E(nc.sbuf_tensor("s_ident", [128, 128], BF16))
        s_shift = E(nc.sbuf_tensor("s_shift", [128, 1], I32))
        s_cpool = E(nc.sbuf_tensor("s_cpool", [128, 8], F32))
        s_onesc = E(nc.sbuf_tensor("s_onesc", [128, 1], BF16))
        s_onesr = E(nc.sbuf_tensor("s_onesr", [1, 128], F32))
        s_E0 = E(nc.sbuf_tensor("s_E0", [128, 128], F32))
        SC = E(nc.sbuf_tensor("SC", [128, 10752], BF16))
        DQ = E(nc.sbuf_tensor("DQ", [128, 16384], BF16))
        p_a = E(nc.psum_tensor("p_a", [128, 512], F32))
        p_b = E(nc.psum_tensor("p_b", [128, 512], F32))
        p_c = E(nc.psum_tensor("p_c", [128, 512], F32))
        p_d = E(nc.psum_tensor("p_d", [128, 512], F32))
        p_s0 = E(nc.psum_tensor("p_s0", [128, 512], F32))
        p_s1 = E(nc.psum_tensor("p_s1", [128, 512], F32))
        p_t = E(nc.psum_tensor("p_t", [128, 1024], BF16))

        bld = Bld(nc)
        for s in ("LD0", "PE", "VE", "SE", "GP", "CC", "XP", "XG",
                  "WP0", "WP1", "XR0", "XR1", "OFU", "GIN", "XF0", "XF1",
                  "BA", "BB", "W0S", "S2R", "DQP", "DQS", "WSG", "WSU",
                  "WSD", "GTS", "ITS", "PDS0", "PDS1", "EXN", "OUTS", "DBG"):
            bld.add_sem(s, E(nc.semaphore(s)))

        eps_ap = s_cpool[:, 0:1]
        zero_ap = s_cpool[:, 1:2]

        MB = 1024 * 1024
        o = 0
        wq_sb, o = _carve(AR, o, BF16, [128, 16, 512])
        wk_sb, o = _carve(AR, o, BF16, [128, 16, 512])
        wv_sb, o = _carve(AR, o, BF16, [128, 16, 1024])
        xp_sb, o = _carve(AR, o, BF16, [128, 16, 512])
        q_sb, o = _carve(AR, o, BF16, [128, 4, 512])
        k_sb, o = _carve(AR, o, BF16, [128, 4, 512])
        v_sb, o = _carve(AR, o, BF16, [128, 4, 2, 512])
        S_f, o = _carve(AR, o, F32, [128, 2, 1024])
        S_b, o = _carve(AR, o, BF16, [128, 2, 1024])
        a_end = o
        ONT_OFF = (73728 * 2) - (4 * MB // 128)
        ont_sb, _ = _carve(AR, ONT_OFF, BF16, [128, 8, 2048])
        assert a_end <= ONT_OFF, (a_end, ONT_OFF)
        o = 0
        ofu_sb, o = _carve(AR, o, BF16, [128, 16, 1024])
        g_sb, o = _carve(AR, o, BF16, [128, 16, 512])
        sres, o = _carve(AR, o, F32, [128, 16, 512])
        sbf, o = _carve(AR, o, BF16, [128, 16, 512])
        wpan0, o = _carve(AR, o, BF16, [128, 16, 128])
        wpan1, o = _carve(AR, o, BF16, [128, 16, 128])
        xres0, o = _carve(AR, o, F32, [128, 512])
        xres1, o = _carve(AR, o, F32, [128, 512])
        assert o <= ONT_OFF
        o = 0
        W0, o = _carve(AR, o, BF16, [128, 16, 2048])
        xf0, o = _carve(AR, o, BF16, [128, 16, 512])
        xf1, o = _carve(AR, o, BF16, [128, 16, 512])
        bufA, o = _carve(AR, o, BF16, [128, 16, 512])
        bufB, o = _carve(AR, o, BF16, [128, 16, 512])
        bufC, o = _carve(AR, o, BF16, [128, 16, 512])
        assert o <= 73728 * 2
        o = 0
        s2_sb, o = _carve(AR, o, BF16, [128, 16, 512])
        sq2_sb, o = _carve(AR, o, BF16, [128, 16, 512])
        outp, o = _carve(AR, o, F32, [128, 16, 512])

        o = 0
        o_scr, o = _carve(SC[:], o, F32, [128, 512])
        sq_scr, o = _carve(SC[:], o, F32, [128, 512])
        attn_sb, o = _carve(SC[:], o, BF16, [128, 128])
        ktil, o = _carve(SC[:], o, BF16, [128, 256])
        qtil, o = _carve(SC[:], o, BF16, [128, 256])
        on_sb, o = _carve(SC[:], o, BF16, [128, 512])
        ms_scr, o = _carve(SC[:], o, F32, [128, 1])
        rr_scr, o = _carve(SC[:], o, F32, [128, 1])
        statm, o = _carve(SC[:], o, F32, [128, 512])
        statr, o = _carve(SC[:], o, F32, [128, 512])
        stat_q, o = _carve(SC[:], o, F32, [1, 512])
        stat_t, o = _carve(SC[:], o, F32, [1, 512])
        mb_b, o = _carve(SC[:], o, F32, [128, 512])
        rb_b, o = _carve(SC[:], o, F32, [128, 512])
        assert o <= 10752 * 2

        o = 0
        s_ropeC, o = _carve(DQ[:], o, BF16, [128, SEQ])
        s_ropeS, o = _carve(DQ[:], o, BF16, [128, SEQ])
        t_rope, o = _carve(DQ[:], o, F32, [128, 512])
        u_rope, o = _carve(DQ[:], o, F32, [128, 512])
        o = 0
        ge_a0, o = _carve(DQ[:], o, F32, [128, 512])
        ge_a1, o = _carve(DQ[:], o, F32, [128, 512])
        ge_b0, o = _carve(DQ[:], o, F32, [128, 512])
        ge_b1, o = _carve(DQ[:], o, F32, [128, 512])
        dq_pack, o = _carve(DQ[:], o, I32, [128, 1024])
        dq_scal, o = _carve(DQ[:], o, F32, [128, 1024])
        dq_idx, o = _carve(DQ[:], o, I32, [128, 512])
        dq_c, o = _carve(DQ[:], o, I32, [128, 512])
        dq_f, o = _carve(DQ[:], o, F32, [128, 512])
        dq_r1, o = _carve(DQ[:], o, F32, [128, 512])
        dq_r2, o = _carve(DQ[:], o, F32, [128, 512])
        dq_s2, o = _carve(DQ[:], o, F32, [128, 512])
        dq_w, o = _carve(DQ[:], o, BF16, [128, 2048])
        assert o <= 16384 * 2, o

        banks = [p_a, p_b, p_c, p_d]
        bank_guard = [None, None, None, None]
        bank_i = [0]

        def next_bank():
            i = bank_i[0] % 4
            bank_i[0] += 1
            if bank_guard[i] is not None:
                bld.wait("tensor", *bank_guard[i])
            return i

        def mm_accum(bi, tiles, n_slice=None):
            pa = banks[bi][:] if n_slice is None else banks[bi][:, 0:n_slice]
            nT = len(tiles)
            v = None
            for i, (lt, rh) in enumerate(tiles):
                st = (i == 0)
                if i == nT - 1:
                    v = bld.op("tensor",
                               lambda e, lt=lt, rh=rh, st=st, pa=pa:
                               e.matmul(pa, lt, rh, start=st, stop=True),
                               inc="PE")
                else:
                    bld.op("tensor",
                           lambda e, lt=lt, rh=rh, st=st, pa=pa:
                           e.matmul(pa, lt, rh, start=st, stop=False))
            return v

        def evac(bi, eng, fn, waits=()):
            sem = {"vector": "VE", "scalar": "SE"}[eng]
            v = bld.op(eng, fn, inc=sem, waits=waits)
            bank_guard[bi] = (sem, v)
            return v

        dq_state = {"guard": []}

        def dequant_block(packR, scalR, WS, pb, st_sem):
            last = None
            for hf in range(2):
                w_ = list(dq_state["guard"])
                hsl = slice(hf * 1024, (hf + 1) * 1024)
                vld = bld.dma("scalar", dq_pack,
                              packR.rearrange("(a p) n -> p a n", p=128)
                              [:, pb, hsl], "DQP", waits=w_)
                vld2 = bld.dma("scalar", dq_scal,
                               scalR.rearrange("(a p) n -> p a n", p=128)
                               [:, pb, hsl], "DQS")
                last = _dq_half(vld, vld2, hf)
                dq_state["guard"] = [("VE", last)]
            vst = bld.dma("scalar",
                          WS.rearrange("(a p) n -> p a n", p=128)[:, pb, :],
                          dq_w, st_sem, waits=[("VE", last)])
            dq_state["guard"] = [("VE", last), (st_sem, vst)]
            return vst

        def _dq_half(vld, vld2, hf):
            last = None
            for i in range(2):
                sl = slice(i * 512, (i + 1) * 512)
                pk = dq_pack[:, sl]
                scl = dq_scal[:, sl]
                wo = dq_w[:, hf * 1024 + i * 512:hf * 1024 + (i + 1) * 512]
                bld.op("vector", lambda e, pk=pk: e.tensor_scalar(
                    dq_c, pk, s_shift[:], 7, ALU.logical_shift_right,
                    ALU.bitwise_and),
                    waits=[("DQP", vld), ("DQS", vld2)])
                bld.op("vector", lambda e, pk=pk: e.tensor_scalar(
                    dq_idx, pk, s_shift[:], 8, ALU.logical_shift_right,
                    ALU.bitwise_and))
                bld.op("vector", lambda e: e.tensor_scalar(
                    dq_f, dq_idx, -0.125, 0.5, ALU.mult, ALU.add))
                bld.op("vector", lambda e: e.tensor_scalar(
                    dq_r1, dq_c, -4.0, 0.0, ALU.add, ALU.max))
                bld.op("vector", lambda e: e.tensor_scalar(
                    dq_r2, dq_c, -6.0, 0.0, ALU.add, ALU.max))
                bld.op("vector", lambda e: e.scalar_tensor_tensor(
                    dq_r2, dq_r2, 2.0, dq_r1, ALU.mult, ALU.add))
                bld.op("vector", lambda e: e.tensor_tensor(
                    dq_s2, dq_r2, dq_c, ALU.add))
                bld.op("vector", lambda e: e.tensor_tensor(
                    dq_s2, dq_s2, dq_f, ALU.mult))
                last = bld.op("vector", lambda e, scl=scl, wo=wo: e.tensor_tensor(
                    wo, dq_s2, scl, ALU.mult), inc="VE")
            return last

        def layer_norm(src_f32, srcbf, dst_bf, dst_f32, sqbuf, ready):
            sum_ap = p_s0[0:1, :]
            sq_ap = p_s1[0:1, :]
            for kb in range(16):
                st, sp = (kb == 0), (kb == 15)
                if sp:
                    vsum = bld.op("tensor", lambda e, kb=kb, st=st: e.matmul(
                        sum_ap, s_onesc[:], srcbf[:, kb, :], start=st, stop=True),
                        inc="PE", waits=list(ready))
                else:
                    bld.op("tensor", lambda e, kb=kb, st=st: e.matmul(
                        sum_ap, s_onesc[:], srcbf[:, kb, :], start=st, stop=False),
                        waits=list(ready))
            vsq_in = bld.op("vector", lambda e: e.tensor_tensor(
                sqbuf[:], srcbf[:], srcbf[:], ALU.mult), inc="VE",
                waits=list(ready))
            for kb in range(16):
                st, sp = (kb == 0), (kb == 15)
                if sp:
                    vsq = bld.op("tensor", lambda e, kb=kb, st=st: e.matmul(
                        sq_ap, s_onesc[:], sqbuf[:, kb, :], start=st, stop=True),
                        inc="PE", waits=[("VE", vsq_in)])
                else:
                    bld.op("tensor", lambda e, kb=kb, st=st: e.matmul(
                        sq_ap, s_onesc[:], sqbuf[:, kb, :], start=st, stop=False),
                        waits=[("VE", vsq_in)])
            vme = bld.op("vector", lambda e: e.tensor_scalar(
                statm[0:1, :], sum_ap, 1.0 / 2048, None, ALU.mult),
                inc="VE", waits=[("PE", vsum)])
            bld.op("vector", lambda e: e.tensor_scalar(
                stat_q, sq_ap, 1.0 / 2048, None, ALU.mult),
                waits=[("PE", vsq)])
            bld.op("vector", lambda e: e.tensor_tensor(
                stat_t, statm[0:1, :], statm[0:1, :], ALU.mult))
            vvar = bld.op("vector", lambda e: e.tensor_tensor(
                stat_q, stat_q, stat_t, ALU.subtract), inc="VE")
            bld.op("scalar", lambda e: e.activation(
                stat_q, stat_q, AF.Ln, bias=eps_ap[0:1, :], scale=1.0),
                waits=[("VE", vvar)])
            vrs = bld.op("scalar", lambda e: e.activation(
                statr[0:1, :], stat_q, AF.Exp, bias=zero_ap[0:1, :],
                scale=-0.5), inc="SE")
            bi = next_bank()
            vbc1 = bld.op("tensor", lambda e, bi=bi: e.matmul(
                banks[bi][:], s_E0[:], statm, start=True, stop=True),
                inc="PE", waits=[("SE", vrs), ("VE", vme)])
            bj = next_bank()
            vbc2 = bld.op("tensor", lambda e, bj=bj: e.matmul(
                banks[bj][:], s_E0[:], statr, start=True, stop=True),
                inc="PE")
            evac(bi, "vector", lambda e, bi=bi: e.tensor_copy(
                mb_b, banks[bi][:]), waits=[("PE", vbc1)])
            evac(bj, "vector", lambda e, bj=bj: e.tensor_copy(
                rb_b, banks[bj][:]), waits=[("PE", vbc2)])
            last = None
            src = src_f32 if src_f32 is not None else srcbf
            dst = dst_bf if dst_bf is not None else dst_f32
            for kb in range(16):
                bld.op("vector", lambda e, kb=kb: e.tensor_tensor(
                    o_scr, src[:, kb, :], mb_b, ALU.subtract))
                last = bld.op("vector", lambda e, kb=kb: e.tensor_tensor(
                    dst[:, kb, :], o_scr, rb_b, ALU.mult), inc="VE")
            return last

        bld.dma("sync", s_dmask[:], dmaskT, "LD0")
        bld.dma("sync", s_ginb[:], ginb, "LD0")
        bld.dma("sync", s_gkb[:], gkb, "LD0")
        bld.dma("sync", s_gCc[:], gCc, "LD0")
        bld.dma("sync", s_gnorm[:], gnormB, "LD0")
        bld.dma("sync", s_shift[:], shifts, "LD0")
        bld.dma("sync", s_cpool[:], cpool, "LD0")
        bld.dma("sync", s_onesr[:], onesrow, "LD0")
        bld.dma("sync", s_E0[:], e0row, "LD0")
        bld.dma("gpsimd", s_ropeC[:], ropeC, "LD0")
        bld.dma("gpsimd", s_ropeS[:], ropeS, "LD0")
        bld.dma("gpsimd", s_ident[:], identM, "LD0")
        bld.dma("gpsimd", s_onesc[:], onescol, "LD0")
        v_xb = bld.dma("gpsimd", XB, xT, "LD0")
        v_wga2 = bld.dma("gpsimd", WGA2, wgaT, "LD0")
        v_wo2 = bld.dma("gpsimd", WO2, woT, "LD0")
        v_wq = bld.dma("gpsimd", wq_sb, wqT.rearrange("(a p) n -> p a n", p=128), "LD0")
        v_wk = bld.dma("gpsimd", wk_sb, wkT.rearrange("(a p) n -> p a n", p=128), "LD0")
        v_wv = bld.dma("gpsimd", wv_sb, wvT.rearrange("(a p) n -> p a n", p=128), "LD0")
        v_consts = bld.val("LD0")
        for e in ENGS:
            bld.wait(e, "LD0", v_consts)

        bld.op("vector", lambda e: e.memset(statm, 0.0))
        bld.op("vector", lambda e: e.memset(statr, 0.0))
        bld.op("vector", lambda e: e.memset(S_f, 0.0))
        vS1 = bld.op("vector", lambda e: e.memset(S_b, 0.0), inc="VE")

        sbf_ready = {0: ("VE", vS1), 1: ("VE", vS1)}
        xp_guard = None
        pt_guard = None
        ps_guard = None
        g_attn = {}
        dbg_guard = {}

        for p in range(4):
            cols = slice(p * 512, (p + 1) * 512)
            w_ = [xp_guard] if xp_guard else []
            vxp = bld.dma("sync", xp_sb,
                          XB.rearrange("(a p) t -> p a t", p=128)[:, :, cols],
                          "XP", waits=w_)
            for (w_sb, dst, vw) in ((wq_sb, q_sb, v_wq), (wk_sb, k_sb, v_wk)):
                for h in range(2):
                    bi = next_bank()
                    bj = next_bank()
                    vmm = None
                    for (b_, mb) in ((bi, 0), (bj, 1)):
                        m = h * 2 + mb
                        tiles = [(w_sb[:, kb, m * 128:(m + 1) * 128],
                                  xp_sb[:, kb, :]) for kb in range(16)]
                        bld.wait("tensor", "XP", vxp)
                        vmm = mm_accum(b_, tiles)
                    p0 = banks[bi][:]
                    p1 = banks[bj][:]
                    cC = s_ropeC[:, cols]
                    cS = s_ropeS[:, cols]
                    d0 = dst[:, h * 2 + 0, :]
                    d1 = dst[:, h * 2 + 1, :]
                    bld.wait("vector", "PE", vmm)
                    bld.op("vector", lambda e, p0=p0, cC=cC: e.tensor_tensor(
                        t_rope, p0, cC, ALU.mult))
                    bld.op("vector", lambda e, p1=p1, cS=cS: e.tensor_tensor(
                        u_rope, p1, cS, ALU.mult))
                    bld.op("vector", lambda e, d0=d0: e.tensor_tensor(
                        d0, t_rope, u_rope, ALU.subtract))
                    bld.op("vector", lambda e, p1=p1, cC=cC: e.tensor_tensor(
                        t_rope, p1, cC, ALU.mult))
                    bld.op("vector", lambda e, p0=p0, cS=cS: e.tensor_tensor(
                        u_rope, p0, cS, ALU.mult))
                    vev = bld.op("vector", lambda e, d1=d1: e.tensor_tensor(
                        d1, t_rope, u_rope, ALU.add), inc="VE")
                    bank_guard[bi] = ("VE", vev)
                    bank_guard[bj] = ("VE", vev)
            for tmb in range(4):
                for h in range(2):
                    bi = next_bank()
                    tiles = [(xp_sb[:, kb, tmb * 128:(tmb + 1) * 128],
                              wv_sb[:, kb, h * 512:(h + 1) * 512])
                             for kb in range(16)]
                    bld.wait("tensor", "XP", vxp)
                    vmm = mm_accum(bi, tiles)
                    evac(bi, "vector",
                         lambda e, bi=bi, tmb=tmb, h=h: e.tensor_copy(
                             v_sb[:, tmb, h, :], banks[bi][:]),
                         waits=[("PE", vmm)])
            vqk_done = bld.val("VE")
            for cc in range(4):
                ch_cols = slice(cc * 128, (cc + 1) * 128)
                gch = 4 * p + cc
                for h in range(2):
                    bi = next_bank()
                    tiles = [(k_sb[:, h * 2 + mb, ch_cols],
                              q_sb[:, h * 2 + mb, ch_cols]) for mb in range(2)]
                    bld.wait("tensor", "VE", vqk_done)
                    vat = mm_accum(bi, tiles, n_slice=128)
                    vev = evac(bi, "vector",
                               lambda e, bi=bi, h=h: e.tensor_tensor(
                                   attn_sb, banks[bi][:, 0:128],
                                   s_dmask[:, h * CH:(h + 1) * CH], ALU.mult),
                               waits=[("PE", vat), g_attn.get("attn")])
                    if pt_guard is not None:
                        bld.wait("tensor", *pt_guard)
                    vtr = None
                    for mb in range(2):
                        vtr = bld.op("tensor",
                                     lambda e, h=h, mb=mb, ch_cols=ch_cols:
                                     e.transpose(
                                         p_t[:, mb * 128:(mb + 1) * 128],
                                         k_sb[:, h * 2 + mb, ch_cols],
                                         s_ident[:]), inc="PE")
                    vkt = bld.op("vector", lambda e, h=h: e.tensor_scalar(
                        ktil, p_t[:, 0:256], s_gkb[:, h:h + 1], None,
                        ALU.mult), inc="VE",
                        waits=[("PE", vtr), g_attn.get("ktil")])
                    vqt = None
                    for mb in range(2):
                        vqt = bld.op("vector",
                                     lambda e, h=h, mb=mb, ch_cols=ch_cols:
                                     e.tensor_tensor(
                                         qtil[:, mb * 128:(mb + 1) * 128],
                                         q_sb[:, h * 2 + mb, ch_cols],
                                         s_ginb[:, h * CH:(h + 1) * CH],
                                         ALU.mult), inc="VE",
                                     waits=[g_attn.get("qtil")])
                    bo = next_bank()
                    bld.wait("tensor", "VE", vev)
                    if DEBUG_NO_INTER:
                        vo = bld.op("tensor", lambda e, bo=bo, cc=cc, h=h:
                                    e.matmul(banks[bo][:], attn_sb,
                                             v_sb[:, cc, h, :],
                                             start=True, stop=True), inc="PE")
                        bld.wait("tensor", "VE", vqt)
                        bld.wait("tensor", *sbf_ready[h])
                    else:
                        bld.op("tensor", lambda e, bo=bo, cc=cc, h=h: e.matmul(
                            banks[bo][:], attn_sb, v_sb[:, cc, h, :],
                            start=True, stop=False))
                        bld.wait("tensor", "VE", vqt)
                        bld.wait("tensor", *sbf_ready[h])
                        bld.op("tensor", lambda e, bo=bo, h=h: e.matmul(
                            banks[bo][:], qtil[:, 0:128], S_b[:, h, 0:512],
                            start=False, stop=False))
                        vo = bld.op("tensor", lambda e, bo=bo, h=h: e.matmul(
                            banks[bo][:], qtil[:, 128:256],
                            S_b[:, h, 512:1024],
                            start=False, stop=True), inc="PE")
                    g_attn["attn"] = ("PE", vo)
                    g_attn["qtil"] = ("PE", vo)
                    bld.wait("tensor", "VE", vkt)
                    if ps_guard is not None:
                        bld.wait("tensor", *ps_guard)
                    vsu = None
                    for mb, psx in ((0, p_s0), (1, p_s1)):
                        vsu = bld.op("tensor",
                                     lambda e, mb=mb, cc=cc, h=h, psx=psx:
                                     e.matmul(
                                         psx[:],
                                         ktil[:, mb * 128:(mb + 1) * 128],
                                         v_sb[:, cc, h, :], start=True,
                                         stop=True), inc="PE")
                    g_attn["ktil"] = ("PE", vsu)
                    bld.op("vector", lambda e, h=h: e.scalar_tensor_tensor(
                        S_f[:, h, 0:512], S_f[:, h, 0:512], s_gCc[:, h:h + 1],
                        p_s0[:], ALU.mult, ALU.add),
                        waits=[("PE", vsu), dbg_guard.get("vg")])
                    va = bld.op("vector", lambda e, h=h: e.scalar_tensor_tensor(
                        S_f[:, h, 512:1024], S_f[:, h, 512:1024],
                        s_gCc[:, h:h + 1], p_s1[:], ALU.mult, ALU.add),
                        inc="VE")
                    ps_guard = ("VE", va)
                    if not DEBUG_FREEZE_SB:
                        vsb = bld.op("vector", lambda e, h=h: e.tensor_copy(
                            S_b[:, h, :], S_f[:, h, :]), inc="VE")
                        sbf_ready[h] = ("VE", vsb)
                    bld.wait("vector", "PE", vo)
                    bld.op("vector", lambda e, bo=bo: e.tensor_copy(
                        o_scr, banks[bo][:]))
                    bld.op("vector", lambda e: e.tensor_tensor(
                        sq_scr, o_scr, o_scr, ALU.mult))
                    voc = bld.op("vector", lambda e: e.tensor_reduce(
                        ms_scr, sq_scr, mybir.AxisListType.X, ALU.add),
                        inc="VE")
                    bank_guard[bo] = ("VE", voc)
                    bld.op("scalar", lambda e: e.activation(
                        rr_scr, ms_scr, AF.Ln, bias=eps_ap, scale=1.0 / DVH),
                        waits=[("VE", voc)])
                    vrr = bld.op("scalar", lambda e: e.activation(
                        rr_scr, rr_scr, AF.Exp, bias=zero_ap, scale=-0.5),
                        inc="SE")
                    von = bld.op("vector", lambda e: e.scalar_tensor_tensor(
                        on_sb, o_scr, rr_scr, s_gnorm[:], ALU.mult, ALU.mult),
                        inc="VE",
                        waits=[("SE", vrr), g_attn.get("on")])
                    bld.wait("tensor", "VE", vkt)
                    vto = None
                    for db in range(4):
                        vto = bld.op("tensor", lambda e, db=db: e.transpose(
                            p_t[:, 256 + db * 128:256 + (db + 1) * 128],
                            on_sb[:, db * 128:(db + 1) * 128], s_ident[:]),
                            inc="PE", waits=[("VE", von)])
                    g_attn["on"] = ("PE", vto)
                    dst = ont_sb[:, h * 4:(h + 1) * 4,
                                 gch * 128:(gch + 1) * 128]
                    src = p_t[:, 256:768].rearrange("p (a b) -> p a b", b=128)
                    vont = bld.op("vector",
                                  lambda e, dst=dst, src=src: e.tensor_copy(
                                      dst, src), inc="VE",
                                  waits=[("PE", vto)])
                    pt_guard = ("VE", vont)
                    if DEBUG_S and p == 0 and cc == 0 and h == 1:
                        vdS = bld.dma("gpsimd", dbg_S, S_f.rearrange(
                            "p a b -> p (a b)"), "DBG",
                            waits=[("SE", bld.val("SE"))])
                        dbg_guard["vg"] = ("DBG", vdS)
                    if DEBUG_S and p == 0 and cc == 1 and h == 0:
                        v1_ = bld.dma("gpsimd", dbg_qt[:, 0:256], qtil,
                                      "DBG", waits=[("VE", vqt)])
                        v2_ = bld.dma("gpsimd", dbg_qt[:, 256:512], ktil,
                                      "DBG", waits=[("VE", vkt)])
                        v3_ = bld.dma("gpsimd", dbg_Sb, S_b[:, 0, :], "DBG")
                        v4_ = bld.dma("gpsimd", dbg_osc, o_scr, "DBG")
                        g_attn["qtil"] = ("DBG", v4_)
                        g_attn["ktil"] = ("DBG", v4_)
                        g_attn["on"] = ("DBG", v4_)
                        sbf_ready[0] = ("DBG", v4_)
                        dbg_guard["vg"] = ("DBG", v4_)
            xp_guard = ("PE", bld.val("PE"))

        v_ont_done = bld.val("VE")
        if DEBUG_ONT:
            bld.dma("gpsimd", dbg_ont.rearrange("(a p) t -> p a t", p=128),
                    ont_sb, "DBG", waits=[("VE", v_ont_done)])
        vex0 = bld.dma("gpsimd", exin0.rearrange("(a p) t -> p a t", p=128),
                       ont_sb[:, :, 0:1024], "EXN",
                       waits=[("VE", v_ont_done)])
        vex1 = bld.dma("gpsimd", exin1.rearrange("(a p) t -> p a t", p=128),
                       ont_sb[:, :, 1024:2048], "EXN")
        pair_groups = [[0, 1], [2, 3], [4, 5], [6, 7]]
        bld.op("gpsimd", lambda e: e.collective_compute(
            "AllGather", ALU.bypass, replica_groups=pair_groups,
            ins=[exin0], outs=[ex0]), inc="CC",
            waits=[("EXN", vex1)])
        bld.op("gpsimd", lambda e: e.collective_compute(
            "AllGather", ALU.bypass, replica_groups=pair_groups,
            ins=[exin1], outs=[ex1]), inc="CC")

        a_pe = bld.val("PE")
        bld.barrier(["sync", "gpsimd"], ["PE", "VE", "SE", "GP"])
        xg_guard = None
        wp_guard = [None, None]
        xr_guard = [None, None]
        og_prev = None
        gin_prev = None
        ofu_pe_guard = a_pe
        for half in range(2):
            exbuf = ex0 if half == 0 else ex1
            vof = bld.dma("sync", ofu_sb,
                          exbuf.rearrange("(a p) t -> p a t", p=128),
                          "OFU",
                          waits=[("CC", half + 1), ("PE", ofu_pe_guard)])
            for grp in range(2):
                gcols = slice(half * 1024 + grp * 512,
                              half * 1024 + grp * 512 + 512)
                ocols = slice(grp * 512, (grp + 1) * 512)
                w_ = list(xg_guard) if xg_guard else []
                vxg = bld.dma("sync", sbf,
                              XB.rearrange("(a p) t -> p a t", p=128)
                              [:, :, gcols], "XG", waits=w_)
                vsilu = None
                for mb in range(16):
                    wp = (wpan0, wpan1)[mb % 2]
                    w_ = [wp_guard[mb % 2]] if wp_guard[mb % 2] else []
                    vwp = bld.dma("sync", wp,
                                  WGA2.rearrange("(p m a) n -> p m a n",
                                                 p=128, m=16)[:, mb, :, :],
                                  "WP%d" % (mb % 2), waits=w_)
                    bi = next_bank()
                    tiles = [(wp[:, kb, :], sbf[:, kb, :]) for kb in range(16)]
                    bld.wait("tensor", "WP%d" % (mb % 2), vwp)
                    bld.wait("tensor", "XG", vxg)
                    vmm = mm_accum(bi, tiles)
                    wp_guard[mb % 2] = ("PE", vmm)
                    vsilu = evac(bi, "scalar",
                                 lambda e, bi=bi, mb=mb: e.activation(
                                     g_sb[:, mb, :], banks[bi][:], AF.Silu,
                                     bias=0.0, scale=1.0),
                                 waits=[("PE", vmm), og_prev])
                if DEBUG_B and half == 0 and grp == 0:
                    vd_ = bld.dma("gpsimd",
                                  dbg_ofu.rearrange("(a p) t -> p a t", p=128),
                                  ofu_sb, "DBG", waits=[("OFU", vof)])
                    vd2_ = bld.dma("gpsimd",
                                   dbg_g.rearrange("(a p) t -> p a t", p=128),
                                   g_sb, "DBG", waits=[("SE", vsilu)])
                    dbg_guard["og"] = ("DBG", vd2_)
                vog = bld.op("vector", lambda e, ocols=ocols: e.tensor_tensor(
                    ofu_sb[:, :, ocols], ofu_sb[:, :, ocols], g_sb[:],
                    ALU.mult), inc="VE",
                    waits=[("SE", vsilu), ("OFU", vof),
                           dbg_guard.get("og")])
                og_prev = ("VE", vog)
                vsh = None
                for mb in range(16):
                    wp = (wpan0, wpan1)[mb % 2]
                    w_ = [wp_guard[mb % 2]] if wp_guard[mb % 2] else []
                    vwp = bld.dma("sync", wp,
                                  WO2.rearrange("(p m a) n -> p m a n",
                                                 p=128, m=16)[:, mb, :, :],
                                  "WP%d" % (mb % 2), waits=w_)
                    xr = (xres0, xres1)[mb % 2]
                    w_ = [xr_guard[mb % 2]] if xr_guard[mb % 2] else []
                    vxr = bld.dma("sync", xr,
                                  xT.rearrange("(a p) t -> p a t", p=128)
                                  [:, mb, gcols], "XR%d" % (mb % 2), waits=w_)
                    bi = next_bank()
                    tiles = [(wp[:, kb, :], ofu_sb[:, kb, ocols])
                             for kb in range(16)]
                    bld.wait("tensor", "WP%d" % (mb % 2), vwp)
                    bld.wait("tensor", "VE", vog)
                    vmm = mm_accum(bi, tiles)
                    wp_guard[mb % 2] = ("PE", vmm)
                    vadd = bld.op("vector",
                                  lambda e, bi=bi, mb=mb, xr=xr:
                                  e.tensor_tensor(sres[:, mb, :],
                                                  banks[bi][:], xr, ALU.add),
                                  inc="VE",
                                  waits=[("PE", vmm),
                                         ("XR%d" % (mb % 2), vxr),
                                         gin_prev])
                    xr_guard[mb % 2] = ("VE", vadd)
                    vsh = evac(bi, "vector", lambda e, mb=mb: e.tensor_copy(
                        sbf[:, mb, :], sres[:, mb, :]))
                if DEBUG_B and half == 0 and grp == 0:
                    vd3_ = bld.dma("gpsimd",
                                   dbg_ao.rearrange("(a p) t -> p a t", p=128),
                                   sres, "DBG", waits=[("VE", vsh)])
                    vsh_g = ("DBG", vd3_)
                else:
                    vsh_g = None
                vx1 = layer_norm(sres, sbf, sbf, None, g_sb,
                                 [("VE", vsh), vsh_g] if vsh_g else
                                 [("VE", vsh)])
                gin_dst = gin.rearrange("(a p) t -> p a t", p=128)[:, :, gcols]
                vgin = bld.dma("sync", gin_dst, sbf, "GIN",
                               waits=[("VE", vx1)])
                gin_prev = ("GIN", vgin)
                if DEBUG_X1:
                    dbg_dst = dbg_x1.rearrange("(a p) t -> p a t", p=128)[:, :, gcols]
                    vdbg = bld.dma("gpsimd", dbg_dst, sbf, "DBG",
                                   waits=[("VE", vx1)])
                    gin_prev = ("DBG", vdbg)
                xg_guard = [("VE", vx1), gin_prev]
                ofu_pe_guard = bld.val("PE")
                if DEBUG_BSYNC:
                    vpe_ = bld.op("tensor", lambda e: e.nop(), inc="PE")
                    vve_ = bld.op("vector", lambda e: e.engine_nop(), inc="VE")
                    vse_ = bld.op("scalar", lambda e: e.nop(), inc="SE")
                    vgp_ = bld.op("gpsimd", lambda e: e.engine_nop(), inc="GP")
                    for eng_ in ("tensor", "vector", "scalar", "gpsimd", "sync"):
                        bld.wait(eng_, "PE", vpe_)
                        bld.wait(eng_, "VE", vve_)
                        bld.wait(eng_, "SE", vse_)
                        bld.wait(eng_, "GP", vgp_)
                if not DEBUG_NO_DEQUANT:
                    for pb in range(half * 8 + grp * 4,
                                    half * 8 + grp * 4 + 4):
                        dequant_block(gpackR, gscalR, WSg, pb, "WSG")
        v_wsg_done = bld.val("WSG")
        v_gin_done = bld.val("GIN")

        batch_groups = [[0, 2, 4, 6], [1, 3, 5, 7]]
        bld.op("gpsimd", lambda e: e.collective_compute(
            "AllGather", ALU.bypass, replica_groups=batch_groups,
            ins=[gin], outs=[X1G]), inc="CC",
            waits=[("GIN", v_gin_done)])
        ccx = bld.val("CC")

        if DEBUG_STOP_AFTER_B:
            bld.wait("gpsimd", "DBG", bld.val("DBG"))
            bld.wait("gpsimd", "GIN", bld.val("GIN"))
            bld.wait("gpsimd", "CC", bld.val("CC"))
            _finish(nc, es, bld, E)
            return nc

        b_pe = bld.val("PE")
        bld.barrier(["sync", "gpsimd"], ["PE", "VE", "SE", "GP"])
        vW0 = bld.dma("sync", W0, WSg.rearrange("(a p) n -> p a n", p=128),
                      "W0S", waits=[("WSG", v_wsg_done), ("PE", b_pe)])

        def x1g_panel(t):
            rb, c0 = t // 4, (t % 4) * 512
            return X1G.rearrange("(rb a p) t -> p rb a t", p=128, rb=4)[
                :, rb, :, c0:c0 + 512]

        GTr = GT.rearrange("(t a p) n -> p t a n", p=128, t=16)
        ITr = IT.rearrange("(t a p) n -> p t a n", p=128, t=16)

        xf_guard = [None, None]
        gA_guard = None
        for t in range(16):
            xf = (xf0, xf1)[t % 2]
            w_ = [("CC", ccx)] + ([xf_guard[t % 2]] if xf_guard[t % 2] else [])
            vxf = bld.dma("sync", xf, x1g_panel(t), "XF%d" % (t % 2),
                          waits=w_)
            last_ev = None
            for mb in range(16):
                bi = next_bank()
                tiles = [(W0[:, kb, mb * 128:(mb + 1) * 128], xf[:, kb, :])
                         for kb in range(16)]
                bld.wait("tensor", "W0S", vW0)
                bld.wait("tensor", "XF%d" % (t % 2), vxf)
                vmm = mm_accum(bi, tiles)
                last_ev = evac(bi, "scalar",
                               lambda e, bi=bi, mb=mb: e.activation(
                                   bufA[:, mb, :], banks[bi][:], AF.Copy,
                                   bias=0.0, scale=1.0),
                               waits=[("PE", vmm), gA_guard])
            xf_guard[t % 2] = ("PE", bld.val("PE"))
            vgt = bld.dma("sync", GTr[:, t, :, :], bufA, "GTS",
                          waits=[("SE", last_ev)])
            gA_guard = ("GTS", vgt)
            for j in range(2):
                pb = 2 * t + j
                if pb < 16:
                    dequant_block(upackR, uscalR, WSu, pb, "WSU")
                else:
                    dequant_block(dpackR, dscalR, WSd, pb - 16, "WSD")
        v_wsu_done = bld.val("WSU")
        p1a_pe = bld.val("PE")

        vW0u = bld.dma("sync", W0, WSu.rearrange("(a p) n -> p a n", p=128),
                       "W0S", waits=[("WSU", v_wsu_done), ("PE", p1a_pe)])
        bld.wait("sync", "GTS", bld.val("GTS"))
        gp_guard = [None, None]
        it_prev = None
        ge_guard = [None, None]
        ge_guard2 = [None, None]
        for t in range(16):
            xf = (xf0, xf1)[t % 2]
            w_ = [xf_guard[t % 2]] if xf_guard[t % 2] else []
            vxf = bld.dma("sync", xf, x1g_panel(t), "XF%d" % (t % 2),
                          waits=w_)
            gp = (bufA, bufB)[t % 2]
            w_ = [gp_guard[t % 2]] if gp_guard[t % 2] else []
            vgp = bld.dma("sync", gp, GTr[:, t, :, :],
                          ("BA", "BB")[t % 2], waits=w_)
            last_ev = None
            vth_last = None
            for mb in range(16):
                bi = next_bank()
                tiles = [(W0[:, kb, mb * 128:(mb + 1) * 128], xf[:, kb, :])
                         for kb in range(16)]
                bld.wait("tensor", "W0S", vW0u)
                bld.wait("tensor", "XF%d" % (t % 2), vxf)
                vmm = mm_accum(bi, tiles)
                gmb = gp[:, mb, :]
                ga = (ge_a0, ge_a1)[mb % 2]
                gb = (ge_b0, ge_b1)[mb % 2]
                bld.wait("vector", ("BA", "BB")[t % 2], vgp)
                w_ = [("SE", ge_guard[mb % 2])] if ge_guard[mb % 2] else []
                bld.op("vector", lambda e, gmb=gmb, ga=ga: e.tensor_tensor(
                    ga, gmb, gmb, ALU.mult), waits=w_)
                bld.op("vector", lambda e, gmb=gmb, ga=ga: e.tensor_tensor(
                    ga, ga, gmb, ALU.mult))
                vu = bld.op("vector", lambda e, gmb=gmb, ga=ga:
                            e.scalar_tensor_tensor(
                                ga, ga, 0.044715, gmb, ALU.mult, ALU.add),
                            inc="VE")
                w_ = [("VE", ge_guard2[mb % 2])] if ge_guard2[mb % 2] else []
                vth = bld.op("scalar", lambda e, ga=ga, gb=gb: e.activation(
                    gb, ga, AF.Tanh, bias=0.0, scale=0.7978845608028654),
                    inc="SE", waits=[("VE", vu)] + w_)
                ge_guard[mb % 2] = vth
                vth_last = vth
                bld.op("vector", lambda e, gb=gb: e.tensor_scalar(
                    gb, gb, 0.5, 0.5, ALU.mult, ALU.add),
                    waits=[("SE", vth)])
                bld.op("vector", lambda e, gmb=gmb, gb=gb: e.tensor_tensor(
                    gb, gb, gmb, ALU.mult))
                w_ = [("PE", vmm)] + ([it_prev] if it_prev else [])
                last_ev = evac(bi, "vector",
                               lambda e, bi=bi, mb=mb, gb=gb: e.tensor_tensor(
                                   bufC[:, mb, :], gb, banks[bi][:],
                                   ALU.mult), waits=w_)
                ge_guard2[mb % 2] = last_ev
            gp_guard[t % 2] = ("VE", last_ev)
            xf_guard[t % 2] = ("PE", bld.val("PE"))
            vit = bld.dma("sync", ITr[:, t, :, :], bufC, "ITS",
                          waits=[("VE", last_ev)])
            it_prev = ("ITS", vit)
        v_wsd_done = bld.val("WSD")
        p1b_pe = bld.val("PE")

        vW0d = bld.dma("sync", W0, WSd.rearrange("(a p) n -> p a n", p=128),
                       "W0S", waits=[("WSD", v_wsd_done), ("PE", p1b_pe)])
        bld.wait("sync", "ITS", bld.val("ITS"))
        order = [0, 2, 4, 6, 8, 10, 12, 14, 1, 3, 5, 7, 9, 11, 13, 15]
        it_guard = [gp_guard[0], gp_guard[1]]
        dn_prev = it_prev
        vrs0 = None
        for oi, t in enumerate(order):
            xf = (xf0, xf1)[oi % 2]
            w_ = [xf_guard[oi % 2]] if xf_guard[oi % 2] else []
            vxf = bld.dma("sync", xf, x1g_panel(t), "XF%d" % (oi % 2),
                          waits=w_)
            it_ = (bufA, bufB)[oi % 2]
            w_ = [it_guard[oi % 2]] if it_guard[oi % 2] else []
            vit = bld.dma("sync", it_, ITr[:, t, :, :],
                          ("BA", "BB")[oi % 2], waits=w_)
            last_ev = None
            for mb in range(16):
                bi = next_bank()
                tiles = [(W0[:, kb, mb * 128:(mb + 1) * 128], it_[:, kb, :])
                         for kb in range(16)]
                bld.wait("tensor", "W0S", vW0d)
                bld.wait("tensor", ("BA", "BB")[oi % 2], vit)
                vmm = mm_accum(bi, tiles)
                w_ = [("PE", vmm), ("XF%d" % (oi % 2), vxf)] + (
                    [dn_prev] if dn_prev else [])
                last_ev = evac(bi, "vector",
                               lambda e, bi=bi, mb=mb, xf=xf:
                               e.scalar_tensor_tensor(
                                   bufC[:, mb, :], xf[:, mb, :], 0.5,
                                   banks[bi][:], ALU.mult, ALU.add),
                               waits=w_)
            it_guard[oi % 2] = ("PE", bld.val("PE"))
            xf_guard[oi % 2] = ("VE", last_ev)
            pd = PD0 if t % 2 == 0 else PD1
            pdsem = "PDS0" if t % 2 == 0 else "PDS1"
            vpd = bld.dma("sync",
                          pd.rearrange("(r a p) n -> p r a n", p=128, r=8)
                          [:, t // 2, :, :], bufC, pdsem,
                          waits=[("VE", last_ev)])
            dn_prev = (pdsem, vpd)
            if oi == 7:
                v_pd0 = bld.val("PDS0")
                vrs0 = bld.op("gpsimd", lambda e: e.collective_compute(
                    "ReduceScatter", ALU.add, replica_groups=[list(range(8))],
                    ins=[PD0], outs=[RSo0]), inc="CC",
                    waits=[("PDS0", v_pd0)])
        v_pd1 = bld.val("PDS1")
        vrs1 = bld.op("gpsimd", lambda e: e.collective_compute(
            "ReduceScatter", ALU.add, replica_groups=[list(range(8))],
            ins=[PD1], outs=[RSo1]), inc="CC",
            waits=[("PDS1", v_pd1)])

        p2_pe = bld.val("PE")
        bld.barrier(["sync"], ["PE", "VE", "SE", "GP"])
        vout = None
        for q in range(2):
            rso = RSo0 if q == 0 else RSo1
            ccv = vrs0 if q == 0 else vrs1
            w_ = [("CC", ccv), ("PE", p2_pe if q == 0 else bld.val("PE"))]
            if vout is not None:
                w_.append(("VE", vout))
            vs2 = bld.dma("sync", s2_sb,
                          rso.rearrange("(a p) n -> p a n", p=128),
                          "S2R", waits=w_)
            vout = layer_norm(None, s2_sb, None, outp, sq2_sb,
                              [("S2R", vs2)])
            bld.dma("sync",
                    out.rearrange("(a p) t -> p a t", p=128)
                    [:, :, q * 512:(q + 1) * 512], outp,
                    "OUTS", waits=[("VE", vout)])
        bld.wait("sync", "OUTS", bld.val("OUTS"))
        bld.wait("gpsimd", "OUTS", bld.val("OUTS"))

        _finish(nc, es, bld, E)
    return nc


def _finish(nc, es, bld, E):
    block = E(nc.Block())

    @block.tensor
    def _(eng):
        for fn in bld.q["tensor"]:
            fn(eng)

    @block.vector
    def _(eng):
        for fn in bld.q["vector"]:
            fn(eng)

    @block.scalar
    def _(eng):
        for fn in bld.q["scalar"]:
            fn(eng)

    @block.gpsimd
    def _(eng):
        for fn in bld.q["gpsimd"]:
            fn(eng)

    @block.sync
    def _(eng):
        for fn in bld.q["sync"]:
            fn(eng)


GAMMA = 1.0 - 2.0 ** (-5.0 - np.arange(H, dtype=np.float64))


def _host_stage(core, inp):
    b = core // 2
    hs = core % 2
    g = GAMMA
    x = np.asarray(inp["L0_attn_x"])
    Wq = np.asarray(inp["L0_attn_Wq"])
    Wk = np.asarray(inp["L0_attn_Wk"])
    Wv = np.asarray(inp["L0_attn_Wv"])
    Wg = np.asarray(inp["L0_attn_Wg"])
    Wo = np.asarray(inp["L0_attn_Wo"])
    gn = np.asarray(inp["L0_attn_gnorm_w"])

    d = {}
    d["xT"] = np.ascontiguousarray(x[b].T)
    d["wqT"] = np.ascontiguousarray(Wq[512 * hs:512 * hs + 512].T)
    d["wkT"] = np.ascontiguousarray(Wk[512 * hs:512 * hs + 512].T)
    d["wvT"] = np.ascontiguousarray(Wv[1024 * hs:1024 * hs + 1024].T)
    def _ilv(w):
        a = np.ascontiguousarray(w.T).reshape(16, 128, 16, 128)
        return np.ascontiguousarray(
            a.transpose(1, 2, 0, 3).reshape(128 * 16 * 16, 128))
    d["wgaT"] = _ilv(Wg)
    d["woT"] = _ilv(Wo)
    d["gnormB"] = np.ascontiguousarray(
        np.broadcast_to(gn.astype(np.float32), (128, DVH)))
    inv = 1.0 / 10000.0 ** (np.arange(0, DKH, 2, dtype=np.float64) / DKH)
    ang = np.arange(SEQ, dtype=np.float64)[:, None] * inv[None, :]
    d["ropeC"] = np.ascontiguousarray(np.cos(ang).T.astype(np.float32))
    d["ropeS"] = np.ascontiguousarray(np.sin(ang).T.astype(np.float32))
    heads = [2 * hs, 2 * hs + 1]
    dm = np.zeros((128, 2 * CH), np.float32)
    gi = np.zeros((128, 2 * CH), np.float32)
    gk = np.zeros((128, 2), np.float32)
    gC = np.zeros((128, 2), np.float32)
    i_ = np.arange(CH)
    for j, h in enumerate(heads):
        gh = g[h]
        expo = (i_[:, None] - i_[None, :]).astype(np.float64)
        mask = np.where(expo >= 0, gh ** expo, 0.0) / 16.0
        dm[:, j * CH:(j + 1) * CH] = mask.T.astype(np.float32)
        gi[:, j * CH:(j + 1) * CH] = np.broadcast_to(
            (gh ** (i_ + 1) / 16.0).astype(np.float32), (128, CH))
        gk[0:CH, j] = (gh ** (CH - 1 - i_)).astype(np.float32)
        gC[:, j] = gh ** CH
    d["dmaskT"] = dm
    d["ginb"] = gi
    d["gkb"] = gk
    d["gCc"] = gC
    d["identM"] = np.eye(128, dtype=np.float32)
    sh = np.zeros((128, 1), np.int32)
    sh[1::2] = 4
    d["shifts"] = sh
    cp = np.zeros((128, 8), np.float32)
    cp[:, 0] = EPS
    d["cpool"] = cp
    d["onescol"] = np.ones((128, 1), np.float32)
    d["onesrow"] = np.ones((1, 128), np.float32)
    e0 = np.zeros((128, 128), np.float32)
    e0[0, :] = 1.0
    d["e0row"] = e0
    gp = np.asarray(inp["L0_ffn_gate_W_packed"])
    gs = np.asarray(inp["L0_ffn_gate_scales"])
    up = np.asarray(inp["L0_ffn_up_W_packed"])
    us = np.asarray(inp["L0_ffn_up_scales"])
    dp = np.asarray(inp["L0_ffn_down_W_packed"])
    ds = np.asarray(inp["L0_ffn_down_scales"])
    sl = slice(core * MSL, (core + 1) * MSL)
    d["gpackR"] = np.ascontiguousarray(np.repeat(gp.T[:, sl], 2, axis=0))
    d["gscalR"] = np.ascontiguousarray(np.repeat(gs.T[:, sl], 32, axis=0))
    d["upackR"] = np.ascontiguousarray(np.repeat(up.T[:, sl], 2, axis=0))
    d["uscalR"] = np.ascontiguousarray(np.repeat(us.T[:, sl], 32, axis=0))
    d["dpackR"] = np.ascontiguousarray(
        np.repeat(dp.T[core * MSL // 2:(core + 1) * MSL // 2, :], 2, axis=0))
    d["dscalR"] = np.ascontiguousarray(
        np.repeat(ds.T[core * MSL // 32:(core + 1) * MSL // 32, :], 32, axis=0))
    return d


def kernel(**inputs):
    in_maps = [_host_stage(c, inputs) for c in range(8)]
    nc = build_program()
    res = run_bass_kernel_spmd(nc, in_maps, core_ids=list(range(8)))
    full = np.empty((BB, SEQ, DM), np.float32)
    for c in range(8):
        o = np.asarray(res.results[c]["out"])
        full[c // 2, 1024 * (c % 2):1024 * (c % 2) + 1024, :] = o.T
    return full


# version 03
# speedup vs baseline: 1.3725x; 1.0962x over previous
import numpy as np
import ml_dtypes
import contextlib
import concourse.bass as bass
import concourse.mybir as mybir
from concourse.bass_utils import run_bass_kernel_spmd

F32 = mybir.dt.float32
BF16 = mybir.dt.bfloat16
I32 = mybir.dt.int32
AF = mybir.ActivationFunctionType
ALU = mybir.AluOpType
NPBF = ml_dtypes.bfloat16

BB, SEQ, DM = 4, 2048, 2048
H, DKH, DVH = 4, 256, 512
CH = 128
NCH = SEQ // CH
MLP = 16384
MSL = MLP // 8
EPS = 1e-5

DEBUG_STOP_AFTER_B = False
DEBUG_X1 = False
DEBUG_ONT = False
DEBUG_S = False
DEBUG_NO_INTER = False
DEBUG_FREEZE_SB = False
DEBUG_RETSYNC = False
DEBUG_B = False
DEBUG_BSYNC = False
DEBUG_NO_DEQUANT = False

ENGS = ("tensor", "vector", "scalar", "gpsimd", "sync")


class Bld:
    def __init__(self, nc):
        self.nc = nc
        self.q = {e: [] for e in ENGS}
        self.sems = {}
        self.seen = {e: {} for e in ENGS}

    def add_sem(self, name, handle):
        self.sems[name] = [handle, 0]

    def val(self, name):
        return self.sems[name][1]

    def emit(self, eng, fn):
        self.q[eng].append(fn)

    def wait(self, eng, name, v=None):
        if v is None:
            v = self.sems[name][1]
        if v is None or v <= 0:
            return
        if self.seen[eng].get(name, -1) >= v:
            return
        self.seen[eng][name] = v
        h = self.sems[name][0]
        self.emit(eng, lambda e, h=h, v=v: e.wait_ge(h, v))

    def op(self, eng, fn, inc=None, inc_n=1, waits=()):
        for w in waits:
            if w is None:
                continue
            n, v = w
            self.wait(eng, n, v)
        if inc is None:
            self.emit(eng, fn)
            return None
        s = self.sems[inc]
        s[1] += inc_n
        v = s[1]
        h = s[0]
        self.emit(eng, lambda e, fn=fn, h=h, n=inc_n: fn(e).then_inc(h, n))
        return v

    def dma(self, eng, out, in_, inc, waits=()):
        return self.op(eng, lambda e, o=out, i=in_: e.dma_start(out=o, in_=i),
                       inc=inc, inc_n=16, waits=waits)

    def barrier(self, eng_list, sem_list):
        for e in eng_list:
            for s in sem_list:
                self.wait(e, s, self.val(s))


def _carve(arena_ap, off_bytes, dt, shape):
    isz = {F32: 4, BF16: 2, I32: 4}[dt]
    assert off_bytes % 32 == 0, off_bytes
    n_el = 1
    for s in shape[1:]:
        n_el *= s
    a = arena_ap.bitcast(dt)
    start = off_bytes // isz
    v = a[:, start:start + n_el]
    if len(shape) == 3:
        v = v.rearrange("p (a b) -> p a b", b=shape[2])
    elif len(shape) == 4:
        v = v.rearrange("p (a b c) -> p a b c", b=shape[2], c=shape[3])
    end = off_bytes + n_el * isz
    return v[0:shape[0]], (end + 31) // 32 * 32


def build_program():
    nc = bass.Bass("TRN2", target_bir_lowering=False, debug=False)
    dt_ = nc.dram_tensor

    xT = dt_("xT", [DM, SEQ], F32, kind="ExternalInput").ap()
    wqT = dt_("wqT", [DM, 512], F32, kind="ExternalInput").ap()
    wkT = dt_("wkT", [DM, 512], F32, kind="ExternalInput").ap()
    wvT = dt_("wvT", [DM, 1024], F32, kind="ExternalInput").ap()
    wgaT = dt_("wgaT", [128 * 16 * 16, 128], F32, kind="ExternalInput").ap()
    woT = dt_("woT", [128 * 16 * 16, 128], F32, kind="ExternalInput").ap()
    gnormB = dt_("gnormB", [128, DVH], F32, kind="ExternalInput").ap()
    ropeC = dt_("ropeC", [128, SEQ], F32, kind="ExternalInput").ap()
    ropeS = dt_("ropeS", [128, SEQ], F32, kind="ExternalInput").ap()
    dmaskT = dt_("dmaskT", [128, 2 * CH], F32, kind="ExternalInput").ap()
    ginb = dt_("ginb", [128, 2 * CH], F32, kind="ExternalInput").ap()
    gkb = dt_("gkb", [128, 2], F32, kind="ExternalInput").ap()
    gCc = dt_("gCc", [128, 2], F32, kind="ExternalInput").ap()
    identM = dt_("identM", [128, 128], F32, kind="ExternalInput").ap()
    shifts = dt_("shifts", [128, 1], I32, kind="ExternalInput").ap()
    cpool = dt_("cpool", [128, 8], F32, kind="ExternalInput").ap()
    onescol = dt_("onescol", [128, 1], F32, kind="ExternalInput").ap()
    onesrow = dt_("onesrow", [1, 128], F32, kind="ExternalInput").ap()
    e0row = dt_("e0row", [128, 128], F32, kind="ExternalInput").ap()
    gpackR = dt_("gpackR", [MSL, DM], I32, kind="ExternalInput").ap()
    gscalR = dt_("gscalR", [MSL, DM], F32, kind="ExternalInput").ap()
    upackR = dt_("upackR", [MSL, DM], I32, kind="ExternalInput").ap()
    uscalR = dt_("uscalR", [MSL, DM], F32, kind="ExternalInput").ap()
    dpackR = dt_("dpackR", [MSL, DM], I32, kind="ExternalInput").ap()
    dscalR = dt_("dscalR", [MSL, DM], F32, kind="ExternalInput").ap()

    out = dt_("out", [DM, 1024], F32, kind="ExternalOutput").ap()
    dbg_x1 = dt_("dbg_x1", [DM, SEQ], F32, kind="ExternalOutput").ap() \
        if DEBUG_X1 else None
    dbg_ont = dt_("dbg_ont", [1024, SEQ], F32, kind="ExternalOutput").ap() \
        if DEBUG_ONT else None
    dbg_S = dt_("dbg_S", [128, 2048], F32, kind="ExternalOutput").ap() \
        if DEBUG_S else None
    dbg_Sb = dt_("dbg_Sb", [128, 1024], F32, kind="ExternalOutput").ap() \
        if DEBUG_S else None
    dbg_osc = dt_("dbg_osc", [128, 512], F32, kind="ExternalOutput").ap() \
        if DEBUG_S else None
    dbg_ofu = dt_("dbg_ofu", [2048, 1024], F32, kind="ExternalOutput").ap() \
        if DEBUG_B else None
    dbg_g = dt_("dbg_g", [2048, 512], F32, kind="ExternalOutput").ap() \
        if DEBUG_B else None
    dbg_ao = dt_("dbg_ao", [2048, 512], F32, kind="ExternalOutput").ap() \
        if DEBUG_B else None
    dbg_qt = dt_("dbg_qt", [128, 512], F32, kind="ExternalOutput").ap() \
        if DEBUG_S else None

    exin0 = dt_("exin0", [1024, 1024], BF16).ap()
    exin1 = dt_("exin1", [1024, 1024], BF16).ap()
    ex0 = dt_("ex0", [2048, 1024], BF16).ap()
    ex1 = dt_("ex1", [2048, 1024], BF16).ap()
    gins = [dt_("gin%d" % q, [DM, 512], BF16).ap() for q in range(4)]
    X1Gs = [dt_("X1G%d" % q, [4 * DM, 512], BF16).ap() for q in range(4)]
    WSg = dt_("WSg", [DM, MSL], BF16).ap()
    WSu = dt_("WSu", [DM, MSL], BF16).ap()
    WSd = dt_("WSd", [MSL, DM], BF16).ap()
    GT = dt_("GT", [16 * MSL, 512], BF16).ap()
    IT = dt_("IT", [16 * MSL, 512], BF16).ap()
    PD0 = dt_("PD0", [8 * DM, 512], BF16).ap()
    PD1 = dt_("PD1", [8 * DM, 512], BF16).ap()
    RSo0 = dt_("RSo0", [DM, 512], BF16).ap()
    RSo1 = dt_("RSo1", [DM, 512], BF16).ap()
    XB = dt_("XB", [DM, SEQ], BF16).ap()
    WGA2 = dt_("WGA2", [128 * 16 * 16, 128], BF16).ap()
    WO2 = dt_("WO2", [128 * 16 * 16, 128], BF16).ap()

    es = contextlib.ExitStack()
    with es:
        E = es.enter_context
        AR_t = E(nc.sbuf_tensor("ARENA", [128, 73728], BF16))
        AR = AR_t[:]
        s_dmask = E(nc.sbuf_tensor("s_dmask", [128, 2 * CH], F32))
        s_ginb = E(nc.sbuf_tensor("s_ginb", [128, 2 * CH], F32))
        s_gkb = E(nc.sbuf_tensor("s_gkb", [128, 2], F32))
        s_gCc = E(nc.sbuf_tensor("s_gCc", [128, 2], F32))
        s_gnorm = E(nc.sbuf_tensor("s_gnorm", [128, DVH], F32))
        s_ident = E(nc.sbuf_tensor("s_ident", [128, 128], BF16))
        s_shift = E(nc.sbuf_tensor("s_shift", [128, 1], I32))
        s_cpool = E(nc.sbuf_tensor("s_cpool", [128, 8], F32))
        s_onesc = E(nc.sbuf_tensor("s_onesc", [128, 1], BF16))
        s_onesr = E(nc.sbuf_tensor("s_onesr", [1, 128], F32))
        s_E0 = E(nc.sbuf_tensor("s_E0", [128, 128], F32))
        SC = E(nc.sbuf_tensor("SC", [128, 10752], BF16))
        DQ = E(nc.sbuf_tensor("DQ", [128, 16384], BF16))
        p_a = E(nc.psum_tensor("p_a", [128, 512], F32))
        p_b = E(nc.psum_tensor("p_b", [128, 512], F32))
        p_c = E(nc.psum_tensor("p_c", [128, 512], F32))
        p_d = E(nc.psum_tensor("p_d", [128, 512], F32))
        p_s0 = E(nc.psum_tensor("p_s0", [128, 512], F32))
        p_s1 = E(nc.psum_tensor("p_s1", [128, 512], F32))
        p_t = E(nc.psum_tensor("p_t", [128, 1024], BF16))

        bld = Bld(nc)
        for s in ("LD0", "PE", "VE", "SE", "GP", "CC", "XP", "XG",
                  "WP0", "WP1", "XR0", "XR1", "OFU", "GIN", "XF0", "XF1",
                  "BA", "BB", "W0S", "S2R", "DQP", "DQS", "WSG", "WSU",
                  "WSD", "GTS", "ITS", "PDS0", "PDS1", "EXN", "OUTS", "DBG"):
            bld.add_sem(s, E(nc.semaphore(s)))

        eps_ap = s_cpool[:, 0:1]
        zero_ap = s_cpool[:, 1:2]

        MB = 1024 * 1024
        o = 0
        wq_sb, o = _carve(AR, o, BF16, [128, 16, 512])
        wk_sb, o = _carve(AR, o, BF16, [128, 16, 512])
        wv_sb, o = _carve(AR, o, BF16, [128, 16, 1024])
        xp_sb, o = _carve(AR, o, BF16, [128, 16, 512])
        q_sb, o = _carve(AR, o, BF16, [128, 4, 512])
        k_sb, o = _carve(AR, o, BF16, [128, 4, 512])
        v_sb, o = _carve(AR, o, BF16, [128, 4, 2, 512])
        S_f, o = _carve(AR, o, F32, [128, 2, 1024])
        S_b, o = _carve(AR, o, BF16, [128, 2, 1024])
        a_end = o
        ONT_OFF = (73728 * 2) - (4 * MB // 128)
        ont_sb, _ = _carve(AR, ONT_OFF, BF16, [128, 8, 2048])
        assert a_end <= ONT_OFF, (a_end, ONT_OFF)
        o = 0
        ofu_sb, o = _carve(AR, o, BF16, [128, 16, 1024])
        g_sb, o = _carve(AR, o, BF16, [128, 16, 512])
        sres, o = _carve(AR, o, F32, [128, 16, 512])
        sbf, o = _carve(AR, o, BF16, [128, 16, 512])
        wpan0, o = _carve(AR, o, BF16, [128, 16, 128])
        wpan1, o = _carve(AR, o, BF16, [128, 16, 128])
        xres0, o = _carve(AR, o, F32, [128, 512])
        xres1, o = _carve(AR, o, F32, [128, 512])
        assert o <= ONT_OFF
        xbp, _ = _carve(AR, ONT_OFF, BF16, [128, 16, 512])
        o = 0
        W0, o = _carve(AR, o, BF16, [128, 16, 2048])
        xf0, o = _carve(AR, o, BF16, [128, 16, 512])
        xf1, o = _carve(AR, o, BF16, [128, 16, 512])
        bufA, o = _carve(AR, o, BF16, [128, 16, 512])
        bufB, o = _carve(AR, o, BF16, [128, 16, 512])
        bufC, o = _carve(AR, o, BF16, [128, 16, 512])
        assert o <= 73728 * 2
        o = 0
        s2_sb, o = _carve(AR, o, BF16, [128, 16, 512])
        sq2_sb, o = _carve(AR, o, BF16, [128, 16, 512])
        outp, o = _carve(AR, o, F32, [128, 16, 512])

        o = 0
        o_scr, o = _carve(SC[:], o, F32, [128, 512])
        sq_scr, o = _carve(SC[:], o, F32, [128, 512])
        attn_sb, o = _carve(SC[:], o, BF16, [128, 128])
        ktil, o = _carve(SC[:], o, BF16, [128, 256])
        qtil, o = _carve(SC[:], o, BF16, [128, 256])
        on_sb, o = _carve(SC[:], o, BF16, [128, 512])
        ms_scr, o = _carve(SC[:], o, F32, [128, 1])
        rr_scr, o = _carve(SC[:], o, F32, [128, 1])
        statm, o = _carve(SC[:], o, F32, [128, 512])
        statr, o = _carve(SC[:], o, F32, [128, 512])
        stat_q, o = _carve(SC[:], o, F32, [1, 512])
        stat_t, o = _carve(SC[:], o, F32, [1, 512])
        mb_b, o = _carve(SC[:], o, F32, [128, 512])
        rb_b, o = _carve(SC[:], o, F32, [128, 512])
        assert o <= 10752 * 2

        o = 0
        s_ropeC, o = _carve(DQ[:], o, BF16, [128, SEQ])
        s_ropeS, o = _carve(DQ[:], o, BF16, [128, SEQ])
        t_rope, o = _carve(DQ[:], o, F32, [128, 512])
        u_rope, o = _carve(DQ[:], o, F32, [128, 512])
        o = 0
        ge_a0, o = _carve(DQ[:], o, F32, [128, 512])
        ge_a1, o = _carve(DQ[:], o, F32, [128, 512])
        ge_b0, o = _carve(DQ[:], o, F32, [128, 512])
        ge_b1, o = _carve(DQ[:], o, F32, [128, 512])
        dq_pack, o = _carve(DQ[:], o, I32, [128, 1024])
        dq_scal, o = _carve(DQ[:], o, F32, [128, 1024])
        dq_idx, o = _carve(DQ[:], o, I32, [128, 512])
        dq_c, o = _carve(DQ[:], o, I32, [128, 512])
        dq_f, o = _carve(DQ[:], o, F32, [128, 512])
        dq_r1, o = _carve(DQ[:], o, F32, [128, 512])
        dq_r2, o = _carve(DQ[:], o, F32, [128, 512])
        dq_s2, o = _carve(DQ[:], o, F32, [128, 512])
        dq_w, o = _carve(DQ[:], o, BF16, [128, 2048])
        assert o <= 16384 * 2, o

        banks = [p_a, p_b, p_c, p_d, p_s0, p_s1]
        bank_guard = [None] * 6
        bank_i = [0]
        bank_mod = [4]

        def next_bank():
            i = bank_i[0] % bank_mod[0]
            bank_i[0] += 1
            if bank_guard[i] is not None:
                bld.wait("tensor", *bank_guard[i])
            return i

        def mm_accum(bi, tiles, n_slice=None):
            pa = banks[bi][:] if n_slice is None else banks[bi][:, 0:n_slice]
            nT = len(tiles)
            v = None
            for i, (lt, rh) in enumerate(tiles):
                st = (i == 0)
                if i == nT - 1:
                    v = bld.op("tensor",
                               lambda e, lt=lt, rh=rh, st=st, pa=pa:
                               e.matmul(pa, lt, rh, start=st, stop=True),
                               inc="PE")
                else:
                    bld.op("tensor",
                           lambda e, lt=lt, rh=rh, st=st, pa=pa:
                           e.matmul(pa, lt, rh, start=st, stop=False))
            return v

        def evac(bi, eng, fn, waits=()):
            sem = {"vector": "VE", "scalar": "SE"}[eng]
            v = bld.op(eng, fn, inc=sem, waits=waits)
            bank_guard[bi] = (sem, v)
            return v

        dq_state = {"guard": []}

        def dequant_block(packR, scalR, WS, pb, st_sem):
            last = None
            for hf in range(2):
                w_ = list(dq_state["guard"])
                hsl = slice(hf * 1024, (hf + 1) * 1024)
                vld = bld.dma("scalar", dq_pack,
                              packR.rearrange("(a p) n -> p a n", p=128)
                              [:, pb, hsl], "DQP", waits=w_)
                vld2 = bld.dma("scalar", dq_scal,
                               scalR.rearrange("(a p) n -> p a n", p=128)
                               [:, pb, hsl], "DQS")
                last = _dq_half(vld, vld2, hf)
                dq_state["guard"] = [("VE", last)]
            vst = bld.dma("scalar",
                          WS.rearrange("(a p) n -> p a n", p=128)[:, pb, :],
                          dq_w, st_sem, waits=[("VE", last)])
            dq_state["guard"] = [("VE", last), (st_sem, vst)]
            return vst

        def _dq_half(vld, vld2, hf):
            last = None
            for i in range(2):
                sl = slice(i * 512, (i + 1) * 512)
                pk = dq_pack[:, sl]
                scl = dq_scal[:, sl]
                wo = dq_w[:, hf * 1024 + i * 512:hf * 1024 + (i + 1) * 512]
                bld.op("vector", lambda e, pk=pk: e.tensor_scalar(
                    dq_c, pk, s_shift[:], 7, ALU.logical_shift_right,
                    ALU.bitwise_and),
                    waits=[("DQP", vld), ("DQS", vld2)])
                bld.op("vector", lambda e, pk=pk: e.tensor_scalar(
                    dq_idx, pk, s_shift[:], 8, ALU.logical_shift_right,
                    ALU.bitwise_and))
                bld.op("vector", lambda e: e.tensor_scalar(
                    dq_f, dq_idx, -0.125, 0.5, ALU.mult, ALU.add))
                bld.op("vector", lambda e: e.tensor_scalar(
                    dq_r1, dq_c, -4.0, 0.0, ALU.add, ALU.max))
                bld.op("vector", lambda e: e.tensor_scalar(
                    dq_r2, dq_c, -6.0, 0.0, ALU.add, ALU.max))
                bld.op("vector", lambda e: e.scalar_tensor_tensor(
                    dq_r2, dq_r2, 2.0, dq_r1, ALU.mult, ALU.add))
                bld.op("vector", lambda e: e.tensor_tensor(
                    dq_s2, dq_r2, dq_c, ALU.add))
                bld.op("vector", lambda e: e.tensor_tensor(
                    dq_s2, dq_s2, dq_f, ALU.mult))
                last = bld.op("vector", lambda e, scl=scl, wo=wo: e.tensor_tensor(
                    wo, dq_s2, scl, ALU.mult), inc="VE")
            return last

        def layer_norm(src_f32, srcbf, dst_bf, dst_f32, sqbuf, ready):
            sum_ap = p_s0[0:1, :]
            sq_ap = p_s1[0:1, :]
            for bidx in (4, 5):
                if bank_guard[bidx] is not None:
                    bld.wait("tensor", *bank_guard[bidx])
                    bank_guard[bidx] = None
            for kb in range(16):
                st, sp = (kb == 0), (kb == 15)
                if sp:
                    vsum = bld.op("tensor", lambda e, kb=kb, st=st: e.matmul(
                        sum_ap, s_onesc[:], srcbf[:, kb, :], start=st, stop=True),
                        inc="PE", waits=list(ready))
                else:
                    bld.op("tensor", lambda e, kb=kb, st=st: e.matmul(
                        sum_ap, s_onesc[:], srcbf[:, kb, :], start=st, stop=False),
                        waits=list(ready))
            vsq_in = bld.op("vector", lambda e: e.tensor_tensor(
                sqbuf[:], srcbf[:], srcbf[:], ALU.mult), inc="VE",
                waits=list(ready))
            for kb in range(16):
                st, sp = (kb == 0), (kb == 15)
                if sp:
                    vsq = bld.op("tensor", lambda e, kb=kb, st=st: e.matmul(
                        sq_ap, s_onesc[:], sqbuf[:, kb, :], start=st, stop=True),
                        inc="PE", waits=[("VE", vsq_in)])
                else:
                    bld.op("tensor", lambda e, kb=kb, st=st: e.matmul(
                        sq_ap, s_onesc[:], sqbuf[:, kb, :], start=st, stop=False),
                        waits=[("VE", vsq_in)])
            vme = bld.op("vector", lambda e: e.tensor_scalar(
                statm[0:1, :], sum_ap, 1.0 / 2048, None, ALU.mult),
                inc="VE", waits=[("PE", vsum)])
            bld.op("vector", lambda e: e.tensor_scalar(
                stat_q, sq_ap, 1.0 / 2048, None, ALU.mult),
                waits=[("PE", vsq)])
            bld.op("vector", lambda e: e.tensor_tensor(
                stat_t, statm[0:1, :], statm[0:1, :], ALU.mult))
            vvar = bld.op("vector", lambda e: e.tensor_tensor(
                stat_q, stat_q, stat_t, ALU.subtract), inc="VE")
            bld.op("scalar", lambda e: e.activation(
                stat_q, stat_q, AF.Ln, bias=eps_ap[0:1, :], scale=1.0),
                waits=[("VE", vvar)])
            vrs = bld.op("scalar", lambda e: e.activation(
                statr[0:1, :], stat_q, AF.Exp, bias=zero_ap[0:1, :],
                scale=-0.5), inc="SE")
            bi = next_bank()
            vbc1 = bld.op("tensor", lambda e, bi=bi: e.matmul(
                banks[bi][:], s_E0[:], statm, start=True, stop=True),
                inc="PE", waits=[("SE", vrs), ("VE", vme)])
            bj = next_bank()
            vbc2 = bld.op("tensor", lambda e, bj=bj: e.matmul(
                banks[bj][:], s_E0[:], statr, start=True, stop=True),
                inc="PE")
            evac(bi, "vector", lambda e, bi=bi: e.tensor_copy(
                mb_b, banks[bi][:]), waits=[("PE", vbc1)])
            evac(bj, "vector", lambda e, bj=bj: e.tensor_copy(
                rb_b, banks[bj][:]), waits=[("PE", vbc2)])
            last = None
            src = src_f32 if src_f32 is not None else srcbf
            dst = dst_bf if dst_bf is not None else dst_f32
            for kb in range(16):
                bld.op("vector", lambda e, kb=kb: e.tensor_tensor(
                    o_scr, src[:, kb, :], mb_b, ALU.subtract))
                last = bld.op("vector", lambda e, kb=kb: e.tensor_tensor(
                    dst[:, kb, :], o_scr, rb_b, ALU.mult), inc="VE")
            return last

        bld.dma("sync", s_dmask[:], dmaskT, "LD0")
        bld.dma("sync", s_ginb[:], ginb, "LD0")
        bld.dma("sync", s_gkb[:], gkb, "LD0")
        bld.dma("sync", s_gCc[:], gCc, "LD0")
        bld.dma("sync", s_gnorm[:], gnormB, "LD0")
        bld.dma("sync", s_shift[:], shifts, "LD0")
        bld.dma("sync", s_cpool[:], cpool, "LD0")
        bld.dma("sync", s_onesr[:], onesrow, "LD0")
        bld.dma("sync", s_E0[:], e0row, "LD0")
        bld.dma("gpsimd", s_ropeC[:], ropeC, "LD0")
        bld.dma("gpsimd", s_ropeS[:], ropeS, "LD0")
        bld.dma("gpsimd", s_ident[:], identM, "LD0")
        bld.dma("gpsimd", s_onesc[:], onescol, "LD0")
        v_xb = bld.dma("gpsimd", XB, xT, "LD0")
        v_wga2 = bld.dma("gpsimd", WGA2, wgaT, "LD0")
        v_wo2 = bld.dma("gpsimd", WO2, woT, "LD0")
        v_wq = bld.dma("gpsimd", wq_sb, wqT.rearrange("(a p) n -> p a n", p=128), "LD0")
        v_wk = bld.dma("gpsimd", wk_sb, wkT.rearrange("(a p) n -> p a n", p=128), "LD0")
        v_wv = bld.dma("gpsimd", wv_sb, wvT.rearrange("(a p) n -> p a n", p=128), "LD0")
        v_consts = bld.val("LD0")
        for e in ENGS:
            bld.wait(e, "LD0", v_consts)

        bld.op("vector", lambda e: e.memset(statm, 0.0))
        bld.op("vector", lambda e: e.memset(statr, 0.0))
        bld.op("vector", lambda e: e.memset(S_f, 0.0))
        vS1 = bld.op("vector", lambda e: e.memset(S_b, 0.0), inc="VE")

        sbf_ready = {0: ("VE", vS1), 1: ("VE", vS1)}
        xp_guard = None
        pt_guard = None
        ps_guard = None
        g_attn = {}
        dbg_guard = {}

        for p in range(4):
            cols = slice(p * 512, (p + 1) * 512)
            w_ = [xp_guard] if xp_guard else []
            vxp = bld.dma("sync", xp_sb,
                          XB.rearrange("(a p) t -> p a t", p=128)[:, :, cols],
                          "XP", waits=w_)
            for (w_sb, dst, vw) in ((wq_sb, q_sb, v_wq), (wk_sb, k_sb, v_wk)):
                for h in range(2):
                    bi = next_bank()
                    bj = next_bank()
                    vmm = None
                    for (b_, mb) in ((bi, 0), (bj, 1)):
                        m = h * 2 + mb
                        tiles = [(w_sb[:, kb, m * 128:(m + 1) * 128],
                                  xp_sb[:, kb, :]) for kb in range(16)]
                        bld.wait("tensor", "XP", vxp)
                        vmm = mm_accum(b_, tiles)
                    p0 = banks[bi][:]
                    p1 = banks[bj][:]
                    cC = s_ropeC[:, cols]
                    cS = s_ropeS[:, cols]
                    d0 = dst[:, h * 2 + 0, :]
                    d1 = dst[:, h * 2 + 1, :]
                    bld.wait("vector", "PE", vmm)
                    bld.op("vector", lambda e, p0=p0, cC=cC: e.tensor_tensor(
                        t_rope, p0, cC, ALU.mult))
                    bld.op("vector", lambda e, p1=p1, cS=cS: e.tensor_tensor(
                        u_rope, p1, cS, ALU.mult))
                    bld.op("vector", lambda e, d0=d0: e.tensor_tensor(
                        d0, t_rope, u_rope, ALU.subtract))
                    bld.op("vector", lambda e, p1=p1, cC=cC: e.tensor_tensor(
                        t_rope, p1, cC, ALU.mult))
                    bld.op("vector", lambda e, p0=p0, cS=cS: e.tensor_tensor(
                        u_rope, p0, cS, ALU.mult))
                    vev = bld.op("vector", lambda e, d1=d1: e.tensor_tensor(
                        d1, t_rope, u_rope, ALU.add), inc="VE")
                    bank_guard[bi] = ("VE", vev)
                    bank_guard[bj] = ("VE", vev)
            for tmb in range(4):
                for h in range(2):
                    bi = next_bank()
                    tiles = [(xp_sb[:, kb, tmb * 128:(tmb + 1) * 128],
                              wv_sb[:, kb, h * 512:(h + 1) * 512])
                             for kb in range(16)]
                    bld.wait("tensor", "XP", vxp)
                    vmm = mm_accum(bi, tiles)
                    evac(bi, "vector",
                         lambda e, bi=bi, tmb=tmb, h=h: e.tensor_copy(
                             v_sb[:, tmb, h, :], banks[bi][:]),
                         waits=[("PE", vmm)])
            vqk_done = bld.val("VE")
            for cc in range(4):
                ch_cols = slice(cc * 128, (cc + 1) * 128)
                gch = 4 * p + cc
                for h in range(2):
                    bi = next_bank()
                    tiles = [(k_sb[:, h * 2 + mb, ch_cols],
                              q_sb[:, h * 2 + mb, ch_cols]) for mb in range(2)]
                    bld.wait("tensor", "VE", vqk_done)
                    vat = mm_accum(bi, tiles, n_slice=128)
                    vev = evac(bi, "vector",
                               lambda e, bi=bi, h=h: e.tensor_tensor(
                                   attn_sb, banks[bi][:, 0:128],
                                   s_dmask[:, h * CH:(h + 1) * CH], ALU.mult),
                               waits=[("PE", vat), g_attn.get("attn")])
                    if pt_guard is not None:
                        bld.wait("tensor", *pt_guard)
                    vtr = None
                    for mb in range(2):
                        vtr = bld.op("tensor",
                                     lambda e, h=h, mb=mb, ch_cols=ch_cols:
                                     e.transpose(
                                         p_t[:, mb * 128:(mb + 1) * 128],
                                         k_sb[:, h * 2 + mb, ch_cols],
                                         s_ident[:]), inc="PE")
                    vkt = bld.op("vector", lambda e, h=h: e.tensor_scalar(
                        ktil, p_t[:, 0:256], s_gkb[:, h:h + 1], None,
                        ALU.mult), inc="VE",
                        waits=[("PE", vtr), g_attn.get("ktil")])
                    vqt = None
                    for mb in range(2):
                        vqt = bld.op("vector",
                                     lambda e, h=h, mb=mb, ch_cols=ch_cols:
                                     e.tensor_tensor(
                                         qtil[:, mb * 128:(mb + 1) * 128],
                                         q_sb[:, h * 2 + mb, ch_cols],
                                         s_ginb[:, h * CH:(h + 1) * CH],
                                         ALU.mult), inc="VE",
                                     waits=[g_attn.get("qtil")])
                    bo = next_bank()
                    bld.wait("tensor", "VE", vev)
                    if DEBUG_NO_INTER:
                        vo = bld.op("tensor", lambda e, bo=bo, cc=cc, h=h:
                                    e.matmul(banks[bo][:], attn_sb,
                                             v_sb[:, cc, h, :],
                                             start=True, stop=True), inc="PE")
                        bld.wait("tensor", "VE", vqt)
                        bld.wait("tensor", *sbf_ready[h])
                    else:
                        bld.op("tensor", lambda e, bo=bo, cc=cc, h=h: e.matmul(
                            banks[bo][:], attn_sb, v_sb[:, cc, h, :],
                            start=True, stop=False))
                        bld.wait("tensor", "VE", vqt)
                        bld.wait("tensor", *sbf_ready[h])
                        bld.op("tensor", lambda e, bo=bo, h=h: e.matmul(
                            banks[bo][:], qtil[:, 0:128], S_b[:, h, 0:512],
                            start=False, stop=False))
                        vo = bld.op("tensor", lambda e, bo=bo, h=h: e.matmul(
                            banks[bo][:], qtil[:, 128:256],
                            S_b[:, h, 512:1024],
                            start=False, stop=True), inc="PE")
                    g_attn["attn"] = ("PE", vo)
                    g_attn["qtil"] = ("PE", vo)
                    bld.wait("tensor", "VE", vkt)
                    if ps_guard is not None:
                        bld.wait("tensor", *ps_guard)
                    vsu = None
                    for mb, psx in ((0, p_s0), (1, p_s1)):
                        vsu = bld.op("tensor",
                                     lambda e, mb=mb, cc=cc, h=h, psx=psx:
                                     e.matmul(
                                         psx[:],
                                         ktil[:, mb * 128:(mb + 1) * 128],
                                         v_sb[:, cc, h, :], start=True,
                                         stop=True), inc="PE")
                    g_attn["ktil"] = ("PE", vsu)
                    bld.op("vector", lambda e, h=h: e.scalar_tensor_tensor(
                        S_f[:, h, 0:512], S_f[:, h, 0:512], s_gCc[:, h:h + 1],
                        p_s0[:], ALU.mult, ALU.add),
                        waits=[("PE", vsu), dbg_guard.get("vg")])
                    va = bld.op("vector", lambda e, h=h: e.scalar_tensor_tensor(
                        S_f[:, h, 512:1024], S_f[:, h, 512:1024],
                        s_gCc[:, h:h + 1], p_s1[:], ALU.mult, ALU.add),
                        inc="VE")
                    ps_guard = ("VE", va)
                    if not DEBUG_FREEZE_SB:
                        vsb = bld.op("vector", lambda e, h=h: e.tensor_copy(
                            S_b[:, h, :], S_f[:, h, :]), inc="VE")
                        sbf_ready[h] = ("VE", vsb)
                    bld.wait("vector", "PE", vo)
                    bld.op("vector", lambda e, bo=bo: e.tensor_copy(
                        o_scr, banks[bo][:]))
                    bld.op("vector", lambda e: e.tensor_tensor(
                        sq_scr, o_scr, o_scr, ALU.mult))
                    voc = bld.op("vector", lambda e: e.tensor_reduce(
                        ms_scr, sq_scr, mybir.AxisListType.X, ALU.add),
                        inc="VE")
                    bank_guard[bo] = ("VE", voc)
                    bld.op("scalar", lambda e: e.activation(
                        rr_scr, ms_scr, AF.Ln, bias=eps_ap, scale=1.0 / DVH),
                        waits=[("VE", voc)])
                    vrr = bld.op("scalar", lambda e: e.activation(
                        rr_scr, rr_scr, AF.Exp, bias=zero_ap, scale=-0.5),
                        inc="SE")
                    von = bld.op("vector", lambda e: e.scalar_tensor_tensor(
                        on_sb, o_scr, rr_scr, s_gnorm[:], ALU.mult, ALU.mult),
                        inc="VE",
                        waits=[("SE", vrr), g_attn.get("on")])
                    bld.wait("tensor", "VE", vkt)
                    vto = None
                    for db in range(4):
                        vto = bld.op("tensor", lambda e, db=db: e.transpose(
                            p_t[:, 256 + db * 128:256 + (db + 1) * 128],
                            on_sb[:, db * 128:(db + 1) * 128], s_ident[:]),
                            inc="PE", waits=[("VE", von)])
                    g_attn["on"] = ("PE", vto)
                    dst = ont_sb[:, h * 4:(h + 1) * 4,
                                 gch * 128:(gch + 1) * 128]
                    src = p_t[:, 256:768].rearrange("p (a b) -> p a b", b=128)
                    vont = bld.op("vector",
                                  lambda e, dst=dst, src=src: e.tensor_copy(
                                      dst, src), inc="VE",
                                  waits=[("PE", vto)])
                    pt_guard = ("VE", vont)
                    if DEBUG_S and p == 0 and cc == 0 and h == 1:
                        vdS = bld.dma("gpsimd", dbg_S, S_f.rearrange(
                            "p a b -> p (a b)"), "DBG",
                            waits=[("SE", bld.val("SE"))])
                        dbg_guard["vg"] = ("DBG", vdS)
                    if DEBUG_S and p == 0 and cc == 1 and h == 0:
                        v1_ = bld.dma("gpsimd", dbg_qt[:, 0:256], qtil,
                                      "DBG", waits=[("VE", vqt)])
                        v2_ = bld.dma("gpsimd", dbg_qt[:, 256:512], ktil,
                                      "DBG", waits=[("VE", vkt)])
                        v3_ = bld.dma("gpsimd", dbg_Sb, S_b[:, 0, :], "DBG")
                        v4_ = bld.dma("gpsimd", dbg_osc, o_scr, "DBG")
                        g_attn["qtil"] = ("DBG", v4_)
                        g_attn["ktil"] = ("DBG", v4_)
                        g_attn["on"] = ("DBG", v4_)
                        sbf_ready[0] = ("DBG", v4_)
                        dbg_guard["vg"] = ("DBG", v4_)
            xp_guard = ("PE", bld.val("PE"))

        v_ont_done = bld.val("VE")
        if DEBUG_ONT:
            bld.dma("gpsimd", dbg_ont.rearrange("(a p) t -> p a t", p=128),
                    ont_sb, "DBG", waits=[("VE", v_ont_done)])
        vex0 = bld.dma("gpsimd", exin0.rearrange("(a p) t -> p a t", p=128),
                       ont_sb[:, :, 0:1024], "EXN",
                       waits=[("VE", v_ont_done)])
        vex1 = bld.dma("gpsimd", exin1.rearrange("(a p) t -> p a t", p=128),
                       ont_sb[:, :, 1024:2048], "EXN")
        pair_groups = [[0, 1], [2, 3], [4, 5], [6, 7]]
        bld.op("gpsimd", lambda e: e.collective_compute(
            "AllGather", ALU.bypass, replica_groups=pair_groups,
            ins=[exin0], outs=[ex0]), inc="CC",
            waits=[("EXN", vex1)])
        bld.op("gpsimd", lambda e: e.collective_compute(
            "AllGather", ALU.bypass, replica_groups=pair_groups,
            ins=[exin1], outs=[ex1]), inc="CC")

        a_pe = bld.val("PE")
        bld.barrier(["sync", "gpsimd"], ["PE", "VE", "SE", "GP"])
        xg_guard = None
        wp_guard = [None, None]
        xr_guard = [None, None]
        og_prev = None
        gin_prev = None
        ofu_pe_guard = a_pe
        for half in range(2):
            exbuf = ex0 if half == 0 else ex1
            vof = bld.dma("sync", ofu_sb,
                          exbuf.rearrange("(a p) t -> p a t", p=128),
                          "OFU",
                          waits=[("CC", half + 1), ("PE", ofu_pe_guard)])
            for grp in range(2):
                gcols = slice(half * 1024 + grp * 512,
                              half * 1024 + grp * 512 + 512)
                ocols = slice(grp * 512, (grp + 1) * 512)
                w_ = list(xg_guard) if xg_guard else []
                w_.append(("EXN", 32))
                vxg = bld.dma("sync", xbp,
                              XB.rearrange("(a p) t -> p a t", p=128)
                              [:, :, gcols], "XG", waits=w_)
                vsilu = None
                for mb in range(16):
                    wp = (wpan0, wpan1)[mb % 2]
                    w_ = [wp_guard[mb % 2]] if wp_guard[mb % 2] else []
                    vwp = bld.dma("sync", wp,
                                  WGA2.rearrange("(p m a) n -> p m a n",
                                                 p=128, m=16)[:, mb, :, :],
                                  "WP%d" % (mb % 2), waits=w_)
                    bi = next_bank()
                    tiles = [(wp[:, kb, :], xbp[:, kb, :]) for kb in range(16)]
                    bld.wait("tensor", "WP%d" % (mb % 2), vwp)
                    bld.wait("tensor", "XG", vxg)
                    vmm = mm_accum(bi, tiles)
                    wp_guard[mb % 2] = ("PE", vmm)
                    vsilu = evac(bi, "scalar",
                                 lambda e, bi=bi, mb=mb: e.activation(
                                     g_sb[:, mb, :], banks[bi][:], AF.Silu,
                                     bias=0.0, scale=1.0),
                                 waits=[("PE", vmm), og_prev])
                if DEBUG_B and half == 0 and grp == 0:
                    vd_ = bld.dma("gpsimd",
                                  dbg_ofu.rearrange("(a p) t -> p a t", p=128),
                                  ofu_sb, "DBG", waits=[("OFU", vof)])
                    vd2_ = bld.dma("gpsimd",
                                   dbg_g.rearrange("(a p) t -> p a t", p=128),
                                   g_sb, "DBG", waits=[("SE", vsilu)])
                    dbg_guard["og"] = ("DBG", vd2_)
                xg_guard = [("PE", bld.val("PE"))]
                vog = bld.op("vector", lambda e, ocols=ocols: e.tensor_tensor(
                    ofu_sb[:, :, ocols], ofu_sb[:, :, ocols], g_sb[:],
                    ALU.mult), inc="VE",
                    waits=[("SE", vsilu), ("OFU", vof),
                           dbg_guard.get("og")])
                og_prev = ("VE", vog)
                vsh = None
                for mb in range(16):
                    wp = (wpan0, wpan1)[mb % 2]
                    w_ = [wp_guard[mb % 2]] if wp_guard[mb % 2] else []
                    vwp = bld.dma("sync", wp,
                                  WO2.rearrange("(p m a) n -> p m a n",
                                                 p=128, m=16)[:, mb, :, :],
                                  "WP%d" % (mb % 2), waits=w_)
                    xr = (xres0, xres1)[mb % 2]
                    w_ = [xr_guard[mb % 2]] if xr_guard[mb % 2] else []
                    vxr = bld.dma("sync", xr,
                                  xT.rearrange("(a p) t -> p a t", p=128)
                                  [:, mb, gcols], "XR%d" % (mb % 2), waits=w_)
                    bi = next_bank()
                    tiles = [(wp[:, kb, :], ofu_sb[:, kb, ocols])
                             for kb in range(16)]
                    bld.wait("tensor", "WP%d" % (mb % 2), vwp)
                    bld.wait("tensor", "VE", vog)
                    vmm = mm_accum(bi, tiles)
                    wp_guard[mb % 2] = ("PE", vmm)
                    vadd = bld.op("vector",
                                  lambda e, bi=bi, mb=mb, xr=xr:
                                  e.tensor_tensor(sres[:, mb, :],
                                                  banks[bi][:], xr, ALU.add),
                                  inc="VE",
                                  waits=[("PE", vmm),
                                         ("XR%d" % (mb % 2), vxr),
                                         gin_prev])
                    xr_guard[mb % 2] = ("VE", vadd)
                    vsh = evac(bi, "vector", lambda e, mb=mb: e.tensor_copy(
                        sbf[:, mb, :], sres[:, mb, :]))
                if DEBUG_B and half == 0 and grp == 0:
                    vd3_ = bld.dma("gpsimd",
                                   dbg_ao.rearrange("(a p) t -> p a t", p=128),
                                   sres, "DBG", waits=[("VE", vsh)])
                    vsh_g = ("DBG", vd3_)
                else:
                    vsh_g = None
                vx1 = layer_norm(sres, sbf, sbf, None, g_sb,
                                 [("VE", vsh), vsh_g] if vsh_g else
                                 [("VE", vsh)])
                qq = half * 2 + grp
                gin_dst = gins[qq].rearrange("(a p) t -> p a t", p=128)
                vgin = bld.dma("sync", gin_dst, sbf, "GIN",
                               waits=[("VE", vx1)])
                gin_prev = ("GIN", vgin)
                bld.op("gpsimd", lambda e, qq=qq: e.collective_compute(
                    "AllGather", ALU.bypass,
                    replica_groups=[[0, 2, 4, 6], [1, 3, 5, 7]],
                    ins=[gins[qq]], outs=[X1Gs[qq]]), inc="CC",
                    waits=[("GIN", vgin)])
                if DEBUG_X1:
                    dbg_dst = dbg_x1.rearrange("(a p) t -> p a t", p=128)[:, :, gcols]
                    vdbg = bld.dma("gpsimd", dbg_dst, sbf, "DBG",
                                   waits=[("VE", vx1)])
                    gin_prev = ("DBG", vdbg)
                ofu_pe_guard = bld.val("PE")
                if DEBUG_BSYNC:
                    vpe_ = bld.op("tensor", lambda e: e.nop(), inc="PE")
                    vve_ = bld.op("vector", lambda e: e.engine_nop(), inc="VE")
                    vse_ = bld.op("scalar", lambda e: e.nop(), inc="SE")
                    vgp_ = bld.op("gpsimd", lambda e: e.engine_nop(), inc="GP")
                    for eng_ in ("tensor", "vector", "scalar", "gpsimd", "sync"):
                        bld.wait(eng_, "PE", vpe_)
                        bld.wait(eng_, "VE", vve_)
                        bld.wait(eng_, "SE", vse_)
                        bld.wait(eng_, "GP", vgp_)
                if not DEBUG_NO_DEQUANT:
                    for pb in range(half * 8 + grp * 4,
                                    half * 8 + grp * 4 + 4):
                        dequant_block(gpackR, gscalR, WSg, pb, "WSG")
        v_wsg_done = bld.val("WSG")
        ccx = bld.val("CC")

        if DEBUG_STOP_AFTER_B:
            bld.wait("gpsimd", "DBG", bld.val("DBG"))
            bld.wait("gpsimd", "GIN", bld.val("GIN"))
            bld.wait("gpsimd", "CC", bld.val("CC"))
            bld.wait("sync", "OUTS", bld.val("OUTS"))
            _finish(nc, es, bld, E)
            return nc

        b_pe = bld.val("PE")
        bld.barrier(["sync", "gpsimd"], ["PE", "VE", "SE", "GP"])
        bank_mod[0] = 6
        vW0 = bld.dma("sync", W0, WSg.rearrange("(a p) n -> p a n", p=128),
                      "W0S", waits=[("WSG", v_wsg_done), ("PE", b_pe)])

        def x1g_panel(t):
            rb, q = t // 4, t % 4
            return X1Gs[q].rearrange("(rb a p) t -> p rb a t", p=128, rb=4)[
                :, rb, :, :]

        GTr = GT.rearrange("(t a p) n -> p t a n", p=128, t=16)
        ITr = IT.rearrange("(t a p) n -> p t a n", p=128, t=16)

        xf_guard = [None, None]
        gA_guard = None
        vxf_pre = {}
        vxf_pre[0] = bld.dma("sync", xf0, x1g_panel(0), "XF0",
                             waits=[("CC", 3)])
        vxf_pre[1] = bld.dma("sync", xf1, x1g_panel(1), "XF1",
                             waits=[("CC", 4)])
        for t in range(16):
            xf = (xf0, xf1)[t % 2]
            vxf = vxf_pre[t % 2]
            last_ev = None
            for mb in range(16):
                bi = next_bank()
                tiles = [(W0[:, kb, mb * 128:(mb + 1) * 128], xf[:, kb, :])
                         for kb in range(16)]
                bld.wait("tensor", "W0S", vW0)
                bld.wait("tensor", "XF%d" % (t % 2), vxf)
                vmm = mm_accum(bi, tiles)
                last_ev = evac(bi, "scalar",
                               lambda e, bi=bi, mb=mb: e.activation(
                                   bufA[:, mb, :], banks[bi][:], AF.Copy,
                                   bias=0.0, scale=1.0),
                               waits=[("PE", vmm), gA_guard])
            xf_guard[t % 2] = ("PE", bld.val("PE"))
            if t + 2 < 16:
                vxf_pre[t % 2] = bld.dma("sync", (xf0, xf1)[t % 2],
                                         x1g_panel(t + 2), "XF%d" % (t % 2),
                                         waits=[xf_guard[t % 2],
                                                ("CC", 2 + min(4, (t + 2) % 4 + 1))])
            vgt = bld.dma("sync", GTr[:, t, :, :], bufA, "GTS",
                          waits=[("SE", last_ev)])
            gA_guard = ("GTS", vgt)
            for j in range(2):
                pb = 2 * t + j
                if pb < 16:
                    dequant_block(upackR, uscalR, WSu, pb, "WSU")
                else:
                    dequant_block(dpackR, dscalR, WSd, pb - 16, "WSD")
        v_wsu_done = bld.val("WSU")
        p1a_pe = bld.val("PE")

        vW0u = bld.dma("sync", W0, WSu.rearrange("(a p) n -> p a n", p=128),
                       "W0S", waits=[("WSU", v_wsu_done), ("PE", p1a_pe)])
        bld.wait("sync", "GTS", bld.val("GTS"))
        gp_guard = [None, None]
        it_prev = None
        ge_guard = [None, None]
        ge_guard2 = [None, None]
        vxf_pre = {}
        vgp_pre = {}
        for tp_ in range(2):
            w_ = [xf_guard[tp_]] if xf_guard[tp_] else []
            vxf_pre[tp_] = bld.dma("sync", (xf0, xf1)[tp_], x1g_panel(tp_),
                                   "XF%d" % tp_, waits=w_)
            vgp_pre[tp_] = bld.dma("sync", (bufA, bufB)[tp_],
                                   GTr[:, tp_, :, :], ("BA", "BB")[tp_])
        for t in range(16):
            xf = (xf0, xf1)[t % 2]
            vxf = vxf_pre[t % 2]
            gp = (bufA, bufB)[t % 2]
            vgp = vgp_pre[t % 2]
            last_ev = None
            vth_last = None
            for mb in range(16):
                bi = next_bank()
                tiles = [(W0[:, kb, mb * 128:(mb + 1) * 128], xf[:, kb, :])
                         for kb in range(16)]
                bld.wait("tensor", "W0S", vW0u)
                bld.wait("tensor", "XF%d" % (t % 2), vxf)
                vmm = mm_accum(bi, tiles)
                gmb = gp[:, mb, :]
                ga = (ge_a0, ge_a1)[mb % 2]
                gb = (ge_b0, ge_b1)[mb % 2]
                bld.wait("vector", ("BA", "BB")[t % 2], vgp)
                w_ = [("SE", ge_guard[mb % 2])] if ge_guard[mb % 2] else []
                bld.op("vector", lambda e, gmb=gmb, ga=ga: e.tensor_tensor(
                    ga, gmb, gmb, ALU.mult), waits=w_)
                bld.op("vector", lambda e, gmb=gmb, ga=ga: e.tensor_tensor(
                    ga, ga, gmb, ALU.mult))
                vu = bld.op("vector", lambda e, gmb=gmb, ga=ga:
                            e.scalar_tensor_tensor(
                                ga, ga, 0.044715, gmb, ALU.mult, ALU.add),
                            inc="VE")
                w_ = [("VE", ge_guard2[mb % 2])] if ge_guard2[mb % 2] else []
                vth = bld.op("scalar", lambda e, ga=ga, gb=gb: e.activation(
                    gb, ga, AF.Tanh, bias=0.0, scale=0.7978845608028654),
                    inc="SE", waits=[("VE", vu)] + w_)
                ge_guard[mb % 2] = vth
                vth_last = vth
                bld.op("vector", lambda e, gb=gb: e.tensor_scalar(
                    gb, gb, 0.5, 0.5, ALU.mult, ALU.add),
                    waits=[("SE", vth)])
                bld.op("vector", lambda e, gmb=gmb, gb=gb: e.tensor_tensor(
                    gb, gb, gmb, ALU.mult))
                w_ = [("PE", vmm)] + ([it_prev] if it_prev else [])
                last_ev = evac(bi, "vector",
                               lambda e, bi=bi, mb=mb, gb=gb: e.tensor_tensor(
                                   bufC[:, mb, :], gb, banks[bi][:],
                                   ALU.mult), waits=w_)
                ge_guard2[mb % 2] = last_ev
            gp_guard[t % 2] = ("VE", last_ev)
            xf_guard[t % 2] = ("PE", bld.val("PE"))
            if t + 2 < 16:
                vxf_pre[t % 2] = bld.dma("sync", xf, x1g_panel(t + 2),
                                         "XF%d" % (t % 2),
                                         waits=[xf_guard[t % 2]])
                vgp_pre[t % 2] = bld.dma("sync", gp, GTr[:, t + 2, :, :],
                                         ("BA", "BB")[t % 2],
                                         waits=[gp_guard[t % 2]])
            vit = bld.dma("sync", ITr[:, t, :, :], bufC, "ITS",
                          waits=[("VE", last_ev)])
            it_prev = ("ITS", vit)
        v_wsd_done = bld.val("WSD")
        p1b_pe = bld.val("PE")

        vW0d = bld.dma("sync", W0, WSd.rearrange("(a p) n -> p a n", p=128),
                       "W0S", waits=[("WSD", v_wsd_done), ("PE", p1b_pe)])
        bld.wait("sync", "ITS", bld.val("ITS"))
        order = [0, 2, 4, 6, 8, 10, 12, 14, 1, 3, 5, 7, 9, 11, 13, 15]
        it_guard = [gp_guard[0], gp_guard[1]]
        dn_prev = it_prev
        vrs0 = None
        vxf_pre = {}
        vit_pre = {}
        for oi_ in range(2):
            w_ = [xf_guard[oi_]] if xf_guard[oi_] else []
            vxf_pre[oi_] = bld.dma("sync", (xf0, xf1)[oi_],
                                   x1g_panel(order[oi_]), "XF%d" % oi_,
                                   waits=w_)
            w_ = [it_guard[oi_]] if it_guard[oi_] else []
            vit_pre[oi_] = bld.dma("sync", (bufA, bufB)[oi_],
                                   ITr[:, order[oi_], :, :],
                                   ("BA", "BB")[oi_], waits=w_)
        for oi, t in enumerate(order):
            xf = (xf0, xf1)[oi % 2]
            vxf = vxf_pre[oi % 2]
            it_ = (bufA, bufB)[oi % 2]
            vit = vit_pre[oi % 2]
            last_ev = None
            for mb in range(16):
                bi = next_bank()
                tiles = [(W0[:, kb, mb * 128:(mb + 1) * 128], it_[:, kb, :])
                         for kb in range(16)]
                bld.wait("tensor", "W0S", vW0d)
                bld.wait("tensor", ("BA", "BB")[oi % 2], vit)
                vmm = mm_accum(bi, tiles)
                w_ = [("PE", vmm), ("XF%d" % (oi % 2), vxf)] + (
                    [dn_prev] if dn_prev else [])
                last_ev = evac(bi, "vector",
                               lambda e, bi=bi, mb=mb, xf=xf:
                               e.scalar_tensor_tensor(
                                   bufC[:, mb, :], xf[:, mb, :], 0.5,
                                   banks[bi][:], ALU.mult, ALU.add),
                               waits=w_)
            it_guard[oi % 2] = ("PE", bld.val("PE"))
            xf_guard[oi % 2] = ("VE", last_ev)
            if oi + 2 < 16:
                vxf_pre[oi % 2] = bld.dma("sync", xf, x1g_panel(order[oi + 2]),
                                          "XF%d" % (oi % 2),
                                          waits=[xf_guard[oi % 2]])
                vit_pre[oi % 2] = bld.dma("sync", it_,
                                          ITr[:, order[oi + 2], :, :],
                                          ("BA", "BB")[oi % 2],
                                          waits=[it_guard[oi % 2]])
            pd = PD0 if t % 2 == 0 else PD1
            pdsem = "PDS0" if t % 2 == 0 else "PDS1"
            vpd = bld.dma("sync",
                          pd.rearrange("(r a p) n -> p r a n", p=128, r=8)
                          [:, t // 2, :, :], bufC, pdsem,
                          waits=[("VE", last_ev)])
            dn_prev = (pdsem, vpd)
            if oi == 7:
                v_pd0 = bld.val("PDS0")
                vrs0 = bld.op("gpsimd", lambda e: e.collective_compute(
                    "ReduceScatter", ALU.add, replica_groups=[list(range(8))],
                    ins=[PD0], outs=[RSo0]), inc="CC",
                    waits=[("PDS0", v_pd0)])
        v_pd1 = bld.val("PDS1")
        vrs1 = bld.op("gpsimd", lambda e: e.collective_compute(
            "ReduceScatter", ALU.add, replica_groups=[list(range(8))],
            ins=[PD1], outs=[RSo1]), inc="CC",
            waits=[("PDS1", v_pd1)])

        bank_mod[0] = 4
        p2_pe = bld.val("PE")
        bld.barrier(["sync"], ["PE", "VE", "SE", "GP"])
        vout = None
        for q in range(2):
            rso = RSo0 if q == 0 else RSo1
            ccv = vrs0 if q == 0 else vrs1
            w_ = [("CC", ccv), ("PE", p2_pe if q == 0 else bld.val("PE"))]
            if vout is not None:
                w_.append(("VE", vout))
            vs2 = bld.dma("sync", s2_sb,
                          rso.rearrange("(a p) n -> p a n", p=128),
                          "S2R", waits=w_)
            vout = layer_norm(None, s2_sb, None, outp, sq2_sb,
                              [("S2R", vs2)])
            bld.dma("sync",
                    out.rearrange("(a p) t -> p a t", p=128)
                    [:, :, q * 512:(q + 1) * 512], outp,
                    "OUTS", waits=[("VE", vout)])
        bld.wait("sync", "OUTS", bld.val("OUTS"))
        bld.wait("gpsimd", "OUTS", bld.val("OUTS"))

        _finish(nc, es, bld, E)
    return nc


def _finish(nc, es, bld, E):
    block = E(nc.Block())

    @block.tensor
    def _(eng):
        for fn in bld.q["tensor"]:
            fn(eng)

    @block.vector
    def _(eng):
        for fn in bld.q["vector"]:
            fn(eng)

    @block.scalar
    def _(eng):
        for fn in bld.q["scalar"]:
            fn(eng)

    @block.gpsimd
    def _(eng):
        for fn in bld.q["gpsimd"]:
            fn(eng)

    @block.sync
    def _(eng):
        for fn in bld.q["sync"]:
            fn(eng)


GAMMA = 1.0 - 2.0 ** (-5.0 - np.arange(H, dtype=np.float64))


def _host_stage(core, inp):
    b = core // 2
    hs = core % 2
    g = GAMMA
    x = np.asarray(inp["L0_attn_x"])
    Wq = np.asarray(inp["L0_attn_Wq"])
    Wk = np.asarray(inp["L0_attn_Wk"])
    Wv = np.asarray(inp["L0_attn_Wv"])
    Wg = np.asarray(inp["L0_attn_Wg"])
    Wo = np.asarray(inp["L0_attn_Wo"])
    gn = np.asarray(inp["L0_attn_gnorm_w"])

    d = {}
    d["xT"] = np.ascontiguousarray(x[b].T)
    d["wqT"] = np.ascontiguousarray(Wq[512 * hs:512 * hs + 512].T)
    d["wkT"] = np.ascontiguousarray(Wk[512 * hs:512 * hs + 512].T)
    d["wvT"] = np.ascontiguousarray(Wv[1024 * hs:1024 * hs + 1024].T)
    def _ilv(w):
        a = np.ascontiguousarray(w.T).reshape(16, 128, 16, 128)
        return np.ascontiguousarray(
            a.transpose(1, 2, 0, 3).reshape(128 * 16 * 16, 128))
    d["wgaT"] = _ilv(Wg)
    d["woT"] = _ilv(Wo)
    d["gnormB"] = np.ascontiguousarray(
        np.broadcast_to(gn.astype(np.float32), (128, DVH)))
    inv = 1.0 / 10000.0 ** (np.arange(0, DKH, 2, dtype=np.float64) / DKH)
    ang = np.arange(SEQ, dtype=np.float64)[:, None] * inv[None, :]
    d["ropeC"] = np.ascontiguousarray(np.cos(ang).T.astype(np.float32))
    d["ropeS"] = np.ascontiguousarray(np.sin(ang).T.astype(np.float32))
    heads = [2 * hs, 2 * hs + 1]
    dm = np.zeros((128, 2 * CH), np.float32)
    gi = np.zeros((128, 2 * CH), np.float32)
    gk = np.zeros((128, 2), np.float32)
    gC = np.zeros((128, 2), np.float32)
    i_ = np.arange(CH)
    for j, h in enumerate(heads):
        gh = g[h]
        expo = (i_[:, None] - i_[None, :]).astype(np.float64)
        mask = np.where(expo >= 0, gh ** expo, 0.0) / 16.0
        dm[:, j * CH:(j + 1) * CH] = mask.T.astype(np.float32)
        gi[:, j * CH:(j + 1) * CH] = np.broadcast_to(
            (gh ** (i_ + 1) / 16.0).astype(np.float32), (128, CH))
        gk[0:CH, j] = (gh ** (CH - 1 - i_)).astype(np.float32)
        gC[:, j] = gh ** CH
    d["dmaskT"] = dm
    d["ginb"] = gi
    d["gkb"] = gk
    d["gCc"] = gC
    d["identM"] = np.eye(128, dtype=np.float32)
    sh = np.zeros((128, 1), np.int32)
    sh[1::2] = 4
    d["shifts"] = sh
    cp = np.zeros((128, 8), np.float32)
    cp[:, 0] = EPS
    d["cpool"] = cp
    d["onescol"] = np.ones((128, 1), np.float32)
    d["onesrow"] = np.ones((1, 128), np.float32)
    e0 = np.zeros((128, 128), np.float32)
    e0[0, :] = 1.0
    d["e0row"] = e0
    gp = np.asarray(inp["L0_ffn_gate_W_packed"])
    gs = np.asarray(inp["L0_ffn_gate_scales"])
    up = np.asarray(inp["L0_ffn_up_W_packed"])
    us = np.asarray(inp["L0_ffn_up_scales"])
    dp = np.asarray(inp["L0_ffn_down_W_packed"])
    ds = np.asarray(inp["L0_ffn_down_scales"])
    sl = slice(core * MSL, (core + 1) * MSL)
    d["gpackR"] = np.ascontiguousarray(np.repeat(gp.T[:, sl], 2, axis=0))
    d["gscalR"] = np.ascontiguousarray(np.repeat(gs.T[:, sl], 32, axis=0))
    d["upackR"] = np.ascontiguousarray(np.repeat(up.T[:, sl], 2, axis=0))
    d["uscalR"] = np.ascontiguousarray(np.repeat(us.T[:, sl], 32, axis=0))
    d["dpackR"] = np.ascontiguousarray(
        np.repeat(dp.T[core * MSL // 2:(core + 1) * MSL // 2, :], 2, axis=0))
    d["dscalR"] = np.ascontiguousarray(
        np.repeat(ds.T[core * MSL // 32:(core + 1) * MSL // 32, :], 32, axis=0))
    return d


def kernel(**inputs):
    in_maps = [_host_stage(c, inputs) for c in range(8)]
    nc = build_program()
    res = run_bass_kernel_spmd(nc, in_maps, core_ids=list(range(8)))
    full = np.empty((BB, SEQ, DM), np.float32)
    for c in range(8):
        o = np.asarray(res.results[c]["out"])
        full[c // 2, 1024 * (c % 2):1024 * (c % 2) + 1024, :] = o.T
    return full


# version 04
# speedup vs baseline: 1.4976x; 1.0912x over previous
import numpy as np
import ml_dtypes
import contextlib
import concourse.bass as bass
import concourse.mybir as mybir
from concourse.bass_utils import run_bass_kernel_spmd

F32 = mybir.dt.float32
BF16 = mybir.dt.bfloat16
I32 = mybir.dt.int32
AF = mybir.ActivationFunctionType
ALU = mybir.AluOpType
NPBF = ml_dtypes.bfloat16

BB, SEQ, DM = 4, 2048, 2048
H, DKH, DVH = 4, 256, 512
CH = 128
NCH = SEQ // CH
MLP = 16384
MSL = MLP // 8
EPS = 1e-5

DEBUG_STOP_AFTER_B = False
DEBUG_X1 = False
DEBUG_ONT = False
DEBUG_S = False
DEBUG_NO_INTER = False
DEBUG_FREEZE_SB = False
DEBUG_RETSYNC = False
DEBUG_B = False
DEBUG_BSYNC = False
DEBUG_NO_DEQUANT = False
DEBUG_WS = False
DEBUG_RS = False

ENGS = ("tensor", "vector", "scalar", "gpsimd", "sync")


class Bld:
    def __init__(self, nc):
        self.nc = nc
        self.q = {e: [] for e in ENGS}
        self.sems = {}
        self.seen = {e: {} for e in ENGS}

    def add_sem(self, name, handle):
        self.sems[name] = [handle, 0]

    def val(self, name):
        return self.sems[name][1]

    def emit(self, eng, fn):
        self.q[eng].append(fn)

    def wait(self, eng, name, v=None):
        if v is None:
            v = self.sems[name][1]
        if v is None or v <= 0:
            return
        if self.seen[eng].get(name, -1) >= v:
            return
        self.seen[eng][name] = v
        h = self.sems[name][0]
        self.emit(eng, lambda e, h=h, v=v: e.wait_ge(h, v))

    def op(self, eng, fn, inc=None, inc_n=1, waits=()):
        for w in waits:
            if w is None:
                continue
            n, v = w
            self.wait(eng, n, v)
        if inc is None:
            self.emit(eng, fn)
            return None
        s = self.sems[inc]
        s[1] += inc_n
        v = s[1]
        h = s[0]
        self.emit(eng, lambda e, fn=fn, h=h, n=inc_n: fn(e).then_inc(h, n))
        return v

    def dma(self, eng, out, in_, inc, waits=()):
        return self.op(eng, lambda e, o=out, i=in_: e.dma_start(out=o, in_=i),
                       inc=inc, inc_n=16, waits=waits)

    def barrier(self, eng_list, sem_list):
        for e in eng_list:
            for s in sem_list:
                self.wait(e, s, self.val(s))


def _carve(arena_ap, off_bytes, dt, shape):
    isz = {F32: 4, BF16: 2, I32: 4}[dt]
    assert off_bytes % 32 == 0, off_bytes
    n_el = 1
    for s in shape[1:]:
        n_el *= s
    a = arena_ap.bitcast(dt)
    start = off_bytes // isz
    v = a[:, start:start + n_el]
    if len(shape) == 3:
        v = v.rearrange("p (a b) -> p a b", b=shape[2])
    elif len(shape) == 4:
        v = v.rearrange("p (a b c) -> p a b c", b=shape[2], c=shape[3])
    end = off_bytes + n_el * isz
    return v[0:shape[0]], (end + 31) // 32 * 32


def build_program():
    nc = bass.Bass("TRN2", target_bir_lowering=False, debug=False)
    dt_ = nc.dram_tensor

    xT = dt_("xT", [DM, SEQ], F32, kind="ExternalInput").ap()
    wqT = dt_("wqT", [DM, 512], F32, kind="ExternalInput").ap()
    wkT = dt_("wkT", [DM, 512], F32, kind="ExternalInput").ap()
    wvT = dt_("wvT", [DM, 1024], F32, kind="ExternalInput").ap()
    wgaT = dt_("wgaT", [128 * 16 * 16, 128], F32, kind="ExternalInput").ap()
    woT = dt_("woT", [128 * 16 * 16, 128], F32, kind="ExternalInput").ap()
    gnormB = dt_("gnormB", [128, DVH], F32, kind="ExternalInput").ap()
    ropeC = dt_("ropeC", [128, SEQ], F32, kind="ExternalInput").ap()
    ropeS = dt_("ropeS", [128, SEQ], F32, kind="ExternalInput").ap()
    dmaskT = dt_("dmaskT", [128, 2 * CH], F32, kind="ExternalInput").ap()
    ginb = dt_("ginb", [128, 2 * CH], F32, kind="ExternalInput").ap()
    gkb = dt_("gkb", [128, 2], F32, kind="ExternalInput").ap()
    gCc = dt_("gCc", [128, 2], F32, kind="ExternalInput").ap()
    identM = dt_("identM", [128, 128], F32, kind="ExternalInput").ap()
    shifts = dt_("shifts", [128, 1], I32, kind="ExternalInput").ap()
    cpool = dt_("cpool", [128, 8], F32, kind="ExternalInput").ap()
    onescol = dt_("onescol", [128, 1], F32, kind="ExternalInput").ap()
    onesrow = dt_("onesrow", [1, 128], F32, kind="ExternalInput").ap()
    e0row = dt_("e0row", [128, 128], F32, kind="ExternalInput").ap()
    gpackR = dt_("gpackR", [MSL, DM], I32, kind="ExternalInput").ap()
    gscalR = dt_("gscalR", [MSL, DM], F32, kind="ExternalInput").ap()
    upackR = dt_("upackR", [MSL, DM], I32, kind="ExternalInput").ap()
    uscalR = dt_("uscalR", [MSL, DM], F32, kind="ExternalInput").ap()
    dpackR = dt_("dpackR", [MSL, DM], I32, kind="ExternalInput").ap()
    dscalR = dt_("dscalR", [MSL, DM], F32, kind="ExternalInput").ap()

    out = dt_("out", [DM, 1024], F32, kind="ExternalOutput").ap()
    dbg_x1 = dt_("dbg_x1", [DM, SEQ], F32, kind="ExternalOutput").ap() \
        if DEBUG_X1 else None
    dbg_ont = dt_("dbg_ont", [1024, SEQ], F32, kind="ExternalOutput").ap() \
        if DEBUG_ONT else None
    dbg_S = dt_("dbg_S", [128, 2048], F32, kind="ExternalOutput").ap() \
        if DEBUG_S else None
    dbg_Sb = dt_("dbg_Sb", [128, 1024], F32, kind="ExternalOutput").ap() \
        if DEBUG_S else None
    dbg_osc = dt_("dbg_osc", [128, 512], F32, kind="ExternalOutput").ap() \
        if DEBUG_S else None
    dbg_ofu = dt_("dbg_ofu", [2048, 1024], F32, kind="ExternalOutput").ap() \
        if DEBUG_B else None
    dbg_ws = dt_("dbg_ws", [DM, MSL], F32, kind="ExternalOutput").ap() \
        if DEBUG_WS else None
    dbg_rs = dt_("dbg_rs", [DM, 1024], F32, kind="ExternalOutput").ap() \
        if DEBUG_RS else None
    dbg_it = dt_("dbg_it", [2048, 1024], F32, kind="ExternalOutput").ap() \
        if DEBUG_RS else None
    dbg_g = dt_("dbg_g", [2048, 512], F32, kind="ExternalOutput").ap() \
        if DEBUG_B else None
    dbg_ao = dt_("dbg_ao", [2048, 512], F32, kind="ExternalOutput").ap() \
        if DEBUG_B else None
    dbg_qt = dt_("dbg_qt", [128, 512], F32, kind="ExternalOutput").ap() \
        if DEBUG_S else None

    exin0 = dt_("exin0", [1024, 1024], BF16).ap()
    exin1 = dt_("exin1", [1024, 1024], BF16).ap()
    ex0 = dt_("ex0", [2048, 1024], BF16).ap()
    ex1 = dt_("ex1", [2048, 1024], BF16).ap()
    gins = [dt_("gin%d" % q, [DM, 512], BF16).ap() for q in range(4)]
    X1Gs = [dt_("X1G%d" % q, [4 * DM, 512], BF16).ap() for q in range(4)]
    WSg = dt_("WSg", [DM, MSL], BF16).ap()
    WSu = dt_("WSu", [DM, MSL], BF16).ap()
    WSd = dt_("WSd", [MSL, DM], BF16).ap()
    GT = dt_("GT", [16 * MSL, 512], BF16).ap()
    IT = dt_("IT", [16 * MSL, 512], BF16).ap()
    PD0 = dt_("PD0", [8 * DM, 512], BF16).ap()
    PD1 = dt_("PD1", [8 * DM, 512], BF16).ap()
    RSo0 = dt_("RSo0", [DM, 512], BF16).ap()
    RSo1 = dt_("RSo1", [DM, 512], BF16).ap()
    XB = dt_("XB", [DM, SEQ], BF16).ap()
    WGA2 = dt_("WGA2", [128 * 16 * 16, 128], BF16).ap()
    WO2 = dt_("WO2", [128 * 16 * 16, 128], BF16).ap()

    es = contextlib.ExitStack()
    with es:
        E = es.enter_context
        AR_t = E(nc.sbuf_tensor("ARENA", [128, 73728], BF16))
        AR = AR_t[:]
        s_dmask = E(nc.sbuf_tensor("s_dmask", [128, 2 * CH], F32))
        s_ginb = E(nc.sbuf_tensor("s_ginb", [128, 2 * CH], F32))
        s_gkb = E(nc.sbuf_tensor("s_gkb", [128, 2], F32))
        s_gCc = E(nc.sbuf_tensor("s_gCc", [128, 2], F32))
        s_gnorm = E(nc.sbuf_tensor("s_gnorm", [128, DVH], F32))
        s_ident = E(nc.sbuf_tensor("s_ident", [128, 128], BF16))
        s_shift = E(nc.sbuf_tensor("s_shift", [128, 1], I32))
        s_cpool = E(nc.sbuf_tensor("s_cpool", [128, 8], F32))
        s_onesc = E(nc.sbuf_tensor("s_onesc", [128, 1], BF16))
        s_onesr = E(nc.sbuf_tensor("s_onesr", [1, 128], F32))
        s_E0 = E(nc.sbuf_tensor("s_E0", [128, 128], F32))
        SC = E(nc.sbuf_tensor("SC", [128, 11776], BF16))
        DQ = E(nc.sbuf_tensor("DQ", [128, 16384], BF16))
        p_a = E(nc.psum_tensor("p_a", [128, 512], F32))
        p_b = E(nc.psum_tensor("p_b", [128, 512], F32))
        p_c = E(nc.psum_tensor("p_c", [128, 512], F32))
        p_d = E(nc.psum_tensor("p_d", [128, 512], F32))
        p_s0 = E(nc.psum_tensor("p_s0", [128, 512], F32))
        p_s1 = E(nc.psum_tensor("p_s1", [128, 512], F32))
        p_t = E(nc.psum_tensor("p_t", [128, 1024], BF16))

        bld = Bld(nc)
        for s in ("LD0", "PE", "VE", "SE", "GP", "CC", "XP", "XG",
                  "WP0", "WP1", "XR0", "XR1", "OFU", "GIN", "XF0", "XF1",
                  "BA", "BB", "W0S", "S2R", "DQP", "DQS", "WSG", "WSU",
                  "WSD", "GTS", "ITS", "PDS0", "PDS1", "EXN", "OUTS", "DBG"):
            bld.add_sem(s, E(nc.semaphore(s)))

        eps_ap = s_cpool[:, 0:1]
        zero_ap = s_cpool[:, 1:2]

        MB = 1024 * 1024
        o = 0
        wq_sb, o = _carve(AR, o, BF16, [128, 16, 512])
        wk_sb, o = _carve(AR, o, BF16, [128, 16, 512])
        wv_sb, o = _carve(AR, o, BF16, [128, 16, 1024])
        xp_sb, o = _carve(AR, o, BF16, [128, 16, 512])
        q_sb, o = _carve(AR, o, BF16, [128, 4, 512])
        k_sb, o = _carve(AR, o, BF16, [128, 4, 512])
        v_sb, o = _carve(AR, o, BF16, [128, 4, 2, 512])
        S_f, o = _carve(AR, o, F32, [128, 2, 1024])
        S_b, o = _carve(AR, o, BF16, [128, 2, 1024])
        a_end = o
        ONT_OFF = (73728 * 2) - (4 * MB // 128)
        ont_sb, _ = _carve(AR, ONT_OFF, BF16, [128, 8, 2048])
        assert a_end <= ONT_OFF, (a_end, ONT_OFF)
        o = 0
        ofu_sb, o = _carve(AR, o, BF16, [128, 16, 1024])
        g_sb, o = _carve(AR, o, BF16, [128, 16, 512])
        sres, o = _carve(AR, o, F32, [128, 16, 512])
        sbf, o = _carve(AR, o, BF16, [128, 16, 512])
        wpan0, o = _carve(AR, o, BF16, [128, 16, 128])
        wpan1, o = _carve(AR, o, BF16, [128, 16, 128])
        xres0, o = _carve(AR, o, F32, [128, 512])
        xres1, o = _carve(AR, o, F32, [128, 512])
        assert o <= ONT_OFF
        xbp, _ = _carve(AR, ONT_OFF, BF16, [128, 16, 512])
        o = 0
        W0, o = _carve(AR, o, BF16, [128, 16, 2048])
        xf0, o = _carve(AR, o, BF16, [128, 16, 512])
        xf1, o = _carve(AR, o, BF16, [128, 16, 512])
        bufA, o = _carve(AR, o, BF16, [128, 16, 512])
        bufB, o = _carve(AR, o, BF16, [128, 16, 512])
        bufC, o = _carve(AR, o, BF16, [128, 16, 512])
        assert o <= 73728 * 2
        o = ONT_OFF
        s2_sb, o = _carve(AR, o, BF16, [128, 16, 512])
        sq2_sb, o = _carve(AR, o, BF16, [128, 16, 512])

        o = 0
        o_scr, o = _carve(SC[:], o, F32, [128, 512])
        sq_scr, o = _carve(SC[:], o, F32, [128, 512])
        attn_sb, o = _carve(SC[:], o, BF16, [128, 128])
        ktil, o = _carve(SC[:], o, BF16, [128, 256])
        qtil, o = _carve(SC[:], o, BF16, [128, 256])
        on_sb, o = _carve(SC[:], o, BF16, [128, 512])
        ms_scr, o = _carve(SC[:], o, F32, [128, 1])
        rr_scr, o = _carve(SC[:], o, F32, [128, 1])
        statm, o = _carve(SC[:], o, F32, [128, 512])
        statr, o = _carve(SC[:], o, F32, [128, 512])
        stat_q, o = _carve(SC[:], o, F32, [1, 512])
        stat_t, o = _carve(SC[:], o, F32, [1, 512])
        mb_b, o = _carve(SC[:], o, F32, [128, 512])
        rb_b, o = _carve(SC[:], o, F32, [128, 512])
        lnod0, o = _carve(SC[:], o, F32, [128, 512])
        lnod1, o = _carve(SC[:], o, F32, [128, 512])
        assert o <= 11776 * 2

        o = 0
        s_ropeC, o = _carve(DQ[:], o, BF16, [128, SEQ])
        s_ropeS, o = _carve(DQ[:], o, BF16, [128, SEQ])
        t_rope, o = _carve(DQ[:], o, F32, [128, 512])
        u_rope, o = _carve(DQ[:], o, F32, [128, 512])
        o = 0
        ge_a0, o = _carve(DQ[:], o, F32, [128, 512])
        ge_a1, o = _carve(DQ[:], o, F32, [128, 512])
        ge_b0, o = _carve(DQ[:], o, F32, [128, 512])
        ge_b1, o = _carve(DQ[:], o, F32, [128, 512])
        dq_pack, o = _carve(DQ[:], o, I32, [128, 1024])
        dq_scal, o = _carve(DQ[:], o, F32, [128, 1024])
        dq_idx, o = _carve(DQ[:], o, I32, [128, 512])
        dq_c, o = _carve(DQ[:], o, I32, [128, 512])
        dq_f, o = _carve(DQ[:], o, F32, [128, 512])
        dq_r1, o = _carve(DQ[:], o, F32, [128, 512])
        dq_r2, o = _carve(DQ[:], o, F32, [128, 512])
        dq_s2, o = _carve(DQ[:], o, F32, [128, 512])
        dq_w, o = _carve(DQ[:], o, BF16, [128, 2048])
        assert o <= 16384 * 2, o

        banks = [p_a, p_b, p_c, p_d, p_s0, p_s1]
        bank_guard = [None] * 6
        bank_i = [0]
        bank_mod = [4]

        def next_bank():
            i = bank_i[0] % bank_mod[0]
            bank_i[0] += 1
            if bank_guard[i] is not None:
                bld.wait("tensor", *bank_guard[i])
            return i

        def mm_accum(bi, tiles, n_slice=None):
            pa = banks[bi][:] if n_slice is None else banks[bi][:, 0:n_slice]
            nT = len(tiles)
            v = None
            for i, (lt, rh) in enumerate(tiles):
                st = (i == 0)
                if i == nT - 1:
                    v = bld.op("tensor",
                               lambda e, lt=lt, rh=rh, st=st, pa=pa:
                               e.matmul(pa, lt, rh, start=st, stop=True),
                               inc="PE")
                else:
                    bld.op("tensor",
                           lambda e, lt=lt, rh=rh, st=st, pa=pa:
                           e.matmul(pa, lt, rh, start=st, stop=False))
            return v

        def evac(bi, eng, fn, waits=()):
            sem = {"vector": "VE", "scalar": "SE"}[eng]
            v = bld.op(eng, fn, inc=sem, waits=waits)
            bank_guard[bi] = (sem, v)
            return v

        class DqFeeder:
            def __init__(self):
                self.jobs = []
                self.pos = 0
                self.guard = []
                self.vld = None
                self.vld2 = None

            def add(self, packR, scalR, WS, st_sem):
                for pb in range(16):
                    self.jobs.append((packR, scalR, WS, pb, st_sem))

            def feed(self, n=1):
                for _ in range(n):
                    self._one()

            def finish_all(self):
                while self.jobs:
                    self._one()

            def _one(self):
                if not self.jobs:
                    return
                packR, scalR, WS, pb, st_sem = self.jobs[0]
                if self.pos % 2 == 0:
                    hf = self.pos // 2
                    hsl = slice(hf * 1024, (hf + 1) * 1024)
                    self.vld = bld.dma(
                        "scalar", dq_pack,
                        packR.rearrange("(a p) n -> p a n", p=128)[:, pb, hsl],
                        "DQP", waits=list(self.guard))
                    self.vld2 = bld.dma(
                        "scalar", dq_scal,
                        scalR.rearrange("(a p) n -> p a n", p=128)[:, pb, hsl],
                        "DQS")
                i = self.pos % 2
                hf = self.pos // 2
                sl = slice(i * 512, (i + 1) * 512)
                pk = dq_pack[:, sl]
                scl = dq_scal[:, sl]
                wo = dq_w[:, hf * 1024 + i * 512:hf * 1024 + (i + 1) * 512]
                bld.op("vector", lambda e, pk=pk: e.tensor_scalar(
                    dq_c, pk, s_shift[:], 7, ALU.logical_shift_right,
                    ALU.bitwise_and),
                    waits=[("DQP", self.vld), ("DQS", self.vld2)] +
                          list(self.guard))
                bld.op("vector", lambda e, pk=pk: e.tensor_scalar(
                    dq_idx, pk, s_shift[:], 8, ALU.logical_shift_right,
                    ALU.bitwise_and))
                bld.op("vector", lambda e: e.tensor_scalar(
                    dq_f, dq_idx, -0.125, 0.5, ALU.mult, ALU.add))
                bld.op("vector", lambda e: e.tensor_scalar(
                    dq_r1, dq_c, -4.0, 0.0, ALU.add, ALU.max))
                bld.op("vector", lambda e: e.tensor_scalar(
                    dq_r2, dq_c, -6.0, 0.0, ALU.add, ALU.max))
                bld.op("vector", lambda e: e.scalar_tensor_tensor(
                    dq_r2, dq_r2, 2.0, dq_r1, ALU.mult, ALU.add))
                bld.op("vector", lambda e: e.tensor_tensor(
                    dq_s2, dq_r2, dq_c, ALU.add))
                bld.op("vector", lambda e: e.tensor_tensor(
                    dq_s2, dq_s2, dq_f, ALU.mult))
                last = bld.op("vector", lambda e, scl=scl, wo=wo:
                              e.tensor_tensor(wo, dq_s2, scl, ALU.mult),
                              inc="VE")
                self.guard = [("VE", last)]
                self.pos += 1
                if self.pos == 4:
                    _, _, WS_, pb_, st_ = self.jobs.pop(0)
                    vst = bld.dma(
                        "scalar",
                        WS_.rearrange("(a p) n -> p a n", p=128)[:, pb_, :],
                        dq_w, st_, waits=[("VE", last)])
                    self.guard = [("VE", last), (st_, vst)]
                    self.pos = 0

        dqf = DqFeeder()

        st_guard = [None, None]

        def layer_norm(src_f32, srcbf, dst_bf, dst_f32, sqbuf, ready,
                       store_fn=None):
            sum_ap = p_s0[0:1, :]
            sq_ap = p_s1[0:1, :]
            for bidx in (4, 5):
                if bank_guard[bidx] is not None:
                    bld.wait("tensor", *bank_guard[bidx])
                    bank_guard[bidx] = None
            for kb in range(16):
                st, sp = (kb == 0), (kb == 15)
                if sp:
                    vsum = bld.op("tensor", lambda e, kb=kb, st=st: e.matmul(
                        sum_ap, s_onesc[:], srcbf[:, kb, :], start=st, stop=True),
                        inc="PE", waits=list(ready))
                else:
                    bld.op("tensor", lambda e, kb=kb, st=st: e.matmul(
                        sum_ap, s_onesc[:], srcbf[:, kb, :], start=st, stop=False),
                        waits=list(ready))
            vsq_in = bld.op("vector", lambda e: e.tensor_tensor(
                sqbuf[:], srcbf[:], srcbf[:], ALU.mult), inc="VE",
                waits=list(ready))
            for kb in range(16):
                st, sp = (kb == 0), (kb == 15)
                if sp:
                    vsq = bld.op("tensor", lambda e, kb=kb, st=st: e.matmul(
                        sq_ap, s_onesc[:], sqbuf[:, kb, :], start=st, stop=True),
                        inc="PE", waits=[("VE", vsq_in)])
                else:
                    bld.op("tensor", lambda e, kb=kb, st=st: e.matmul(
                        sq_ap, s_onesc[:], sqbuf[:, kb, :], start=st, stop=False),
                        waits=[("VE", vsq_in)])
            vme = bld.op("vector", lambda e: e.tensor_scalar(
                statm[0:1, :], sum_ap, 1.0 / 2048, None, ALU.mult),
                inc="VE", waits=[("PE", vsum)])
            bld.op("vector", lambda e: e.tensor_scalar(
                stat_q, sq_ap, 1.0 / 2048, None, ALU.mult),
                waits=[("PE", vsq)])
            bld.op("vector", lambda e: e.tensor_tensor(
                stat_t, statm[0:1, :], statm[0:1, :], ALU.mult))
            vvar = bld.op("vector", lambda e: e.tensor_tensor(
                stat_q, stat_q, stat_t, ALU.subtract), inc="VE")
            bld.op("scalar", lambda e: e.activation(
                stat_q, stat_q, AF.Ln, bias=eps_ap[0:1, :], scale=1.0),
                waits=[("VE", vvar)])
            vrs = bld.op("scalar", lambda e: e.activation(
                statr[0:1, :], stat_q, AF.Exp, bias=zero_ap[0:1, :],
                scale=-0.5), inc="SE")
            bi = next_bank()
            vbc1 = bld.op("tensor", lambda e, bi=bi: e.matmul(
                banks[bi][:], s_E0[:], statm, start=True, stop=True),
                inc="PE", waits=[("SE", vrs), ("VE", vme)])
            bj = next_bank()
            vbc2 = bld.op("tensor", lambda e, bj=bj: e.matmul(
                banks[bj][:], s_E0[:], statr, start=True, stop=True),
                inc="PE")
            evac(bi, "vector", lambda e, bi=bi: e.tensor_copy(
                mb_b, banks[bi][:]), waits=[("PE", vbc1)])
            evac(bj, "vector", lambda e, bj=bj: e.tensor_copy(
                rb_b, banks[bj][:]), waits=[("PE", vbc2)])
            last = None
            src = src_f32 if src_f32 is not None else srcbf
            dst = dst_bf if dst_bf is not None else dst_f32
            for kb in range(16):
                if store_fn is not None:
                    ob = (lnod0, lnod1)[kb % 2]
                    w_ = [st_guard[kb % 2]] if st_guard[kb % 2] else []
                    bld.op("vector", lambda e, kb=kb: e.tensor_tensor(
                        o_scr, src[:, kb, :], mb_b, ALU.subtract), waits=w_)
                    last = bld.op("vector", lambda e, kb=kb, ob=ob:
                                  e.tensor_tensor(ob, o_scr, rb_b, ALU.mult),
                                  inc="VE")
                    st_guard[kb % 2] = ("OUTS", store_fn(kb, ob, last))
                else:
                    bld.op("vector", lambda e, kb=kb: e.tensor_tensor(
                        o_scr, src[:, kb, :], mb_b, ALU.subtract))
                    last = bld.op("vector", lambda e, kb=kb: e.tensor_tensor(
                        dst[:, kb, :], o_scr, rb_b, ALU.mult), inc="VE")
            return last

        bld.dma("sync", s_dmask[:], dmaskT, "LD0")
        bld.dma("sync", s_ginb[:], ginb, "LD0")
        bld.dma("sync", s_gkb[:], gkb, "LD0")
        bld.dma("sync", s_gCc[:], gCc, "LD0")
        bld.dma("sync", s_gnorm[:], gnormB, "LD0")
        bld.dma("sync", s_shift[:], shifts, "LD0")
        bld.dma("sync", s_cpool[:], cpool, "LD0")
        bld.dma("sync", s_onesr[:], onesrow, "LD0")
        bld.dma("sync", s_E0[:], e0row, "LD0")
        bld.dma("gpsimd", s_ropeC[:], ropeC, "LD0")
        bld.dma("gpsimd", s_ropeS[:], ropeS, "LD0")
        bld.dma("gpsimd", s_ident[:], identM, "LD0")
        bld.dma("gpsimd", s_onesc[:], onescol, "LD0")
        v_xb = bld.dma("gpsimd", XB, xT, "LD0")
        v_wga2 = bld.dma("gpsimd", WGA2, wgaT, "LD0")
        v_wo2 = bld.dma("gpsimd", WO2, woT, "LD0")
        v_wq = bld.dma("gpsimd", wq_sb, wqT.rearrange("(a p) n -> p a n", p=128), "LD0")
        v_wk = bld.dma("gpsimd", wk_sb, wkT.rearrange("(a p) n -> p a n", p=128), "LD0")
        v_wv = bld.dma("gpsimd", wv_sb, wvT.rearrange("(a p) n -> p a n", p=128), "LD0")
        v_consts = bld.val("LD0")
        for e in ENGS:
            bld.wait(e, "LD0", v_consts)

        bld.op("vector", lambda e: e.memset(statm, 0.0))
        bld.op("vector", lambda e: e.memset(statr, 0.0))
        bld.op("vector", lambda e: e.memset(S_f, 0.0))
        vS1 = bld.op("vector", lambda e: e.memset(S_b, 0.0), inc="VE")

        sbf_ready = {0: ("VE", vS1), 1: ("VE", vS1)}
        xp_guard = None
        pt_guard = None
        ps_guard = None
        g_attn = {}
        dbg_guard = {}

        for p in range(4):
            cols = slice(p * 512, (p + 1) * 512)
            w_ = [xp_guard] if xp_guard else []
            vxp = bld.dma("sync", xp_sb,
                          XB.rearrange("(a p) t -> p a t", p=128)[:, :, cols],
                          "XP", waits=w_)
            for (w_sb, dst, vw) in ((wq_sb, q_sb, v_wq), (wk_sb, k_sb, v_wk)):
                for h in range(2):
                    bi = next_bank()
                    bj = next_bank()
                    vmm = None
                    for (b_, mb) in ((bi, 0), (bj, 1)):
                        m = h * 2 + mb
                        tiles = [(w_sb[:, kb, m * 128:(m + 1) * 128],
                                  xp_sb[:, kb, :]) for kb in range(16)]
                        bld.wait("tensor", "XP", vxp)
                        vmm = mm_accum(b_, tiles)
                    p0 = banks[bi][:]
                    p1 = banks[bj][:]
                    cC = s_ropeC[:, cols]
                    cS = s_ropeS[:, cols]
                    d0 = dst[:, h * 2 + 0, :]
                    d1 = dst[:, h * 2 + 1, :]
                    bld.wait("vector", "PE", vmm)
                    bld.op("vector", lambda e, p0=p0, cC=cC: e.tensor_tensor(
                        t_rope, p0, cC, ALU.mult))
                    bld.op("vector", lambda e, p1=p1, cS=cS: e.tensor_tensor(
                        u_rope, p1, cS, ALU.mult))
                    bld.op("vector", lambda e, d0=d0: e.tensor_tensor(
                        d0, t_rope, u_rope, ALU.subtract))
                    bld.op("vector", lambda e, p1=p1, cC=cC: e.tensor_tensor(
                        t_rope, p1, cC, ALU.mult))
                    bld.op("vector", lambda e, p0=p0, cS=cS: e.tensor_tensor(
                        u_rope, p0, cS, ALU.mult))
                    vev = bld.op("vector", lambda e, d1=d1: e.tensor_tensor(
                        d1, t_rope, u_rope, ALU.add), inc="VE")
                    bank_guard[bi] = ("VE", vev)
                    bank_guard[bj] = ("VE", vev)
            for tmb in range(4):
                for h in range(2):
                    bi = next_bank()
                    tiles = [(xp_sb[:, kb, tmb * 128:(tmb + 1) * 128],
                              wv_sb[:, kb, h * 512:(h + 1) * 512])
                             for kb in range(16)]
                    bld.wait("tensor", "XP", vxp)
                    vmm = mm_accum(bi, tiles)
                    evac(bi, "vector",
                         lambda e, bi=bi, tmb=tmb, h=h: e.tensor_copy(
                             v_sb[:, tmb, h, :], banks[bi][:]),
                         waits=[("PE", vmm)])
            vqk_done = bld.val("VE")
            for cc in range(4):
                ch_cols = slice(cc * 128, (cc + 1) * 128)
                gch = 4 * p + cc
                for h in range(2):
                    bi = next_bank()
                    tiles = [(k_sb[:, h * 2 + mb, ch_cols],
                              q_sb[:, h * 2 + mb, ch_cols]) for mb in range(2)]
                    bld.wait("tensor", "VE", vqk_done)
                    vat = mm_accum(bi, tiles, n_slice=128)
                    vev = evac(bi, "vector",
                               lambda e, bi=bi, h=h: e.tensor_tensor(
                                   attn_sb, banks[bi][:, 0:128],
                                   s_dmask[:, h * CH:(h + 1) * CH], ALU.mult),
                               waits=[("PE", vat), g_attn.get("attn")])
                    if pt_guard is not None:
                        bld.wait("tensor", *pt_guard)
                    vtr = None
                    for mb in range(2):
                        vtr = bld.op("tensor",
                                     lambda e, h=h, mb=mb, ch_cols=ch_cols:
                                     e.transpose(
                                         p_t[:, mb * 128:(mb + 1) * 128],
                                         k_sb[:, h * 2 + mb, ch_cols],
                                         s_ident[:]), inc="PE")
                    vkt = bld.op("vector", lambda e, h=h: e.tensor_scalar(
                        ktil, p_t[:, 0:256], s_gkb[:, h:h + 1], None,
                        ALU.mult), inc="VE",
                        waits=[("PE", vtr), g_attn.get("ktil")])
                    vqt = None
                    for mb in range(2):
                        vqt = bld.op("vector",
                                     lambda e, h=h, mb=mb, ch_cols=ch_cols:
                                     e.tensor_tensor(
                                         qtil[:, mb * 128:(mb + 1) * 128],
                                         q_sb[:, h * 2 + mb, ch_cols],
                                         s_ginb[:, h * CH:(h + 1) * CH],
                                         ALU.mult), inc="VE",
                                     waits=[g_attn.get("qtil")])
                    bo = next_bank()
                    bld.wait("tensor", "VE", vev)
                    if DEBUG_NO_INTER:
                        vo = bld.op("tensor", lambda e, bo=bo, cc=cc, h=h:
                                    e.matmul(banks[bo][:], attn_sb,
                                             v_sb[:, cc, h, :],
                                             start=True, stop=True), inc="PE")
                        bld.wait("tensor", "VE", vqt)
                        bld.wait("tensor", *sbf_ready[h])
                    else:
                        bld.op("tensor", lambda e, bo=bo, cc=cc, h=h: e.matmul(
                            banks[bo][:], attn_sb, v_sb[:, cc, h, :],
                            start=True, stop=False))
                        bld.wait("tensor", "VE", vqt)
                        bld.wait("tensor", *sbf_ready[h])
                        bld.op("tensor", lambda e, bo=bo, h=h: e.matmul(
                            banks[bo][:], qtil[:, 0:128], S_b[:, h, 0:512],
                            start=False, stop=False))
                        vo = bld.op("tensor", lambda e, bo=bo, h=h: e.matmul(
                            banks[bo][:], qtil[:, 128:256],
                            S_b[:, h, 512:1024],
                            start=False, stop=True), inc="PE")
                    g_attn["attn"] = ("PE", vo)
                    g_attn["qtil"] = ("PE", vo)
                    bld.wait("tensor", "VE", vkt)
                    if ps_guard is not None:
                        bld.wait("tensor", *ps_guard)
                    vsu = None
                    for mb, psx in ((0, p_s0), (1, p_s1)):
                        vsu = bld.op("tensor",
                                     lambda e, mb=mb, cc=cc, h=h, psx=psx:
                                     e.matmul(
                                         psx[:],
                                         ktil[:, mb * 128:(mb + 1) * 128],
                                         v_sb[:, cc, h, :], start=True,
                                         stop=True), inc="PE")
                    g_attn["ktil"] = ("PE", vsu)
                    bld.op("vector", lambda e, h=h: e.scalar_tensor_tensor(
                        S_f[:, h, 0:512], S_f[:, h, 0:512], s_gCc[:, h:h + 1],
                        p_s0[:], ALU.mult, ALU.add),
                        waits=[("PE", vsu), dbg_guard.get("vg")])
                    va = bld.op("vector", lambda e, h=h: e.scalar_tensor_tensor(
                        S_f[:, h, 512:1024], S_f[:, h, 512:1024],
                        s_gCc[:, h:h + 1], p_s1[:], ALU.mult, ALU.add),
                        inc="VE")
                    ps_guard = ("VE", va)
                    if not DEBUG_FREEZE_SB:
                        vsb = bld.op("vector", lambda e, h=h: e.tensor_copy(
                            S_b[:, h, :], S_f[:, h, :]), inc="VE")
                        sbf_ready[h] = ("VE", vsb)
                    bld.wait("vector", "PE", vo)
                    bld.op("vector", lambda e, bo=bo: e.tensor_copy(
                        o_scr, banks[bo][:]))
                    bld.op("vector", lambda e: e.tensor_tensor(
                        sq_scr, o_scr, o_scr, ALU.mult))
                    voc = bld.op("vector", lambda e: e.tensor_reduce(
                        ms_scr, sq_scr, mybir.AxisListType.X, ALU.add),
                        inc="VE")
                    bank_guard[bo] = ("VE", voc)
                    bld.op("scalar", lambda e: e.activation(
                        rr_scr, ms_scr, AF.Ln, bias=eps_ap, scale=1.0 / DVH),
                        waits=[("VE", voc)])
                    vrr = bld.op("scalar", lambda e: e.activation(
                        rr_scr, rr_scr, AF.Exp, bias=zero_ap, scale=-0.5),
                        inc="SE")
                    von = bld.op("vector", lambda e: e.scalar_tensor_tensor(
                        on_sb, o_scr, rr_scr, s_gnorm[:], ALU.mult, ALU.mult),
                        inc="VE",
                        waits=[("SE", vrr), g_attn.get("on")])
                    bld.wait("tensor", "VE", vkt)
                    vto = None
                    for db in range(4):
                        vto = bld.op("tensor", lambda e, db=db: e.transpose(
                            p_t[:, 256 + db * 128:256 + (db + 1) * 128],
                            on_sb[:, db * 128:(db + 1) * 128], s_ident[:]),
                            inc="PE", waits=[("VE", von)])
                    g_attn["on"] = ("PE", vto)
                    dst = ont_sb[:, h * 4:(h + 1) * 4,
                                 gch * 128:(gch + 1) * 128]
                    src = p_t[:, 256:768].rearrange("p (a b) -> p a b", b=128)
                    vont = bld.op("vector",
                                  lambda e, dst=dst, src=src: e.tensor_copy(
                                      dst, src), inc="VE",
                                  waits=[("PE", vto)])
                    pt_guard = ("VE", vont)
                    if DEBUG_S and p == 0 and cc == 0 and h == 1:
                        vdS = bld.dma("gpsimd", dbg_S, S_f.rearrange(
                            "p a b -> p (a b)"), "DBG",
                            waits=[("SE", bld.val("SE"))])
                        dbg_guard["vg"] = ("DBG", vdS)
                    if DEBUG_S and p == 0 and cc == 1 and h == 0:
                        v1_ = bld.dma("gpsimd", dbg_qt[:, 0:256], qtil,
                                      "DBG", waits=[("VE", vqt)])
                        v2_ = bld.dma("gpsimd", dbg_qt[:, 256:512], ktil,
                                      "DBG", waits=[("VE", vkt)])
                        v3_ = bld.dma("gpsimd", dbg_Sb, S_b[:, 0, :], "DBG")
                        v4_ = bld.dma("gpsimd", dbg_osc, o_scr, "DBG")
                        g_attn["qtil"] = ("DBG", v4_)
                        g_attn["ktil"] = ("DBG", v4_)
                        g_attn["on"] = ("DBG", v4_)
                        sbf_ready[0] = ("DBG", v4_)
                        dbg_guard["vg"] = ("DBG", v4_)
            xp_guard = ("PE", bld.val("PE"))

        v_ont_done = bld.val("VE")
        if DEBUG_ONT:
            bld.dma("gpsimd", dbg_ont.rearrange("(a p) t -> p a t", p=128),
                    ont_sb, "DBG", waits=[("VE", v_ont_done)])
        vex0 = bld.dma("gpsimd", exin0.rearrange("(a p) t -> p a t", p=128),
                       ont_sb[:, :, 0:1024], "EXN",
                       waits=[("VE", v_ont_done)])
        vex1 = bld.dma("gpsimd", exin1.rearrange("(a p) t -> p a t", p=128),
                       ont_sb[:, :, 1024:2048], "EXN")
        pair_groups = [[0, 1], [2, 3], [4, 5], [6, 7]]
        bld.op("gpsimd", lambda e: e.collective_compute(
            "AllGather", ALU.bypass, replica_groups=pair_groups,
            ins=[exin0], outs=[ex0]), inc="CC",
            waits=[("EXN", vex1)])
        bld.op("gpsimd", lambda e: e.collective_compute(
            "AllGather", ALU.bypass, replica_groups=pair_groups,
            ins=[exin1], outs=[ex1]), inc="CC")

        a_pe = bld.val("PE")
        bld.barrier(["sync", "gpsimd"], ["PE", "VE", "SE", "GP"])
        xg_guard = None
        wp_guard = [None, None]
        xr_guard = [None, None]
        og_prev = None
        gin_prev = None
        dqf.add(gpackR, gscalR, WSg, "WSG")
        ofu_pe_guard = a_pe
        for half in range(2):
            exbuf = ex0 if half == 0 else ex1
            vof = bld.dma("sync", ofu_sb,
                          exbuf.rearrange("(a p) t -> p a t", p=128),
                          "OFU",
                          waits=[("CC", 2), ("PE", ofu_pe_guard)])
            for grp in range(2):
                gcols = slice(half * 1024 + grp * 512,
                              half * 1024 + grp * 512 + 512)
                ocols = slice(grp * 512, (grp + 1) * 512)
                w_ = list(xg_guard) if xg_guard else []
                w_.append(("EXN", 32))
                vxg = bld.dma("sync", xbp,
                              XB.rearrange("(a p) t -> p a t", p=128)
                              [:, :, gcols], "XG", waits=w_)
                vsilu = None
                for mb in range(16):
                    wp = (wpan0, wpan1)[mb % 2]
                    w_ = [wp_guard[mb % 2]] if wp_guard[mb % 2] else []
                    vwp = bld.dma("sync", wp,
                                  WGA2.rearrange("(p m a) n -> p m a n",
                                                 p=128, m=16)[:, mb, :, :],
                                  "WP%d" % (mb % 2), waits=w_)
                    bi = next_bank()
                    tiles = [(wp[:, kb, :], xbp[:, kb, :]) for kb in range(16)]
                    bld.wait("tensor", "WP%d" % (mb % 2), vwp)
                    bld.wait("tensor", "XG", vxg)
                    vmm = mm_accum(bi, tiles)
                    wp_guard[mb % 2] = ("PE", vmm)
                    vsilu = evac(bi, "scalar",
                                 lambda e, bi=bi, mb=mb: e.activation(
                                     g_sb[:, mb, :], banks[bi][:], AF.Silu,
                                     bias=0.0, scale=1.0),
                                 waits=[("PE", vmm), og_prev])
                if DEBUG_B and half == 0 and grp == 0:
                    vd_ = bld.dma("gpsimd",
                                  dbg_ofu.rearrange("(a p) t -> p a t", p=128),
                                  ofu_sb, "DBG", waits=[("OFU", vof)])
                    vd2_ = bld.dma("gpsimd",
                                   dbg_g.rearrange("(a p) t -> p a t", p=128),
                                   g_sb, "DBG", waits=[("SE", vsilu)])
                    dbg_guard["og"] = ("DBG", vd2_)
                xg_guard = [("PE", bld.val("PE"))]
                vog = bld.op("vector", lambda e, ocols=ocols: e.tensor_tensor(
                    ofu_sb[:, :, ocols], ofu_sb[:, :, ocols], g_sb[:],
                    ALU.mult), inc="VE",
                    waits=[("SE", vsilu), ("OFU", vof),
                           dbg_guard.get("og")])
                og_prev = ("VE", vog)
                vsh = None
                for mb in range(16):
                    wp = (wpan0, wpan1)[mb % 2]
                    w_ = [wp_guard[mb % 2]] if wp_guard[mb % 2] else []
                    vwp = bld.dma("sync", wp,
                                  WO2.rearrange("(p m a) n -> p m a n",
                                                 p=128, m=16)[:, mb, :, :],
                                  "WP%d" % (mb % 2), waits=w_)
                    xr = (xres0, xres1)[mb % 2]
                    w_ = [xr_guard[mb % 2]] if xr_guard[mb % 2] else []
                    vxr = bld.dma("sync", xr,
                                  xT.rearrange("(a p) t -> p a t", p=128)
                                  [:, mb, gcols], "XR%d" % (mb % 2), waits=w_)
                    bi = next_bank()
                    tiles = [(wp[:, kb, :], ofu_sb[:, kb, ocols])
                             for kb in range(16)]
                    bld.wait("tensor", "WP%d" % (mb % 2), vwp)
                    bld.wait("tensor", "VE", vog)
                    vmm = mm_accum(bi, tiles)
                    wp_guard[mb % 2] = ("PE", vmm)
                    vadd = bld.op("vector",
                                  lambda e, bi=bi, mb=mb, xr=xr:
                                  e.tensor_tensor(sres[:, mb, :],
                                                  banks[bi][:], xr, ALU.add),
                                  inc="VE",
                                  waits=[("PE", vmm),
                                         ("XR%d" % (mb % 2), vxr),
                                         gin_prev])
                    xr_guard[mb % 2] = ("VE", vadd)
                    vsh = evac(bi, "vector", lambda e, mb=mb: e.tensor_copy(
                        sbf[:, mb, :], sres[:, mb, :]))
                    dqf.feed(1)
                if DEBUG_B and half == 0 and grp == 0:
                    vd3_ = bld.dma("gpsimd",
                                   dbg_ao.rearrange("(a p) t -> p a t", p=128),
                                   sres, "DBG", waits=[("VE", vsh)])
                    vsh_g = ("DBG", vd3_)
                else:
                    vsh_g = None
                vx1 = layer_norm(sres, sbf, sbf, None, g_sb,
                                 [("VE", vsh), vsh_g] if vsh_g else
                                 [("VE", vsh)])
                qq = half * 2 + grp
                gin_dst = gins[qq].rearrange("(a p) t -> p a t", p=128)
                vgin = bld.dma("sync", gin_dst, sbf, "GIN",
                               waits=[("VE", vx1)])
                gin_prev = ("GIN", vgin)
                bld.op("gpsimd", lambda e, qq=qq: e.collective_compute(
                    "AllGather", ALU.bypass,
                    replica_groups=[[0, 2, 4, 6], [1, 3, 5, 7]],
                    ins=[gins[qq]], outs=[X1Gs[qq]]), inc="CC",
                    waits=[("GIN", vgin)])
                if DEBUG_X1:
                    dbg_dst = dbg_x1.rearrange("(a p) t -> p a t", p=128)[:, :, gcols]
                    vdbg = bld.dma("gpsimd", dbg_dst, sbf, "DBG",
                                   waits=[("VE", vx1)])
                    gin_prev = ("DBG", vdbg)
                ofu_pe_guard = bld.val("PE")
                if DEBUG_BSYNC:
                    vpe_ = bld.op("tensor", lambda e: e.nop(), inc="PE")
                    vve_ = bld.op("vector", lambda e: e.engine_nop(), inc="VE")
                    vse_ = bld.op("scalar", lambda e: e.nop(), inc="SE")
                    vgp_ = bld.op("gpsimd", lambda e: e.engine_nop(), inc="GP")
                    for eng_ in ("tensor", "vector", "scalar", "gpsimd", "sync"):
                        bld.wait(eng_, "PE", vpe_)
                        bld.wait(eng_, "VE", vve_)
                        bld.wait(eng_, "SE", vse_)
                        bld.wait(eng_, "GP", vgp_)

        while dqf.jobs:
            dqf.feed(1)
        v_wsg_done = bld.val("WSG")
        ccx = bld.val("CC")

        if DEBUG_STOP_AFTER_B:
            bld.wait("gpsimd", "DBG", bld.val("DBG"))
            bld.wait("gpsimd", "GIN", bld.val("GIN"))
            bld.wait("gpsimd", "CC", bld.val("CC"))
            bld.wait("sync", "OUTS", bld.val("OUTS"))
            _finish(nc, es, bld, E)
            return nc

        b_pe = bld.val("PE")
        bld.barrier(["sync", "gpsimd"], ["PE", "VE", "SE", "GP"])
        if DEBUG_WS:
            bld.dma("gpsimd", dbg_ws, WSg, "DBG",
                    waits=[("WSG", v_wsg_done)])
        bank_mod[0] = 6
        vW0 = bld.dma("sync", W0, WSg.rearrange("(a p) n -> p a n", p=128),
                      "W0S", waits=[("WSG", v_wsg_done), ("PE", b_pe)])

        def x1g_panel(t):
            rb, q = t // 4, t % 4
            return X1Gs[q].rearrange("(rb a p) t -> p rb a t", p=128, rb=4)[
                :, rb, :, :]

        GTr = GT.rearrange("(t a p) n -> p t a n", p=128, t=16)
        ITr = IT.rearrange("(t a p) n -> p t a n", p=128, t=16)

        xf_guard = [None, None]
        gA_guard = None
        dqf.add(upackR, uscalR, WSu, "WSU")
        dqf.add(dpackR, dscalR, WSd, "WSD")
        vxf_pre = {}
        vxf_pre[0] = bld.dma("sync", xf0, x1g_panel(0), "XF0",
                             waits=[("CC", 3)])
        vxf_pre[1] = bld.dma("sync", xf1, x1g_panel(1), "XF1",
                             waits=[("CC", 4)])
        for t in range(16):
            xf = (xf0, xf1)[t % 2]
            vxf = vxf_pre[t % 2]
            last_ev = None
            for mb in range(16):
                bi = next_bank()
                tiles = [(W0[:, kb, mb * 128:(mb + 1) * 128], xf[:, kb, :])
                         for kb in range(16)]
                bld.wait("tensor", "W0S", vW0)
                bld.wait("tensor", "XF%d" % (t % 2), vxf)
                vmm = mm_accum(bi, tiles)
                last_ev = evac(bi, "scalar",
                               lambda e, bi=bi, mb=mb: e.activation(
                                   bufA[:, mb, :], banks[bi][:], AF.Copy,
                                   bias=0.0, scale=1.0),
                               waits=[("PE", vmm), gA_guard])
                dqf.feed(1)
            xf_guard[t % 2] = ("PE", bld.val("PE"))
            if t + 2 < 16:
                vxf_pre[t % 2] = bld.dma("sync", (xf0, xf1)[t % 2],
                                         x1g_panel(t + 2), "XF%d" % (t % 2),
                                         waits=[xf_guard[t % 2],
                                                ("CC", 2 + min(4, (t + 2) % 4 + 1))])
            vgt = bld.dma("sync", GTr[:, t, :, :], bufA, "GTS",
                          waits=[("SE", last_ev)])
            gA_guard = ("GTS", vgt)
        while dqf.jobs:
            dqf.feed(1)
        v_wsu_done = bld.val("WSU")
        p1a_pe = bld.val("PE")

        vW0u = bld.dma("sync", W0, WSu.rearrange("(a p) n -> p a n", p=128),
                       "W0S", waits=[("WSU", v_wsu_done), ("PE", p1a_pe)])
        bld.wait("sync", "GTS", bld.val("GTS"))
        gp_guard = [None, None]
        it_prev = None
        ge_guard = [None, None]
        ge_guard2 = [None, None]
        vxf_pre = {}
        vgp_pre = {}
        for tp_ in range(2):
            w_ = [xf_guard[tp_]] if xf_guard[tp_] else []
            vxf_pre[tp_] = bld.dma("sync", (xf0, xf1)[tp_], x1g_panel(tp_),
                                   "XF%d" % tp_, waits=w_)
            vgp_pre[tp_] = bld.dma("sync", (bufA, bufB)[tp_],
                                   GTr[:, tp_, :, :], ("BA", "BB")[tp_])
        for t in range(16):
            xf = (xf0, xf1)[t % 2]
            vxf = vxf_pre[t % 2]
            gp = (bufA, bufB)[t % 2]
            vgp = vgp_pre[t % 2]
            last_ev = None
            vth_last = None
            for mb in range(16):
                bi = next_bank()
                tiles = [(W0[:, kb, mb * 128:(mb + 1) * 128], xf[:, kb, :])
                         for kb in range(16)]
                bld.wait("tensor", "W0S", vW0u)
                bld.wait("tensor", "XF%d" % (t % 2), vxf)
                vmm = mm_accum(bi, tiles)
                gmb = gp[:, mb, :]
                ga = (ge_a0, ge_a1)[mb % 2]
                gb = (ge_b0, ge_b1)[mb % 2]
                bld.wait("vector", ("BA", "BB")[t % 2], vgp)
                w_ = [("SE", ge_guard[mb % 2])] if ge_guard[mb % 2] else []
                bld.op("vector", lambda e, gmb=gmb, ga=ga: e.tensor_tensor(
                    ga, gmb, gmb, ALU.mult), waits=w_)
                bld.op("vector", lambda e, gmb=gmb, ga=ga: e.tensor_tensor(
                    ga, ga, gmb, ALU.mult))
                vu = bld.op("vector", lambda e, gmb=gmb, ga=ga:
                            e.scalar_tensor_tensor(
                                ga, ga, 0.044715, gmb, ALU.mult, ALU.add),
                            inc="VE")
                w_ = [("VE", ge_guard2[mb % 2])] if ge_guard2[mb % 2] else []
                vth = bld.op("scalar", lambda e, ga=ga, gb=gb: e.activation(
                    gb, ga, AF.Tanh, bias=0.0, scale=0.7978845608028654),
                    inc="SE", waits=[("VE", vu)] + w_)
                ge_guard[mb % 2] = vth
                vth_last = vth
                bld.op("vector", lambda e, gb=gb: e.tensor_scalar(
                    gb, gb, 0.5, 0.5, ALU.mult, ALU.add),
                    waits=[("SE", vth)])
                bld.op("vector", lambda e, gmb=gmb, gb=gb: e.tensor_tensor(
                    gb, gb, gmb, ALU.mult))
                w_ = [("PE", vmm)] + ([it_prev] if it_prev else [])
                last_ev = evac(bi, "vector",
                               lambda e, bi=bi, mb=mb, gb=gb: e.tensor_tensor(
                                   bufC[:, mb, :], gb, banks[bi][:],
                                   ALU.mult), waits=w_)
                ge_guard2[mb % 2] = last_ev
            gp_guard[t % 2] = ("VE", last_ev)
            xf_guard[t % 2] = ("PE", bld.val("PE"))
            if t + 2 < 16:
                vxf_pre[t % 2] = bld.dma("sync", xf, x1g_panel(t + 2),
                                         "XF%d" % (t % 2),
                                         waits=[xf_guard[t % 2]])
                vgp_pre[t % 2] = bld.dma("sync", gp, GTr[:, t + 2, :, :],
                                         ("BA", "BB")[t % 2],
                                         waits=[gp_guard[t % 2]])
            vit = bld.dma("sync", ITr[:, t, :, :], bufC, "ITS",
                          waits=[("VE", last_ev)])
            it_prev = ("ITS", vit)
        v_wsd_done = bld.val("WSD")
        p1b_pe = bld.val("PE")

        ln2_state = {"vs2_guard": None}

        def _final_ln(q, ccv):
            rso = RSo0 if q == 0 else RSo1
            w_ = [("CC", ccv)]
            if ln2_state["vs2_guard"] is not None:
                w_.append(ln2_state["vs2_guard"])
            vs2 = bld.dma("scalar", s2_sb,
                          rso.rearrange("(a p) n -> p a n", p=128),
                          "S2R", waits=w_)

            def store_fn(kb, ob, ve_val):
                return bld.dma("scalar",
                               out.rearrange("(a p) t -> p a t", p=128)
                               [:, kb, q * 512:(q + 1) * 512], ob,
                               "OUTS", waits=[("VE", ve_val)])

            vout = layer_norm(None, s2_sb, None, None, sq2_sb,
                              [("S2R", vs2)], store_fn=store_fn)
            ln2_state["vs2_guard"] = ("VE", vout)

        vW0d = bld.dma("sync", W0, WSd.rearrange("(a p) n -> p a n", p=128),
                       "W0S", waits=[("WSD", v_wsd_done), ("PE", p1b_pe)])
        bld.wait("sync", "ITS", bld.val("ITS"))
        order = [0, 2, 4, 6, 8, 10, 12, 14, 1, 3, 5, 7, 9, 11, 13, 15]
        it_guard = [gp_guard[0], gp_guard[1]]
        dn_prev = it_prev
        vrs0 = None
        vxf_pre = {}
        vit_pre = {}
        for oi_ in range(2):
            w_ = [xf_guard[oi_]] if xf_guard[oi_] else []
            vxf_pre[oi_] = bld.dma("sync", (xf0, xf1)[oi_],
                                   x1g_panel(order[oi_]), "XF%d" % oi_,
                                   waits=w_)
            w_ = [it_guard[oi_]] if it_guard[oi_] else []
            vit_pre[oi_] = bld.dma("sync", (bufA, bufB)[oi_],
                                   ITr[:, order[oi_], :, :],
                                   ("BA", "BB")[oi_], waits=w_)
        for oi, t in enumerate(order):
            xf = (xf0, xf1)[oi % 2]
            vxf = vxf_pre[oi % 2]
            it_ = (bufA, bufB)[oi % 2]
            vit = vit_pre[oi % 2]
            last_ev = None
            for mb in range(16):
                bi = next_bank()
                tiles = [(W0[:, kb, mb * 128:(mb + 1) * 128], it_[:, kb, :])
                         for kb in range(16)]
                bld.wait("tensor", "W0S", vW0d)
                bld.wait("tensor", ("BA", "BB")[oi % 2], vit)
                vmm = mm_accum(bi, tiles)
                w_ = [("PE", vmm), ("XF%d" % (oi % 2), vxf)] + (
                    [dn_prev] if dn_prev else [])
                last_ev = evac(bi, "vector",
                               lambda e, bi=bi, mb=mb, xf=xf:
                               e.scalar_tensor_tensor(
                                   bufC[:, mb, :], xf[:, mb, :], 0.5,
                                   banks[bi][:], ALU.mult, ALU.add),
                               waits=w_)
            it_guard[oi % 2] = ("PE", bld.val("PE"))
            xf_guard[oi % 2] = ("VE", last_ev)
            if oi + 2 < 16:
                vxf_pre[oi % 2] = bld.dma("sync", xf, x1g_panel(order[oi + 2]),
                                          "XF%d" % (oi % 2),
                                          waits=[xf_guard[oi % 2]])
                vit_pre[oi % 2] = bld.dma("sync", it_,
                                          ITr[:, order[oi + 2], :, :],
                                          ("BA", "BB")[oi % 2],
                                          waits=[it_guard[oi % 2]])
            pd = PD0 if t % 2 == 0 else PD1
            pdsem = "PDS0" if t % 2 == 0 else "PDS1"
            vpd = bld.dma("sync",
                          pd.rearrange("(r a p) n -> p r a n", p=128, r=8)
                          [:, t // 2, :, :], bufC, pdsem,
                          waits=[("VE", last_ev)])
            dn_prev = (pdsem, vpd)
            if oi == 7:
                v_pd0 = bld.val("PDS0")
                vrs0 = bld.op("gpsimd", lambda e: e.collective_compute(
                    "ReduceScatter", ALU.add, replica_groups=[list(range(8))],
                    ins=[PD0], outs=[RSo0]), inc="CC",
                    waits=[("PDS0", v_pd0)])
            if oi == 8:
                bank_mod[0] = 4
        v_pd1 = bld.val("PDS1")
        vrs1 = bld.op("gpsimd", lambda e: e.collective_compute(
            "ReduceScatter", ALU.add, replica_groups=[list(range(8))],
            ins=[PD1], outs=[RSo1]), inc="CC",
            waits=[("PDS1", v_pd1)])

        if DEBUG_RS:
            bld.dma("gpsimd", dbg_rs[:, 0:512], RSo0, "DBG",
                    waits=[("CC", vrs0)])
            bld.dma("gpsimd", dbg_rs[:, 512:1024], RSo1, "DBG",
                    waits=[("CC", vrs1)])
            bld.dma("gpsimd", dbg_it[:, 0:512], IT[0:2048, :], "DBG")
            bld.dma("gpsimd", dbg_it[:, 512:1024], IT[2048:4096, :], "DBG")
        _final_ln(0, vrs1)
        _final_ln(1, vrs1)
        bld.wait("scalar", "OUTS", bld.val("OUTS"))
        bld.wait("gpsimd", "OUTS", bld.val("OUTS"))

        _finish(nc, es, bld, E)
    return nc


def _finish(nc, es, bld, E):
    block = E(nc.Block())

    @block.tensor
    def _(eng):
        for fn in bld.q["tensor"]:
            fn(eng)

    @block.vector
    def _(eng):
        for fn in bld.q["vector"]:
            fn(eng)

    @block.scalar
    def _(eng):
        for fn in bld.q["scalar"]:
            fn(eng)

    @block.gpsimd
    def _(eng):
        for fn in bld.q["gpsimd"]:
            fn(eng)

    @block.sync
    def _(eng):
        for fn in bld.q["sync"]:
            fn(eng)


GAMMA = 1.0 - 2.0 ** (-5.0 - np.arange(H, dtype=np.float64))


def _host_stage(core, inp):
    b = core // 2
    hs = core % 2
    g = GAMMA
    x = np.asarray(inp["L0_attn_x"])
    Wq = np.asarray(inp["L0_attn_Wq"])
    Wk = np.asarray(inp["L0_attn_Wk"])
    Wv = np.asarray(inp["L0_attn_Wv"])
    Wg = np.asarray(inp["L0_attn_Wg"])
    Wo = np.asarray(inp["L0_attn_Wo"])
    gn = np.asarray(inp["L0_attn_gnorm_w"])

    d = {}
    d["xT"] = np.ascontiguousarray(x[b].T)
    d["wqT"] = np.ascontiguousarray(Wq[512 * hs:512 * hs + 512].T)
    d["wkT"] = np.ascontiguousarray(Wk[512 * hs:512 * hs + 512].T)
    d["wvT"] = np.ascontiguousarray(Wv[1024 * hs:1024 * hs + 1024].T)
    def _ilv(w):
        a = np.ascontiguousarray(w.T).reshape(16, 128, 16, 128)
        return np.ascontiguousarray(
            a.transpose(1, 2, 0, 3).reshape(128 * 16 * 16, 128))
    d["wgaT"] = _ilv(Wg)
    d["woT"] = _ilv(Wo)
    d["gnormB"] = np.ascontiguousarray(
        np.broadcast_to(gn.astype(np.float32), (128, DVH)))
    inv = 1.0 / 10000.0 ** (np.arange(0, DKH, 2, dtype=np.float64) / DKH)
    ang = np.arange(SEQ, dtype=np.float64)[:, None] * inv[None, :]
    d["ropeC"] = np.ascontiguousarray(np.cos(ang).T.astype(np.float32))
    d["ropeS"] = np.ascontiguousarray(np.sin(ang).T.astype(np.float32))
    heads = [2 * hs, 2 * hs + 1]
    dm = np.zeros((128, 2 * CH), np.float32)
    gi = np.zeros((128, 2 * CH), np.float32)
    gk = np.zeros((128, 2), np.float32)
    gC = np.zeros((128, 2), np.float32)
    i_ = np.arange(CH)
    for j, h in enumerate(heads):
        gh = g[h]
        expo = (i_[:, None] - i_[None, :]).astype(np.float64)
        mask = np.where(expo >= 0, gh ** expo, 0.0) / 16.0
        dm[:, j * CH:(j + 1) * CH] = mask.T.astype(np.float32)
        gi[:, j * CH:(j + 1) * CH] = np.broadcast_to(
            (gh ** (i_ + 1) / 16.0).astype(np.float32), (128, CH))
        gk[0:CH, j] = (gh ** (CH - 1 - i_)).astype(np.float32)
        gC[:, j] = gh ** CH
    d["dmaskT"] = dm
    d["ginb"] = gi
    d["gkb"] = gk
    d["gCc"] = gC
    d["identM"] = np.eye(128, dtype=np.float32)
    sh = np.zeros((128, 1), np.int32)
    sh[1::2] = 4
    d["shifts"] = sh
    cp = np.zeros((128, 8), np.float32)
    cp[:, 0] = EPS
    d["cpool"] = cp
    d["onescol"] = np.ones((128, 1), np.float32)
    d["onesrow"] = np.ones((1, 128), np.float32)
    e0 = np.zeros((128, 128), np.float32)
    e0[0, :] = 1.0
    d["e0row"] = e0
    gp = np.asarray(inp["L0_ffn_gate_W_packed"])
    gs = np.asarray(inp["L0_ffn_gate_scales"])
    up = np.asarray(inp["L0_ffn_up_W_packed"])
    us = np.asarray(inp["L0_ffn_up_scales"])
    dp = np.asarray(inp["L0_ffn_down_W_packed"])
    ds = np.asarray(inp["L0_ffn_down_scales"])
    sl = slice(core * MSL, (core + 1) * MSL)
    d["gpackR"] = np.ascontiguousarray(np.repeat(gp.T[:, sl], 2, axis=0))
    d["gscalR"] = np.ascontiguousarray(np.repeat(gs.T[:, sl], 32, axis=0))
    d["upackR"] = np.ascontiguousarray(np.repeat(up.T[:, sl], 2, axis=0))
    d["uscalR"] = np.ascontiguousarray(np.repeat(us.T[:, sl], 32, axis=0))
    d["dpackR"] = np.ascontiguousarray(
        np.repeat(dp.T[core * MSL // 2:(core + 1) * MSL // 2, :], 2, axis=0))
    d["dscalR"] = np.ascontiguousarray(
        np.repeat(ds.T[core * MSL // 32:(core + 1) * MSL // 32, :], 32, axis=0))
    return d


def kernel(**inputs):
    in_maps = [_host_stage(c, inputs) for c in range(8)]
    nc = build_program()
    res = run_bass_kernel_spmd(nc, in_maps, core_ids=list(range(8)))
    full = np.empty((BB, SEQ, DM), np.float32)
    for c in range(8):
        o = np.asarray(res.results[c]["out"])
        full[c // 2, 1024 * (c % 2):1024 * (c % 2) + 1024, :] = o.T
    return full
